# version 05
# baseline (speedup 1.0000x reference)
import contextlib
import numpy as np
import ml_dtypes
import concourse.bass as bass
import concourse.mybir as mybir
from concourse.bass_utils import run_bass_kernel_spmd

F32 = mybir.dt.float32
BF16 = mybir.dt.bfloat16
U32 = mybir.dt.uint32
AF = mybir.ActivationFunctionType
ALU = mybir.AluOpType
AX = mybir.AxisListType

H = 8
DH = 128
CH = 64
D = 1024
NTOK = 16384
NCHUNK = NTOK // CH
NPAIR = NCHUNK // 2
WTOK = NTOK // 8
EPS_LN = 1e-5
EPS_RMS = 1e-6
LR_EFF = 0.01 * 2.0 / DH

DEBUG = []


class Sem:
    def __init__(self, stack, nc, name):
        self.h = stack.enter_context(nc.semaphore(name))
        self.name = name
        self.n = 0

    def inc(self, instr, k=1):
        instr.then_inc(self.h, k)
        self.n += k
        return instr

    def dma(self, instr):
        return self.inc(instr, 16)


_WAITED = {}


def W(eng, ename, sem, val):
    if val is None or val <= 0:
        return
    key = (ename, sem.name)
    if _WAITED.get(key, -1) >= val:
        return
    _WAITED[key] = val
    eng.wait_ge(sem.h, val)


def build_program():
    _WAITED.clear()
    nc = bass.Bass("TRN2", target_bir_lowering=False)

    x_full = nc.dram_tensor("x_full", [NTOK, D], F32, kind="ExternalInput").ap()
    x_slice = nc.dram_tensor("x_slice", [WTOK, D], F32, kind="ExternalInput").ap()
    wk_h = nc.dram_tensor("wk_h", [D, DH], F32, kind="ExternalInput").ap()
    wv_h = nc.dram_tensor("wv_h", [D, DH], F32, kind="ExternalInput").ap()
    wq_h = nc.dram_tensor("wq_h", [D, DH], F32, kind="ExternalInput").ap()
    w4_h = nc.dram_tensor("w4_h", [D, 2], F32, kind="ExternalInput").ap()
    wmd_h = nc.dram_tensor("wmd_h", [D, 2], F32, kind="ExternalInput").ap()
    sgain_in = nc.dram_tensor("sgain", [8, 128], F32, kind="ExternalInput").ap()
    rgain_in = nc.dram_tensor("rgain", [8, 128], F32, kind="ExternalInput").ap()
    mw0_in = nc.dram_tensor("mw0", [128, 128], F32, kind="ExternalInput").ap()
    mw1_in = nc.dram_tensor("mw1", [128, 128], F32, kind="ExternalInput").ap()
    gamma_in = nc.dram_tensor("gamma_row", [1, 128], F32, kind="ExternalInput").ap()
    wg_in = nc.dram_tensor("wg_in", [D, H], F32, kind="ExternalInput").ap()
    w1_in = nc.dram_tensor("w1_in", [8, D, 4096], F32, kind="ExternalInput").ap()
    w2_in = nc.dram_tensor("w2_in", [8, 4096, D], F32, kind="ExternalInput").ap()
    wc_in = nc.dram_tensor("wc_in", [D, D], F32, kind="ExternalInput").ap()
    idb_in = nc.dram_tensor("idb_in", [128, 128], BF16, kind="ExternalInput").ap()
    idf_in = nc.dram_tensor("idf_in", [128, 128], F32, kind="ExternalInput").ap()
    oseg_in = nc.dram_tensor("oseg_in", [128, 2], BF16, kind="ExternalInput").ap()
    cv_in = nc.dram_tensor("cv_in", [128, 4], F32, kind="ExternalInput").ap()
    widx_in = nc.dram_tensor("widx_in", [128, 16], U32, kind="ExternalInput").ap()
    bscale_in = nc.dram_tensor("bscale_in", [128, 1], F32, kind="ExternalInput").ap()
    zf_in = nc.dram_tensor("zf_in", [128, 128], F32, kind="ExternalInput").ap()
    out = nc.dram_tensor("out", [WTOK, D], F32, kind="ExternalOutput").ap()

    def scratch(name, shape, dtype):
        if name in DEBUG:
            return nc.dram_tensor(name, shape, dtype, kind="ExternalOutput").ap()
        return nc.dram_tensor(name, shape, dtype).ap()

    kTd = scratch("kTd", [128, NTOK], BF16)
    vTd = scratch("vTd", [128, NTOK], BF16)
    rTd = scratch("rTd", [128, NTOK + 64], BF16)
    gswd = scratch("gswd", [128, 128], F32)
    lrd = scratch("lrd", [1, NTOK], F32)
    gaterd = scratch("gaterd", [8, WTOK], BF16)
    agin_t = nc.dram_tensor("agin", [128, NTOK], BF16)
    agout_t = nc.dram_tensor("agout", [8, 128, NTOK], BF16)
    agin = agin_t.ap()
    agout = agout_t.ap()
    dbg = {}
    for nm, shape, dt in [
        ("agin_d", [128, NTOK], BF16),
        ("momb_d", [128, 256], F32), ("decb_d", [128, 256], F32),
        ("cm_d", [2, 128, 1024], F32),
        ("gacc_d", [8, 128, WTOK], F32),
        ("xnT_d", [8, 128, WTOK], BF16),
    ]:
        if nm in DEBUG:
            dbg[nm] = nc.dram_tensor(nm, shape, dt, kind="ExternalOutput").ap()

    st = contextlib.ExitStack()
    E = st.enter_context

    LD = Sem(st, nc, "LD")
    LG = Sem(st, nc, "LG")
    LX = [Sem(st, nc, "LX0"), Sem(st, nc, "LX1")]
    SK = Sem(st, nc, "SK")
    LW = [Sem(st, nc, "LW0"), Sem(st, nc, "LW1")]
    KV = [Sem(st, nc, "KV0"), Sem(st, nc, "KV1")]
    RW = [Sem(st, nc, "RW0"), Sem(st, nc, "RW1")]
    SA = [Sem(st, nc, "SA0"), Sem(st, nc, "SA1")]
    SC2 = Sem(st, nc, "SC2")
    SPK = Sem(st, nc, "SPK")
    SG3 = Sem(st, nc, "SG3")
    SG4 = Sem(st, nc, "SG4")
    SV5 = Sem(st, nc, "SV5")
    TE = Sem(st, nc, "TE")
    VE = Sem(st, nc, "VE")
    SE = Sem(st, nc, "SE")
    CC = Sem(st, nc, "CC")
    SO = Sem(st, nc, "SO")
    FN = Sem(st, nc, "FN")

    pb = [E(nc.psum_tensor(f"pb{i}", [128, 512], F32)) for i in range(6)]
    pbt = [E(nc.psum_tensor(f"pbt{i}", [128, 1024], BF16)) for i in range(2)]

    idb = E(nc.sbuf_tensor("idb", [128, 128], BF16))
    idf = E(nc.sbuf_tensor("idf", [128, 128], F32))
    oseg = E(nc.sbuf_tensor("oseg", [128, 2], BF16))
    cv = E(nc.sbuf_tensor("cv", [128, 4], F32))
    ones1 = E(nc.sbuf_tensor("ones1", [1, 128], BF16))
    onescol = E(nc.sbuf_tensor("onescol", [128, 1], BF16))
    onesf = E(nc.sbuf_tensor("onesf", [1, 128], F32))
    onec1f = E(nc.sbuf_tensor("onec1f", [1, 1], F32))
    mw0f = E(nc.sbuf_tensor("mw0f", [128, 128], F32))
    mw1f = E(nc.sbuf_tensor("mw1f", [128, 128], F32))
    w0b = E(nc.sbuf_tensor("w0b", [128, 128], BF16))
    w1b = E(nc.sbuf_tensor("w1b", [128, 128], BF16))
    w1Tb = E(nc.sbuf_tensor("w1Tb", [128, 128], BF16))
    wkb = E(nc.sbuf_tensor("wkb", [128, 8 * 128], BF16))
    wvb = E(nc.sbuf_tensor("wvb", [128, 8 * 128], BF16))
    wqb = E(nc.sbuf_tensor("wqb", [128, 8 * 128], BF16))
    w4b = E(nc.sbuf_tensor("w4b", [128, 8 * 2], BF16))
    wmdb = E(nc.sbuf_tensor("wmdb", [128, 8 * 2], BF16))
    gp1row = E(nc.sbuf_tensor("gp1row", [1, 128], F32))
    momb = E(nc.sbuf_tensor("momb", [128, 256], F32))
    decb = E(nc.sbuf_tensor("decb", [128, 256], F32))
    M0 = E(nc.sbuf_tensor("M0", [128, 128], F32))
    M1 = E(nc.sbuf_tensor("M1", [128, 128], F32))
    U0 = E(nc.sbuf_tensor("U0", [128, 128], F32))
    U1 = E(nc.sbuf_tensor("U1", [128, 128], F32))
    xnT = E(nc.sbuf_tensor("xnT", [128, 8 * WTOK], BF16))
    gacc = E(nc.sbuf_tensor("gacc", [128, 8 * WTOK], F32))
    widxS = E(nc.sbuf_tensor("widxS", [128, 16], U32))
    bscaleS = E(nc.sbuf_tensor("bscaleS", [128, 1], F32))
    zrow = E(nc.sbuf_tensor("zrow", [128, 64], BF16))

    with nc.Block() as blk:

        def fence():
            base = FN.n
            FN.n = base + 5
            tgt = FN.n

            @blk.sync
            def _(sync):
                for s in [LD, SO, SK, LX[0], LX[1], LW[0], LW[1], KV[0], KV[1]]:
                    W(sync, "sync", s, s.n)
                sync.sem_inc(FN.h, 1)
                sync.wait_ge(FN.h, tgt)

            @blk.gpsimd
            def _(gpsimd):
                for s in [LG, SC2, SPK, SG3, SG4, SV5, RW[0], RW[1], SA[0], SA[1]]:
                    W(gpsimd, "gpsimd", s, s.n)
                gpsimd.sem_inc(FN.h, 1)
                gpsimd.wait_ge(FN.h, tgt)

            @blk.tensor
            def _(tensor):
                tensor.wait_ge(TE.h, TE.n)
                tensor.sem_inc(FN.h, 1)
                tensor.wait_ge(FN.h, tgt)

            @blk.vector
            def _(vector):
                vector.wait_ge(VE.h, VE.n)
                vector.sem_inc(FN.h, 1)
                vector.wait_ge(FN.h, tgt)

            @blk.scalar
            def _(scalar):
                scalar.wait_ge(SE.h, SE.n)
                scalar.sem_inc(FN.h, 1)
                scalar.wait_ge(FN.h, tgt)

        p0 = contextlib.ExitStack()
        E0 = p0.enter_context
        wkst = E0(nc.sbuf_tensor("wkst", [128, 8 * 128], F32))
        wvst = E0(nc.sbuf_tensor("wvst", [128, 8 * 128], F32))
        wqst = E0(nc.sbuf_tensor("wqst", [128, 8 * 128], F32))
        w4st = E0(nc.sbuf_tensor("w4st", [128, 8 * 2], F32))
        wmdst = E0(nc.sbuf_tensor("wmdst", [128, 8 * 2], F32))
        sgainS = E0(nc.sbuf_tensor("sgainS", [128, 8], F32))
        rgainS = E0(nc.sbuf_tensor("rgainS", [128, 8], F32))
        grow0 = E0(nc.sbuf_tensor("grow0", [1, 128], F32))
        zf32 = E0(nc.sbuf_tensor("zf32", [128, 128], F32))

        @blk.sync
        def _(sync):
            for (dst, src) in [(idb, idb_in), (idf, idf_in), (oseg, oseg_in),
                               (cv, cv_in), (mw0f, mw0_in), (mw1f, mw1_in),
                               (widxS, widx_in), (bscaleS, bscale_in),
                               (grow0, gamma_in), (zf32, zf_in)]:
                LD.dma(sync.dma_start(out=dst[:], in_=src[:, :]))
            for k in range(8):
                LD.dma(sync.dma_start(out=wkst[:, 128 * k:128 * k + 128], in_=wk_h[128 * k:128 * k + 128, :]))
                LD.dma(sync.dma_start(out=wvst[:, 128 * k:128 * k + 128], in_=wv_h[128 * k:128 * k + 128, :]))
                LD.dma(sync.dma_start(out=wqst[:, 128 * k:128 * k + 128], in_=wq_h[128 * k:128 * k + 128, :]))
                LD.dma(sync.dma_start(out=w4st[:, 2 * k:2 * k + 2], in_=w4_h[128 * k:128 * k + 128, :]))
                LD.dma(sync.dma_start(out=wmdst[:, 2 * k:2 * k + 2], in_=wmd_h[128 * k:128 * k + 128, :]))
                LD.dma(sync.dma_start(out=sgainS[:, k:k + 1], in_=sgain_in[k:k + 1, :]))
                LD.dma(sync.dma_start(out=rgainS[:, k:k + 1], in_=rgain_in[k:k + 1, :]))

        @blk.vector
        def _(vector):
            W(vector, "vector", LD, LD.n)
            vector.memset(ones1[:], 1.0)
            vector.memset(onescol[:], 1.0)
            vector.memset(onesf[:], 1.0)
            vector.memset(onec1f[:], 1.0)
            vector.memset(M0[:], 0.0)
            vector.memset(M1[:], 0.0)
            vector.memset(U0[:], 0.0)
            vector.memset(U1[:], 0.0)
            vector.memset(zrow[:], 0.0)
            vector.memset(gacc[:], 0.0)
            for k in range(8):
                s = slice(128 * k, 128 * k + 128)
                vector.tensor_scalar(out=wkb[:, s], in0=wkst[:, s], scalar1=sgainS[:, k:k + 1],
                                     scalar2=None, op0=ALU.mult)
                vector.tensor_scalar(out=wvb[:, s], in0=wvst[:, s], scalar1=sgainS[:, k:k + 1],
                                     scalar2=None, op0=ALU.mult)
                vector.tensor_scalar(out=wqb[:, s], in0=wqst[:, s], scalar1=rgainS[:, k:k + 1],
                                     scalar2=None, op0=ALU.mult)
                vector.tensor_scalar(out=w4b[:, 2 * k:2 * k + 1], in0=w4st[:, 2 * k:2 * k + 1],
                                     scalar1=sgainS[:, k:k + 1], scalar2=None, op0=ALU.mult)
                vector.tensor_scalar(out=w4b[:, 2 * k + 1:2 * k + 2], in0=w4st[:, 2 * k + 1:2 * k + 2],
                                     scalar1=rgainS[:, k:k + 1], scalar2=None, op0=ALU.mult)
                vector.tensor_scalar(out=wmdb[:, 2 * k:2 * k + 2], in0=wmdst[:, 2 * k:2 * k + 2],
                                     scalar1=sgainS[:, k:k + 1], scalar2=None, op0=ALU.mult)
            vector.tensor_scalar(out=gp1row[:], in0=grow0[:], scalar1=1.0, scalar2=None, op0=ALU.add)
            vector.tensor_copy(w0b[:], mw0f[:])
            VE.inc(vector.tensor_copy(w1b[:], mw1f[:]))

        @blk.tensor
        def _(tensor):
            W(tensor, "tensor", VE, VE.n)
            TE.inc(tensor.transpose(pbt[0][:, 0:128], w1b[:], idb[:]))

        @blk.scalar
        def _(scalar):
            W(scalar, "scalar", TE, TE.n)
            SE.inc(scalar.copy(out=w1Tb[:], in_=pbt[0][:, 0:128]))

        @blk.gpsimd
        def _(gpsimd):
            W(gpsimd, "gpsimd", VE, VE.n)
            LG.dma(gpsimd.dma_start(out=rTd[:, NTOK:NTOK + 64], in_=zrow[:]))
            W(gpsimd, "gpsimd", LD, LD.n)
            LG.dma(gpsimd.dma_start(out=gswd[:, :], in_=zf32[:]))

        fence()
        p0.close()

        p1 = contextlib.ExitStack()
        E1 = p1.enter_context
        xst = E1(nc.sbuf_tensor("xst", [128, 2 * 1024], F32))
        xsq = E1(nc.sbuf_tensor("xsq", [128, 2 * 1024], F32))
        ytok = E1(nc.sbuf_tensor("ytok", [128, 2 * 1024], BF16))
        yTst = E1(nc.sbuf_tensor("yTst", [128, 8192], BF16))
        stats = E1(nc.sbuf_tensor("stats", [128, 2 * 16], F32))
        cm2 = E1(nc.sbuf_tensor("cm2", [2, 2 * 1024], F32))
        cmtok = E1(nc.sbuf_tensor("cmtok", [128, 2 * 1024], F32))
        kvr = E1(nc.sbuf_tensor("kvr", [128, 2 * 3 * 512], BF16))
        sgm = E1(nc.sbuf_tensor("sgm", [2, 2 * 512], F32))
        lrrow = E1(nc.sbuf_tensor("lrrow", [1, 2 * 512], F32))

        se_y = {}
        se_sq1 = {}
        ve_st = {}
        ve_ev = {}
        ve_cm = {}
        te_tc = {}
        te_pj = {}
        se_ev = {}
        ve_lr = {}
        ld_x = {}
        ld_kvr = {}
        lg_cm = {}
        lg_pk = {}


        for g in range(128):
            t, i = divmod(g, 4)
            par = g % 2
            so = 16 * par
            xv = xst[:, 1024 * par:1024 * par + 1024]
            xqv = xsq[:, 1024 * par:1024 * par + 1024]
            yv = ytok[:, 1024 * par:1024 * par + 1024]

            @blk.sync
            def _(sync, g=g, par=par, xv=xv):
                W(sync, "sync", VE, se_y.get(g - 2, 0))
                LX[par].dma(sync.dma_start(out=xv, in_=x_full[128 * g:128 * g + 128, :]))
                ld_x[g] = LX[par].n

            @blk.vector
            def _(vector, g=g, par=par, xv=xv, xqv=xqv, so=so):
                W(vector, "vector", LX[par], ld_x[g])
                W(vector, "vector", SE, se_sq1.get(g - 2, 0))
                vector.reduce_sum(stats[:, so + 0:so + 1], xv, axis=AX.X)
                vector.tensor_tensor(out=xqv, in0=xv, in1=xv, op=ALU.mult)
                vector.drain()
                vector.reduce_sum(stats[:, so + 1:so + 2], xqv, axis=AX.X)
                vector.tensor_scalar(out=stats[:, so + 2:so + 3], in0=stats[:, so + 0:so + 1],
                                     scalar1=1.0 / D, scalar2=None, op0=ALU.mult)
                vector.drain()
                vector.tensor_scalar(out=stats[:, so + 3:so + 4], in0=stats[:, so + 1:so + 2],
                                     scalar1=1.0 / D, scalar2=None, op0=ALU.mult)
                vector.drain()
                VE.inc(vector.scalar_tensor_tensor(out=stats[:, so + 4:so + 5], in0=stats[:, so + 2:so + 3],
                                                   scalar=stats[:, so + 2:so + 3], in1=stats[:, so + 3:so + 4],
                                                   op0=ALU.mult, op1=ALU.subtract))
                ve_st[(g, "var")] = VE.n

            @blk.scalar
            def _(scalar, g=g, so=so):
                W(scalar, "scalar", VE, ve_st[(g, "var")])
                SE.inc(scalar.activation(out=stats[:, so + 5:so + 6], in_=stats[:, so + 4:so + 5],
                                         func=AF.Sqrt, scale=-(1.0 + EPS_RMS), bias=cv[:, 2:3]))
                se_sq1[g] = SE.n

            @blk.vector
            def _(vector, g=g, so=so):
                W(vector, "vector", SE, se_sq1[g])
                vector.reciprocal(stats[:, so + 6:so + 7], stats[:, so + 5:so + 6])
                vector.drain()
                VE.inc(vector.scalar_tensor_tensor(out=stats[:, so + 7:so + 8], in0=stats[:, so + 2:so + 3],
                                                   scalar=-1.0, in1=stats[:, so + 6:so + 7],
                                                   op0=ALU.mult, op1=ALU.mult))
                ve_st[g] = VE.n

            @blk.vector
            def _(vector, g=g, xv=xv, yv=yv, so=so):
                W(vector, "vector", TE, te_tc.get(g - 2, 0))
                vector.drain()
                VE.inc(vector.tensor_scalar(out=yv, in0=xv, scalar1=stats[:, so + 6:so + 7],
                                            scalar2=stats[:, so + 7:so + 8], op0=ALU.mult, op1=ALU.add))
                se_y[g] = VE.n

            @blk.tensor
            def _(tensor, g=g, yv=yv, par=par):
                W(tensor, "tensor", VE, se_y[g])
                W(tensor, "tensor", VE, ve_ev.get(g - 2, 0))
                W(tensor, "tensor", VE, ve_cm.get(g - 2, 0))
                for k in range(7):
                    tensor.transpose(pbt[par][:, 128 * k:128 * k + 128], yv[:, 128 * k:128 * k + 128], idb[:])
                TE.inc(tensor.transpose(pbt[par][:, 896:1024], yv[:, 896:1024], idb[:]))
                tensor.matmul(pb[4][0:2, 0:512], oseg[:], yv[:, 0:512], start=True, stop=True)
                TE.inc(tensor.matmul(pb[5][0:2, 0:512], oseg[:], yv[:, 512:1024], start=True, stop=True))
                te_tc[g] = TE.n

            @blk.vector
            def _(vector, g=g, par=par, t=t, i=i):
                W(vector, "vector", TE, te_tc[g])
                buf = t % 2
                if t >= 2 and i == 0:
                    W(vector, "vector", TE, te_pj.get(t - 2, 0))
                dstv = yTst[:].rearrange("p (b k u c) -> p b k u c", b=2, k=8, u=4, c=128)[:, buf, :, i, :]
                srcv = pbt[par][:].rearrange("p (k c) -> p k c", k=8, c=128)
                vector.tensor_copy(dstv, srcv)
                W(vector, "vector", SC2, lg_cm.get(g - 2, 0))
                vector.tensor_copy(cm2[:, 1024 * par:1024 * par + 512], pb[4][0:2, 0:512])
                VE.inc(vector.tensor_copy(cm2[:, 1024 * par + 512:1024 * par + 1024], pb[5][0:2, 0:512]))
                ve_cm[g] = VE.n
                ve_ev[g] = VE.n

            @blk.gpsimd
            def _(gpsimd, g=g, par=par):
                W(gpsimd, "gpsimd", SC2, lg_cm.get(g - 1, 0))
                W(gpsimd, "gpsimd", VE, ve_cm[g])
                tix = g // 64
                row = (2 * g) % 128
                SC2.dma(gpsimd.dma_start(out=cmtok[row:row + 2, 1024 * tix:1024 * tix + 1024],
                                        in_=cm2[:, 1024 * par:1024 * par + 1024]))
                lg_cm[g] = SC2.n

            if i == 3:
                buf = t % 2
                pso = 512 * 0

                @blk.tensor
                def _(tensor, t=t, buf=buf, g=g):
                    W(tensor, "tensor", VE, ve_ev[g])
                    W(tensor, "tensor", SE, se_ev.get(t - 2, 0))
                    yview = yTst[:].rearrange("p (b k u) -> p b k u", b=2, k=8, u=512)
                    for k in range(8):
                        tensor.matmul(pb[0][:, 0:512], wkb[:, 128 * k:128 * k + 128],
                                      yview[:, buf, k, :], start=(k == 0), stop=(k == 7))
                    for k in range(8):
                        tensor.matmul(pb[1][:, 0:512], wvb[:, 128 * k:128 * k + 128],
                                      yview[:, buf, k, :], start=(k == 0), stop=(k == 7))
                    for k in range(8):
                        tensor.matmul(pb[2][:, 0:512], wqb[:, 128 * k:128 * k + 128],
                                      yview[:, buf, k, :], start=(k == 0), stop=(k == 7))
                    for k in range(7):
                        tensor.matmul(pb[3][0:2, 0:512], w4b[:, 2 * k:2 * k + 2],
                                      yview[:, buf, k, :], start=(k == 0), stop=False)
                    TE.inc(tensor.matmul(pb[3][0:2, 0:512], w4b[:, 14:16],
                                         yview[:, buf, 7, :], start=False, stop=True))
                    te_pj[t] = TE.n

                @blk.scalar
                def _(scalar, t=t, buf=buf):
                    W(scalar, "scalar", TE, te_pj[t])
                    W(scalar, "scalar", SK, ld_kvr.get(t - 2, 0))
                    W(scalar, "scalar", VE, ve_lr.get(t - 2, 0))
                    o = 3 * 512 * buf
                    scalar.copy(out=kvr[:, o:o + 512], in_=pb[0][:, 0:512])
                    scalar.copy(out=kvr[:, o + 512:o + 1024], in_=pb[1][:, 0:512])
                    scalar.copy(out=kvr[:, o + 1024:o + 1536], in_=pb[2][:, 0:512])
                    SE.inc(scalar.activation(out=sgm[:, 512 * buf:512 * buf + 512],
                                             in_=pb[3][0:2, 0:512], func=AF.Sigmoid))
                    se_ev[t] = SE.n

                @blk.vector
                def _(vector, t=t, buf=buf):
                    W(vector, "vector", SE, se_ev[t])
                    W(vector, "vector", SPK, lg_pk.get(t - 2, 0))
                    VE.inc(vector.tensor_scalar(out=lrrow[:, 512 * buf:512 * buf + 512],
                                                in0=sgm[0:1, 512 * buf:512 * buf + 512],
                                                scalar1=LR_EFF, scalar2=None, op0=ALU.mult))
                    ve_lr[t] = VE.n

                @blk.sync
                def _(sync, t=t, buf=buf):
                    W(sync, "sync", SK, ld_kvr.get(t - 1, 0))
                    W(sync, "sync", SE, se_ev[t])
                    o = 3 * 512 * buf
                    SK.dma(sync.dma_start(out=kTd[:, 512 * t:512 * t + 512], in_=kvr[:, o:o + 512]))
                    SK.dma(sync.dma_start(out=vTd[:, 512 * t:512 * t + 512], in_=kvr[:, o + 512:o + 1024]))
                    SK.dma(sync.dma_start(out=rTd[:, 512 * t:512 * t + 512], in_=kvr[:, o + 1024:o + 1536]))
                    ld_kvr[t] = SK.n

                @blk.gpsimd
                def _(gpsimd, t=t, buf=buf):
                    W(gpsimd, "gpsimd", SPK, lg_pk.get(t - 1, 0))
                    W(gpsimd, "gpsimd", VE, ve_lr[t])
                    SPK.dma(gpsimd.dma_start(out=lrd[0:1, 512 * t:512 * t + 512],
                                            in_=lrrow[0:1, 512 * buf:512 * buf + 512]))
                    gv = sgm[1:2, 512 * buf:512 * buf + 512]
                    if t > 0:
                        SPK.dma(gpsimd.dma_start(out=gswd[4 * t - 1:4 * t, 65:128], in_=gv[:, 0:63]))
                    SPK.dma(gpsimd.dma_start(out=gswd[4 * t:4 * t + 3, :], in_=gv[:, 63:447]))
                    SPK.dma(gpsimd.dma_start(out=gswd[4 * t + 3:4 * t + 4, 0:65], in_=gv[:, 447:512]))
                    lg_pk[t] = SPK.n

        fence()

        if "cm_d" in DEBUG:
            @blk.gpsimd
            def _(gpsimd):
                LG.dma(gpsimd.dma_start(out=dbg["cm_d"][0], in_=cmtok[:, 0:1024]))
                LG.dma(gpsimd.dma_start(out=dbg["cm_d"][1], in_=cmtok[:, 1024:2048]))
                gpsimd.wait_ge(LG.h, LG.n)

        cmbf = E1(nc.sbuf_tensor("cmbf", [128, 1024], BF16))
        cmT = E1(nc.sbuf_tensor("cmT", [128, 8 * 256], BF16))
        mrow = E1(nc.sbuf_tensor("mrow", [1, 256], F32))
        drow = E1(nc.sbuf_tensor("drow", [1, 256], F32))

        for half in range(2):
            @blk.scalar
            def _(scalar, half=half):
                if half == 1:
                    W(scalar, "scalar", VE, VE.n)
                SE.inc(scalar.copy(out=cmbf[:], in_=cmtok[:, 1024 * half:1024 * half + 1024]))

            @blk.tensor
            def _(tensor, half=half):
                W(tensor, "tensor", SE, SE.n)
                if half == 1:
                    W(tensor, "tensor", VE, VE.n)
                for k in range(7):
                    tensor.transpose(pbt[half][:, 128 * k:128 * k + 128],
                                     cmbf[:, 128 * k:128 * k + 128], idb[:])
                TE.inc(tensor.transpose(pbt[half][:, 896:1024], cmbf[:, 896:1024], idb[:]))

            @blk.vector
            def _(vector, half=half):
                W(vector, "vector", TE, TE.n)
                dstv = cmT[:].rearrange("p (k c) -> p k c", k=8, c=256)[:, :, 128 * half:128 * half + 128]
                VE.inc(vector.tensor_copy(dstv, pbt[half][:].rearrange("p (k c) -> p k c", k=8, c=128)))

        @blk.tensor
        def _(tensor):
            W(tensor, "tensor", VE, VE.n)
            for k in range(8):
                tensor.matmul(pb[0][0:1, 0:256], wmdb[:, 2 * k:2 * k + 1],
                              cmT[:, 256 * k:256 * k + 256], start=(k == 0), stop=(k == 7))
            for k in range(7):
                tensor.matmul(pb[3][0:1, 0:256], wmdb[:, 2 * k + 1:2 * k + 2],
                              cmT[:, 256 * k:256 * k + 256], start=(k == 0), stop=False)
            TE.inc(tensor.matmul(pb[3][0:1, 0:256], wmdb[:, 15:16],
                                 cmT[:, 1792:2048], start=False, stop=True))

        @blk.scalar
        def _(scalar):
            W(scalar, "scalar", TE, TE.n)
            scalar.activation(out=mrow[:], in_=pb[0][0:1, 0:256], func=AF.Sigmoid)
            SE.inc(scalar.activation(out=drow[:], in_=pb[3][0:1, 0:256], func=AF.Sigmoid))

        @blk.vector
        def _(vector):
            W(vector, "vector", SE, SE.n)
            VE.inc(vector.tensor_scalar(out=drow[:], in0=drow[:], scalar1=-1.0,
                                        scalar2=1.0, op0=ALU.mult, op1=ALU.add))

        @blk.tensor
        def _(tensor):
            W(tensor, "tensor", VE, VE.n)
            tensor.matmul(pb[1][:, 0:256], onesf[:], mrow[:], start=True, stop=True)
            TE.inc(tensor.matmul(pb[2][:, 0:256], onesf[:], drow[:], start=True, stop=True))

        @blk.vector
        def _(vector):
            W(vector, "vector", TE, TE.n)
            vector.tensor_copy(momb[:], pb[1][:, 0:256])
            VE.inc(vector.tensor_copy(decb[:], pb[2][:, 0:256]))

        fence()
        p1.close()

        if "momb_d" in DEBUG:
            @blk.gpsimd
            def _(gpsimd):
                LG.dma(gpsimd.dma_start(out=dbg["momb_d"][:, :], in_=momb[:]))
                LG.dma(gpsimd.dma_start(out=dbg["decb_d"][:, :], in_=decb[:]))
                gpsimd.wait_ge(LG.h, LG.n)

        p2 = contextlib.ExitStack()
        E2 = p2.enter_context
        kp = E2(nc.sbuf_tensor("kp", [128, 8 * 128], BF16))
        vp = E2(nc.sbuf_tensor("vp", [128, 8 * 128], BF16))
        rp = E2(nc.sbuf_tensor("rp", [128, 8 * 128], BF16))
        grow2 = E2(nc.sbuf_tensor("grow2", [1, 8 * 128], F32))
        lrow2 = E2(nc.sbuf_tensor("lrow2", [1, 8 * 128], F32))
        a1fS = E2(nc.sbuf_tensor("a1fS", [128, 2 * 128], BF16))
        a1tS = E2(nc.sbuf_tensor("a1tS", [128, 2 * 128], BF16))
        sigS = E2(nc.sbuf_tensor("sigS", [128, 2 * 128], BF16))
        dpfS = E2(nc.sbuf_tensor("dpfS", [128, 2 * 128], BF16))
        dptS = E2(nc.sbuf_tensor("dptS", [128, 2 * 128], BF16))
        dzS = E2(nc.sbuf_tensor("dzS", [128, 2 * 128], BF16))
        ktS = E2(nc.sbuf_tensor("ktS", [128, 2 * 128], BF16))
        vtS = E2(nc.sbuf_tensor("vtS", [128, 2 * 128], BF16))
        tmpa = E2(nc.sbuf_tensor("tmpa", [128, 2 * 128], F32))
        tmpb = E2(nc.sbuf_tensor("tmpb", [128, 2 * 128], F32))
        w0cS = E2(nc.sbuf_tensor("w0cS", [128, 2 * 128], BF16))
        w1cS = E2(nc.sbuf_tensor("w1cS", [128, 2 * 128], BF16))
        s1S = E2(nc.sbuf_tensor("s1S", [128, 2 * 128], BF16))
        sqS = E2(nc.sbuf_tensor("sqS", [128, 2 * 128], BF16))
        r0S = E2(nc.sbuf_tensor("r0S", [1, 2 * 128], F32))
        r2S = E2(nc.sbuf_tensor("r2S", [1, 2 * 128], F32))
        valsS = E2(nc.sbuf_tensor("valsS", [128, 2 * 128], BF16))

        ld_pr = {}
        lg_gt = {}
        se_a1 = {}
        se_srow = {}
        se_s1r = {}
        se_sq = {}
        ve_dp = {}
        ve_dpf = {}
        ve_dz = {}
        ve_r2 = {}
        ve_vals = {}
        ve_scanw = {}
        te_z = {}
        te_m = {}
        te_dpt = {}
        te_da = {}
        te_g = {}
        te_r = {}
        te_ssq = {}
        te_sc = {}
        lg_v = {}

        def emit_kv_group(G):
            lane = G % 2
            off = 512 * (G % 2)

            @blk.sync
            def _(sync, G=G, lane=lane, off=off):
                if G >= 2:
                    W(sync, "sync", TE, te_r[(4 * G - 5, 1, 0)])
                KV[lane].dma(sync.dma_start(out=kp[:, off:off + 512], in_=kTd[:, 512 * G:512 * G + 512]))
                KV[lane].dma(sync.dma_start(out=vp[:, off:off + 512], in_=vTd[:, 512 * G:512 * G + 512]))
                KV[lane].dma(sync.dma_start(out=rp[:, off:off + 512], in_=rTd[:, 512 * G + 63:512 * G + 63 + 512]))
                for pp in range(4 * G, min(4 * G + 4, NPAIR)):
                    ld_pr[pp] = (lane, KV[lane].n)

            @blk.gpsimd
            def _(gpsimd, G=G, lane=lane, off=off):
                if G >= 2:
                    W(gpsimd, "gpsimd", VE, ve_r2.get(4 * G - 5, 0))
                RW[lane].dma(gpsimd.dma_start(out=grow2[0:1, off:off + 512], in_=gswd[4 * G:4 * G + 4, :]))
                RW[lane].dma(gpsimd.dma_start(out=lrow2[0:1, off:off + 512], in_=lrd[0:1, 512 * G:512 * G + 512]))
                for pp in range(4 * G, min(4 * G + 4, NPAIR)):
                    lg_gt[pp] = (lane, RW[lane].n)

        emit_kv_group(0)
        emit_kv_group(1)

        for p in range(NPAIR):
            par = p % 2
            o = 128 * par
            s_ = slice(o, o + 128)
            ko = 128 * (p % 8)
            ks_ = slice(ko, ko + 128)
            zb = pb[0] if par == 0 else pb[1]
            wb = pb[2] if par == 0 else pb[3]
            gb0, gb1 = pb[4], pb[5]

            if p % 4 == 0 and p >= 4 and (p // 4 + 1) * 4 < NPAIR + 4:
                G = p // 4 + 1
                if 4 * G < NPAIR:
                    emit_kv_group(G)

            @blk.tensor
            def _(tensor, p=p, ks_=ks_, zb=zb):
                lane, val = ld_pr[p]
                W(tensor, "tensor", KV[lane], val)
                W(tensor, "tensor", SE, se_srow.get(p - 2, 0))
                W(tensor, "tensor", VE, ve_vals.get(p - 2, 0))
                tensor.matmul(zb[:, 0:128], kp[:, ks_], w0b[:], start=True, stop=True)
                TE.inc(tensor.matmul(zb[:, 128:256], w0b[:], kp[:, ks_], start=True, stop=True))
                te_z[p] = TE.n

            @blk.scalar
            def _(scalar, p=p, s_=s_, zb=zb):
                W(scalar, "scalar", TE, te_z[p])
                scalar.activation(out=a1fS[:, s_], in_=zb[:, 128:256], func=AF.Silu)
                scalar.activation(out=a1tS[:, s_], in_=zb[:, 0:128], func=AF.Silu)
                SE.inc(scalar.activation(out=sigS[:, s_], in_=zb[:, 0:128], func=AF.Sigmoid))
                se_a1[p] = SE.n

            @blk.tensor
            def _(tensor, p=p, s_=s_, ks_=ks_, ko=ko, zb=zb, wb=wb, par=par):
                W(tensor, "tensor", SE, se_a1[p])
                W(tensor, "tensor", VE, ve_dpf.get(p - 2, 0))
                lane, val = lg_gt[p]
                W(tensor, "tensor", RW[lane], val)
                W(tensor, "tensor", VE, ve_vals.get(p - 2, 0))
                tensor.matmul(zb[:, 384:512], a1fS[:, s_], w1b[:], start=True, stop=True)
                tensor.matmul(wb[:, 128:129], lrow2[0:1, ko:ko + 128], onec1f[:], start=True, stop=True)
                tensor.transpose(pbt[par][:, 0:128], kp[:, ks_], idb[:])
                TE.inc(tensor.transpose(pbt[par][:, 128:256], vp[:, ks_], idb[:]))
                te_m[p] = TE.n

            @blk.vector
            def _(vector, p=p, s_=s_, zb=zb, wb=wb, par=par):
                W(vector, "vector", TE, te_m[p])
                vector.tensor_copy(ktS[:, s_], pbt[par][:, 0:128])
                vector.tensor_copy(vtS[:, s_], pbt[par][:, 128:256])
                vector.drain()
                vector.tensor_tensor(out=tmpb[:, s_], in0=zb[:, 384:512], in1=vtS[:, s_], op=ALU.subtract)
                vector.drain()
                VE.inc(vector.tensor_scalar(out=dptS[:, s_], in0=tmpb[:, s_], scalar1=wb[:, 128:129],
                                            scalar2=None, op0=ALU.mult))
                ve_dp[p] = VE.n

            @blk.tensor
            def _(tensor, p=p, s_=s_, par=par):
                W(tensor, "tensor", VE, ve_dp[p])
                TE.inc(tensor.transpose(pbt[par][:, 256:384], dptS[:, s_], idb[:]))
                te_dpt[p] = TE.n

            @blk.vector
            def _(vector, p=p, s_=s_, par=par):
                W(vector, "vector", TE, te_dpt[p])
                VE.inc(vector.tensor_copy(dpfS[:, s_], pbt[par][:, 256:384]))
                ve_dpf[p] = VE.n

            @blk.tensor
            def _(tensor, p=p, s_=s_, wb=wb):
                W(tensor, "tensor", VE, ve_dpf[p])
                W(tensor, "tensor", VE, ve_vals.get(p - 2, 0))
                TE.inc(tensor.matmul(wb[:, 256:384], dpfS[:, s_], w1Tb[:], start=True, stop=True))
                te_da[p] = TE.n

            @blk.vector
            def _(vector, p=p, s_=s_, wb=wb):
                W(vector, "vector", TE, te_da[p])
                vector.tensor_scalar(out=tmpa[:, s_], in0=sigS[:, s_], scalar1=-1.0, scalar2=1.0,
                                     op0=ALU.mult, op1=ALU.add)
                vector.drain()
                vector.tensor_tensor(out=tmpa[:, s_], in0=a1tS[:, s_], in1=tmpa[:, s_], op=ALU.mult)
                vector.drain()
                vector.tensor_tensor(out=tmpa[:, s_], in0=tmpa[:, s_], in1=sigS[:, s_], op=ALU.add)
                vector.drain()
                VE.inc(vector.tensor_tensor(out=dzS[:, s_], in0=wb[:, 256:384], in1=tmpa[:, s_], op=ALU.mult))
                ve_dz[p] = VE.n

            @blk.tensor
            def _(tensor, p=p, s_=s_):
                W(tensor, "tensor", VE, ve_dz[p])
                W(tensor, "tensor", VE, ve_vals.get(p - 1, 0))
                tensor.matmul(gb0[:, 0:128], a1tS[0:64, s_], dptS[0:64, s_], start=True, stop=True)
                tensor.matmul(gb0[:, 128:256], ktS[0:64, s_], dzS[0:64, s_], start=True, stop=True)
                tensor.matmul(gb1[:, 0:128], a1tS[64:128, s_], dptS[64:128, s_], start=True, stop=True)
                TE.inc(tensor.matmul(gb1[:, 128:256], ktS[64:128, s_], dzS[64:128, s_], start=True, stop=True))
                te_g[p] = TE.n

            for ci in range(2):
                c = 2 * p + ci
                gbank = gb0 if ci == 0 else gb1

                @blk.vector
                def _(vector, p=p, c=c, ci=ci, gbank=gbank):
                    if ci == 0:
                        W(vector, "vector", TE, te_g[p])
                    vector.drain()
                    vector.scalar_tensor_tensor(out=M1[:], in0=M1[:], scalar=momb[:, c:c + 1],
                                                in1=gbank[:, 0:128], op0=ALU.mult, op1=ALU.subtract)
                    vector.scalar_tensor_tensor(out=M0[:], in0=M0[:], scalar=momb[:, c:c + 1],
                                                in1=gbank[:, 128:256], op0=ALU.mult, op1=ALU.subtract)
                    vector.drain()
                    vector.scalar_tensor_tensor(out=U1[:], in0=U1[:], scalar=decb[:, c:c + 1],
                                                in1=M1[:], op0=ALU.mult, op1=ALU.add)
                    vector.scalar_tensor_tensor(out=U0[:], in0=U0[:], scalar=decb[:, c:c + 1],
                                                in1=M0[:], op0=ALU.mult, op1=ALU.add)
                    vector.drain()
                    vector.tensor_tensor(out=w0cS[:, 128 * ci:128 * ci + 128], in0=U0[:], in1=mw0f[:], op=ALU.add)
                    VE.inc(vector.tensor_tensor(out=w1cS[:, 128 * ci:128 * ci + 128], in0=U1[:], in1=mw1f[:], op=ALU.add))
                    ve_scanw[(p, ci)] = VE.n

                @blk.tensor
                def _(tensor, p=p, ci=ci, ko=ko, wb=wb):
                    W(tensor, "tensor", VE, ve_scanw[(p, ci)])
                    TE.inc(tensor.matmul(wb[:, 384 + 64 * ci:384 + 64 * ci + 64],
                                         w0cS[:, 128 * ci:128 * ci + 128], rp[:, ko + 64 * ci:ko + 64 * ci + 64],
                                         start=True, stop=True))
                    te_r[(p, ci, 0)] = TE.n

            @blk.scalar
            def _(scalar, p=p, o=o, wb=wb):
                for ci in range(2):
                    W(scalar, "scalar", TE, te_r[(p, ci, 0)])
                    SE.inc(scalar.activation(out=s1S[:, o + 64 * ci:o + 64 * ci + 64],
                                             in_=wb[:, 384 + 64 * ci:384 + 64 * ci + 64], func=AF.Silu))
                    se_s1r[(p, ci)] = SE.n

            for ci in range(2):
                gbank = gb0 if ci == 0 else gb1

                @blk.tensor
                def _(tensor, p=p, ci=ci, o=o, gbank=gbank):
                    W(tensor, "tensor", SE, se_s1r[(p, ci)])
                    TE.inc(tensor.matmul(gbank[:, 256 + 64 * ci:256 + 64 * ci + 64],
                                         w1cS[:, 128 * ci:128 * ci + 128],
                                         s1S[:, o + 64 * ci:o + 64 * ci + 64], start=True, stop=True))
                    te_r[(p, ci, 1)] = TE.n

            @blk.vector
            def _(vector, p=p, s_=s_, o=o):
                W(vector, "vector", TE, te_r[(p, 1, 1)])
                vector.tensor_copy(tmpb[:, o:o + 64], gb0[:, 256:320])
                vector.tensor_copy(tmpb[:, o + 64:o + 128], gb1[:, 320:384])
                vector.drain()
                VE.inc(vector.tensor_tensor(out=s1S[:, s_], in0=tmpb[:, s_],
                                            in1=tmpb[:, s_], op=ALU.mult))
                se_sq[p] = VE.n

            @blk.tensor
            def _(tensor, p=p, s_=s_, zb=zb):
                W(tensor, "tensor", VE, se_sq[p])
                TE.inc(tensor.matmul(zb[0:1, 0:128], onescol[:], s1S[:, s_], start=True, stop=True))
                te_ssq[p] = TE.n

            @blk.scalar
            def _(scalar, p=p, par=par, zb=zb):
                W(scalar, "scalar", TE, te_ssq[p])
                SE.inc(scalar.activation(out=r0S[0:1, 128 * par:128 * par + 128], in_=zb[0:1, 0:128],
                                         func=AF.Sqrt, scale=1.0 / DH, bias=cv[0:1, 1:2]))
                se_srow[p] = SE.n

            @blk.vector
            def _(vector, p=p, par=par, ko=ko):
                W(vector, "vector", SE, se_srow[p])
                lane, val = lg_gt[p]
                W(vector, "vector", RW[lane], val)
                vector.reciprocal(r2S[0:1, 128 * par:128 * par + 128], r0S[0:1, 128 * par:128 * par + 128])
                vector.drain()
                VE.inc(vector.tensor_tensor(out=r2S[0:1, 128 * par:128 * par + 128],
                                            in0=r2S[0:1, 128 * par:128 * par + 128],
                                            in1=grow2[0:1, ko:ko + 128], op=ALU.mult))
                ve_r2[p] = VE.n

            @blk.tensor
            def _(tensor, p=p, par=par, wb=wb):
                W(tensor, "tensor", VE, ve_r2[p])
                TE.inc(tensor.matmul(wb[:, 384:512], gp1row[:], r2S[0:1, 128 * par:128 * par + 128],
                                     start=True, stop=True))
                te_sc[p] = TE.n

            @blk.vector
            def _(vector, p=p, s_=s_, o=o, wb=wb):
                W(vector, "vector", TE, te_sc[p])
                if p >= 2:
                    W(vector, "vector", SA[p % 2], lg_v.get(p - 2, 0))
                VE.inc(vector.tensor_tensor(out=valsS[:, s_], in0=tmpb[:, s_], in1=wb[:, 384:512], op=ALU.mult))
                ve_vals[p] = VE.n

            @blk.gpsimd
            def _(gpsimd, p=p, s_=s_):
                W(gpsimd, "gpsimd", VE, ve_vals[p])
                SA[p % 2].dma(gpsimd.dma_start(out=agin[:, 128 * p:128 * p + 128], in_=valsS[:, s_]))
                lg_v[p] = SA[p % 2].n

        @blk.gpsimd
        def _(gpsimd):
            W(gpsimd, "gpsimd", SA[0], SA[0].n)
            W(gpsimd, "gpsimd", SA[1], SA[1].n)
            W(gpsimd, "gpsimd", LG, LG.n)
            CC.inc(gpsimd.collective_compute(
                "AllGather", ALU.bypass,
                replica_groups=[list(range(8))],
                ins=[agin_t.ap()],
                outs=[agout_t.ap()],
            ))

        if "agin_d" in DEBUG:
            @blk.gpsimd
            def _(gpsimd):
                LG.dma(gpsimd.dma_start(out=dbg["agin_d"][:, :], in_=agin[:, :]))
                gpsimd.wait_ge(LG.h, LG.n)

        fence()
        p2.close()

        p3 = contextlib.ExitStack()
        E3 = p3.enter_context
        xst3 = E3(nc.sbuf_tensor("xst3", [128, 2 * 1024], F32))
        xsq3 = E3(nc.sbuf_tensor("xsq3", [128, 2 * 1024], F32))
        xntok = E3(nc.sbuf_tensor("xntok", [128, 2 * 1024], BF16))
        st3 = E3(nc.sbuf_tensor("st3", [128, 2 * 16], F32))
        mx8 = E3(nc.sbuf_tensor("mx8", [128, 8], F32))
        gtmp = E3(nc.sbuf_tensor("gtmp", [128, 32], F32))
        gatesS = E3(nc.sbuf_tensor("gatesS", [128, 8], F32))
        gst3 = E3(nc.sbuf_tensor("gst3", [8, 2 * 128], BF16))
        wg32 = E3(nc.sbuf_tensor("wg32", [128, 64], F32))
        xnT32 = E3(nc.sbuf_tensor("xnT32", [128, 2 * 1024], F32))
        xn32 = E3(nc.sbuf_tensor("xn32", [128, 2 * 1024], F32))

        ld3 = {}
        se3 = {}
        ve3 = {}
        te3 = {}

        @blk.sync
        def _(sync):
            sync.wait_ge(LD.h, LD.n)
            for k in range(8):
                LD.dma(sync.dma_start(out=wg32[:, 8 * k:8 * k + 8], in_=wg_in[128 * k:128 * k + 128, :]))
            ld3["wg"] = LD.n

        for tc in range(16):
            par = tc % 2
            so = 16 * par
            xv = xst3[:, 1024 * par:1024 * par + 1024]
            xqv = xsq3[:, 1024 * par:1024 * par + 1024]
            yv = xntok[:, 1024 * par:1024 * par + 1024]

            @blk.sync
            def _(sync, tc=tc, par=par, xv=xv):
                W(sync, "sync", VE, se3.get((tc - 2, "y"), 0))
                LX[par].dma(sync.dma_start(out=xv, in_=x_slice[128 * tc:128 * tc + 128, :]))
                ld3[tc] = LX[par].n

            @blk.vector
            def _(vector, tc=tc, par=par, xv=xv, xqv=xqv, so=so):
                W(vector, "vector", LX[par], ld3[tc])
                W(vector, "vector", SE, se3.get((tc - 2, "sqrt"), 0))
                vector.reduce_sum(st3[:, so + 0:so + 1], xv, axis=AX.X)
                vector.tensor_tensor(out=xqv, in0=xv, in1=xv, op=ALU.mult)
                vector.drain()
                vector.reduce_sum(st3[:, so + 1:so + 2], xqv, axis=AX.X)
                vector.tensor_scalar(out=st3[:, so + 2:so + 3], in0=st3[:, so + 0:so + 1],
                                     scalar1=1.0 / D, scalar2=None, op0=ALU.mult)
                vector.drain()
                vector.tensor_scalar(out=st3[:, so + 3:so + 4], in0=st3[:, so + 1:so + 2],
                                     scalar1=1.0 / D, scalar2=None, op0=ALU.mult)
                vector.drain()
                VE.inc(vector.scalar_tensor_tensor(out=st3[:, so + 4:so + 5], in0=st3[:, so + 2:so + 3],
                                                   scalar=st3[:, so + 2:so + 3], in1=st3[:, so + 3:so + 4],
                                                   op0=ALU.mult, op1=ALU.subtract))
                ve3[(tc, "var")] = VE.n

            @blk.scalar
            def _(scalar, tc=tc, so=so):
                W(scalar, "scalar", VE, ve3[(tc, "var")])
                SE.inc(scalar.activation(out=st3[:, so + 5:so + 6], in_=st3[:, so + 4:so + 5],
                                         func=AF.Sqrt, scale=-1.0, bias=cv[:, 0:1]))
                se3[(tc, "sqrt")] = SE.n

            @blk.vector
            def _(vector, tc=tc, so=so):
                W(vector, "vector", SE, SE.n)
                vector.reciprocal(st3[:, so + 6:so + 7], st3[:, so + 5:so + 6])
                vector.drain()
                VE.inc(vector.scalar_tensor_tensor(out=st3[:, so + 7:so + 8], in0=st3[:, so + 2:so + 3],
                                                   scalar=-1.0, in1=st3[:, so + 6:so + 7],
                                                   op0=ALU.mult, op1=ALU.mult))
                ve3[(tc, "st")] = VE.n

            @blk.vector
            def _(vector, tc=tc, par=par, xv=xv, yv=yv, so=so):
                W(vector, "vector", TE, te3.get((tc - 2, "tr"), 0))
                vector.drain()
                vector.tensor_scalar(out=xn32[:, 1024 * par:1024 * par + 1024], in0=xv,
                                     scalar1=st3[:, so + 6:so + 7], scalar2=st3[:, so + 7:so + 8],
                                     op0=ALU.mult, op1=ALU.add)
                VE.inc(vector.tensor_scalar(out=yv, in0=xv, scalar1=st3[:, so + 6:so + 7],
                                            scalar2=st3[:, so + 7:so + 8], op0=ALU.mult, op1=ALU.add))
                se3[(tc, "y")] = VE.n

            for half in range(2):
                @blk.tensor
                def _(tensor, tc=tc, par=par, half=half):
                    if half == 0:
                        W(tensor, "tensor", VE, se3[(tc, "y")])
                    W(tensor, "tensor", VE, ve3.get((tc - 2, "ev", half), 0))
                    xsrc = xn32[:, 1024 * par:1024 * par + 1024]
                    hb5 = pb[2] if half == 0 else pb[3]
                    for k in range(3):
                        tensor.transpose(hb5[:, 128 * k:128 * k + 128],
                                         xsrc[:, 512 * half + 128 * k:512 * half + 128 * k + 128], idf[:])
                    TE.inc(tensor.transpose(hb5[:, 384:512], xsrc[:, 512 * half + 384:512 * half + 512], idf[:]))
                    te3[(tc, "tr", half)] = TE.n

                @blk.vector
                def _(vector, tc=tc, par=par, half=half):
                    W(vector, "vector", TE, te3[(tc, "tr", half)])
                    hb5 = pb[2] if half == 0 else pb[3]
                    dstv = xnT[:].rearrange("p (k w) -> p k w", k=8, w=WTOK)[:, 4 * half:4 * half + 4, 128 * tc:128 * tc + 128]
                    vector.tensor_copy(dstv, hb5[:].rearrange("p (k c) -> p k c", k=4, c=128))
                    dst32 = xnT32[:].rearrange("p (b k c) -> p b k c", b=2, k=4, c=128)[:, par, :, :] \
                        if False else xnT32[:, 1024 * par + 512 * half:1024 * par + 512 * half + 512]
                    VE.inc(vector.tensor_copy(dst32, hb5[:, 0:512]))
                    ve3[(tc, "ev", half)] = VE.n
            te3[(tc, "tr")] = te3[(tc, "tr", 1)]
            ve3[(tc, "ev")] = ve3[(tc, "ev", 1)]

            @blk.tensor
            def _(tensor, tc=tc, par=par):
                W(tensor, "tensor", LD, ld3["wg"])
                W(tensor, "tensor", VE, ve3[(tc, "ev")])
                W(tensor, "tensor", VE, ve3.get((tc - 1, "g"), 0))
                for k in range(7):
                    tensor.matmul(pb[0][:, 0:8], xnT32[:, 1024 * par + 128 * k:1024 * par + 128 * k + 128],
                                  wg32[:, 8 * k:8 * k + 8], start=(k == 0), stop=False)
                TE.inc(tensor.matmul(pb[0][:, 0:8], xnT32[:, 1024 * par + 896:1024 * par + 1024],
                                     wg32[:, 56:64], start=False, stop=True))
                te3[(tc, "lg")] = TE.n

            @blk.vector
            def _(vector, tc=tc):
                W(vector, "vector", TE, te3[(tc, "lg")])
                VE.inc(vector.max(mx8[:], pb[0][:, 0:8]))
                ve3[(tc, "mx")] = VE.n

            @blk.vector
            def _(vector, tc=tc):
                vector.drain()
                VE.inc(vector.tensor_tensor(out=gtmp[:, 10:11], in0=mx8[:, 0:1], in1=mx8[:, 1:2], op=ALU.subtract))
                ve3[(tc, "d")] = VE.n

            @blk.scalar
            def _(scalar, tc=tc):
                W(scalar, "scalar", VE, ve3[(tc, "d")])
                SE.inc(scalar.activation(out=gtmp[:, 11:12], in_=gtmp[:, 10:11], func=AF.Sigmoid))
                se3[(tc, "w1")] = SE.n

            @blk.vector
            def _(vector, tc=tc):
                W(vector, "vector", SE, se3[(tc, "w1")])
                vector.tensor_scalar(out=gtmp[:, 12:13], in0=gtmp[:, 11:12], scalar1=-1.0, scalar2=1.0,
                                     op0=ALU.mult, op1=ALU.add)
                vector.tensor_scalar(out=gtmp[:, 16:24], in0=pb[0][:, 0:8], scalar1=mx8[:, 0:1],
                                     scalar2=None, op0=ALU.is_equal)
                vector.tensor_scalar(out=gtmp[:, 24:32], in0=pb[0][:, 0:8], scalar1=mx8[:, 1:2],
                                     scalar2=None, op0=ALU.is_equal)
                vector.drain()
                vector.tensor_scalar(out=gtmp[:, 16:24], in0=gtmp[:, 16:24], scalar1=gtmp[:, 11:12],
                                     scalar2=None, op0=ALU.mult)
                vector.tensor_scalar(out=gtmp[:, 24:32], in0=gtmp[:, 24:32], scalar1=gtmp[:, 12:13],
                                     scalar2=None, op0=ALU.mult)
                vector.drain()
                VE.inc(vector.tensor_tensor(out=gatesS[:], in0=gtmp[:, 16:24], in1=gtmp[:, 24:32], op=ALU.add))
                ve3[(tc, "g")] = VE.n

            @blk.tensor
            def _(tensor, tc=tc):
                W(tensor, "tensor", VE, ve3[(tc, "g")])
                TE.inc(tensor.transpose(pb[1][0:8, 0:128], gatesS[:], idf[:]))
                te3[(tc, "gt")] = TE.n

            @blk.vector
            def _(vector, tc=tc, par=par):
                W(vector, "vector", TE, te3[(tc, "gt")])
                W(vector, "vector", SG3, ve3.get((tc - 2, "grdma"), 0))
                VE.inc(vector.tensor_copy(gst3[:, 128 * par:128 * par + 128], pb[1][0:8, 0:128]))
                ve3[(tc, "gr")] = VE.n

            @blk.gpsimd
            def _(gpsimd, tc=tc, par=par):
                W(gpsimd, "gpsimd", SG3, ve3.get((tc - 1, "grdma"), 0))
                W(gpsimd, "gpsimd", VE, ve3[(tc, "gr")])
                SG3.dma(gpsimd.dma_start(out=gaterd[:, 128 * tc:128 * tc + 128],
                                        in_=gst3[0:8, 128 * par:128 * par + 128]))
                ve3[(tc, "grdma")] = SG3.n

        fence()
        p3.close()

        if "xnT_d" in DEBUG:
            @blk.gpsimd
            def _(gpsimd):
                for k in range(8):
                    LG.dma(gpsimd.dma_start(out=dbg["xnT_d"][k], in_=xnT[:, WTOK * k:WTOK * k + WTOK]))
                gpsimd.wait_ge(LG.h, LG.n)

        p4 = contextlib.ExitStack()
        E4 = p4.enter_context
        wst4 = E4(nc.sbuf_tensor("wst4", [128, 2 * 1024], F32))
        w1q = E4(nc.sbuf_tensor("w1q", [128, 2 * 8 * 1024], BF16))
        w2q = E4(nc.sbuf_tensor("w2q", [128, 2 * 8 * 1024], BF16))
        hbuf = E4(nc.sbuf_tensor("hbuf", [128, 2 * 8 * 512], BF16))
        gbc = E4(nc.sbuf_tensor("gbc", [128, WTOK], BF16))
        grow4 = E4(nc.sbuf_tensor("grow4", [1, WTOK], BF16))

        ld4 = {}
        cv4 = {}
        te4 = {}
        ve4 = {}
        se4 = {}
        h_ve = []
        acc_ve = []
        l2_te = []
        gmul_last = {}

        l2done = {}

        xnv = xnT[:].rearrange("p (k w) -> p k w", k=8, w=WTOK)

        def stage_step(eq, kk):
            if eq >= 32:
                return
            e, q = divmod(eq, 4)
            wo = 8192 * (eq % 2)
            hh = kk % 2

            def _dma(eng, ename):
                gk = 16 * eq + kk
                if gk >= 2:
                    peq, pkk = divmod(gk - 2, 16)
                    W(eng, ename, VE, cv4.get((peq, pkk), 0))
                if kk < 8:
                    src = w1_in[e, 128 * kk:128 * kk + 128, 1024 * q:1024 * q + 1024]
                else:
                    c2 = kk - 8
                    src = w2_in[e, 1024 * q + 128 * c2:1024 * q + 128 * c2 + 128, 0:1024]
                LW[hh].dma(eng.dma_start(out=wst4[:, 1024 * hh:1024 * hh + 1024], in_=src))
                ld4[(eq, kk)] = LW[hh].n

            if hh == 0:
                @blk.sync
                def _(sync):
                    _dma(sync, "sync")
            else:
                @blk.gpsimd
                def _(gpsimd):
                    _dma(gpsimd, "gpsimd")

            @blk.vector
            def _(vector):
                W(vector, "vector", LW[hh], ld4[(eq, kk)])
                if kk == 0:
                    W(vector, "vector", TE, te4.get((eq - 2, "done"), 0))
                dst = w1q if kk < 8 else w2q
                ki = kk % 8
                VE.inc(vector.tensor_copy(dst[:, wo + 1024 * ki:wo + 1024 * ki + 1024],
                                          wst4[:, 1024 * hh:1024 * hh + 1024]))
                cv4[(eq, kk)] = VE.n

        def emit_gbc(eq, e):
            @blk.gpsimd
            def _(gpsimd):
                W(gpsimd, "gpsimd", SG4, ld4.get((eq - 4, "grow"), 0))
                if e > 0:
                    W(gpsimd, "gpsimd", TE, te4.get((4 * e - 1, "gbcmm"), 0))
                SG4.dma(gpsimd.dma_start(out=grow4[:], in_=gaterd[e:e + 1, :]))
                ld4[(eq, "grow")] = SG4.n

            for tt in range(4):
                @blk.tensor
                def _(tensor, tt=tt):
                    W(tensor, "tensor", SG4, ld4[(eq, "grow")])
                    if e > 0:
                        W(tensor, "tensor", VE, gmul_last.get(e - 1, 0))
                    W(tensor, "tensor", VE, ve4.get((eq, "gbc", tt - 1), 0))
                    if h_ve:
                        W(tensor, "tensor", VE, h_ve[-1])
                    TE.inc(tensor.matmul(pb[4][:, 0:512], ones1[:],
                                         grow4[0:1, 512 * tt:512 * tt + 512], start=True, stop=True))
                    te4[(eq, "gbc", tt)] = TE.n
                    te4[(eq + 3, "gbcmm")] = TE.n

                @blk.vector
                def _(vector, tt=tt):
                    W(vector, "vector", TE, te4[(eq, "gbc", tt)])
                    VE.inc(vector.tensor_copy(gbc[:, 512 * tt:512 * tt + 512], pb[4][:, 0:512]))
                    ve4[(eq, "gbc", tt)] = VE.n
            ve4[(eq, "gbc")] = VE.n

        L1BANKS = [pb[0], pb[1], pb[5], pb[4]]

        def emit_L1(eq, tt):
            e = eq // 4
            wo = 8192 * (eq % 2)
            ho = 4096 * (tt % 2)
            for grp in range(2):
                ms = range(4 * grp, 4 * grp + 4)
                for m in ms:
                    gm = 32 * eq + 8 * tt + m
                    hb = L1BANKS[gm % 4]

                    @blk.tensor
                    def _(tensor, m=m, hb=hb, gm=gm):
                        if m == 0 and tt == 0:
                            W(tensor, "tensor", VE, cv4[(eq, 7)])
                            W(tensor, "tensor", VE, ve4[(4 * (eq // 4), "gbc")])
                        if gm >= 4:
                            W(tensor, "tensor", VE, h_ve[gm - 4])
                        for k in range(7):
                            tensor.matmul(hb[:, 0:512],
                                          w1q[:, wo + 1024 * k + 128 * m:wo + 1024 * k + 128 * m + 128],
                                          xnv[:, k, 512 * tt:512 * tt + 512], start=(k == 0), stop=False)
                        TE.inc(tensor.matmul(hb[:, 0:512],
                                             w1q[:, wo + 7 * 1024 + 128 * m:wo + 7 * 1024 + 128 * m + 128],
                                             xnv[:, 7, 512 * tt:512 * tt + 512], start=False, stop=True))
                        te4[(eq, tt, m)] = TE.n

                @blk.scalar
                def _(scalar, ms=ms):
                    for m in ms:
                        gm = 32 * eq + 8 * tt + m
                        W(scalar, "scalar", TE, te4[(eq, tt, m)])
                        SE.inc(scalar.activation(out=L1BANKS[gm % 4][:, 0:512],
                                                 in_=L1BANKS[gm % 4][:, 0:512], func=AF.Silu))
                        se4[(eq, tt, m)] = SE.n

                @blk.vector
                def _(vector, ms=ms, ho=ho):
                    for m in ms:
                        gm = 32 * eq + 8 * tt + m
                        W(vector, "vector", SE, se4[(eq, tt, m)])
                        if m == 0:
                            prev = (eq, tt - 2) if tt >= 2 else (eq - 1, tt + 2)
                            W(vector, "vector", TE, l2done.get(prev, 0))
                        VE.inc(vector.tensor_tensor(out=hbuf[:, ho + 512 * m:ho + 512 * m + 512],
                                                    in0=L1BANKS[gm % 4][:, 0:512],
                                                    in1=gbc[:, 512 * tt:512 * tt + 512],
                                                    op=ALU.mult))
                        ve4[(eq, tt, m)] = VE.n
                        h_ve.append(VE.n)
                        if m == 7:
                            gmul_last[e] = VE.n

        def emit_L2(eq, tt):
            wo = 8192 * (eq % 2)
            ho = 4096 * (tt % 2)
            for oc in range(8):
                go = 32 * eq + 8 * tt + oc
                ob = pb[2] if go % 2 == 0 else pb[3]

                @blk.tensor
                def _(tensor, oc=oc, ob=ob, go=go):
                    if tt == 0 and oc == 0:
                        W(tensor, "tensor", VE, cv4[(eq, 15)])
                    W(tensor, "tensor", VE, ve4[(eq, tt, 7)])
                    if go >= 2:
                        W(tensor, "tensor", VE, acc_ve[go - 2])
                    for m in range(7):
                        tensor.matmul(ob[:, 0:512],
                                      w2q[:, wo + 1024 * m + 128 * oc:wo + 1024 * m + 128 * oc + 128],
                                      hbuf[:, ho + 512 * m:ho + 512 * m + 512], start=(m == 0), stop=False)
                    TE.inc(tensor.matmul(ob[:, 0:512],
                                         w2q[:, wo + 7 * 1024 + 128 * oc:wo + 7 * 1024 + 128 * oc + 128],
                                         hbuf[:, ho + 512 * 7:ho + 512 * 7 + 512], start=False, stop=True))
                    te4[(eq, tt, "l2", oc)] = TE.n

                @blk.vector
                def _(vector, oc=oc, ob=ob):
                    W(vector, "vector", TE, te4[(eq, tt, "l2", oc)])
                    VE.inc(vector.tensor_tensor(
                        out=gacc[:, WTOK * oc + 512 * tt:WTOK * oc + 512 * tt + 512],
                        in0=gacc[:, WTOK * oc + 512 * tt:WTOK * oc + 512 * tt + 512],
                        in1=ob[:, 0:512], op=ALU.add))
                    acc_ve.append(VE.n)
            l2done[(eq, tt)] = TE.n
            if tt == 3:
                te4[(eq, "done")] = TE.n

        for kk in range(16):
            stage_step(0, kk)

        SEQ = [("L1", 0), ("L1", 1), ("L2", 0), ("L1", 2), ("L2", 1), ("L1", 3), ("L2", 2), ("L2", 3)]
        for eq in range(32):
            e, q = divmod(eq, 4)
            if q == 0:
                emit_gbc(eq, e)
            for bi, (kind, tt) in enumerate(SEQ):
                if kind == "L1":
                    emit_L1(eq, tt)
                else:
                    emit_L2(eq, tt)
                stage_step(eq + 1, 2 * bi)
                stage_step(eq + 1, 2 * bi + 1)

        fence()
        p4.close()

        if "gacc_d" in DEBUG:
            @blk.gpsimd
            def _(gpsimd):
                for oc in range(8):
                    LG.dma(gpsimd.dma_start(out=dbg["gacc_d"][oc], in_=gacc[:, WTOK * oc:WTOK * oc + WTOK]))
                gpsimd.wait_ge(LG.h, LG.n)

        p5 = contextlib.ExitStack()
        E5 = p5.enter_context
        wcst = E5(nc.sbuf_tensor("wcst", [128, 1024], F32))
        wcb = E5(nc.sbuf_tensor("wcb", [128, 8 * 1024], BF16))
        vw = E5(nc.sbuf_tensor("vw", [128, 8 * 2176], BF16))
        sgt = E5(nc.sbuf_tensor("sgt", [128, 2 * 512], F32))
        gsl = E5(nc.sbuf_tensor("gsl", [128, 2 * 512], F32))
        otok = E5(nc.sbuf_tensor("otok", [128, 4 * 1024], F32))
        xsl5 = E5(nc.sbuf_tensor("xsl5", [128, 2 * 1024], F32))

        ld5 = {}
        se5 = {}
        ve5 = {}
        te5 = {}
        lg5 = {}
        so5 = {}

        for k in range(8):
            @blk.sync
            def _(sync, k=k):
                if k == 0:
                    sync.wait_ge(LD.h, LD.n)
                W(sync, "sync", SE, se5.get(("wc", k - 1), 0))
                LD.dma(sync.dma_start(out=wcst[:], in_=wc_in[128 * k:128 * k + 128, :]))
                ld5[("wc", k)] = LD.n

            @blk.scalar
            def _(scalar, k=k):
                W(scalar, "scalar", LD, ld5[("wc", k)])
                SE.inc(scalar.copy(out=wcb[:, 1024 * k:1024 * k + 1024], in_=wcst[:]))
                se5[("wc", k)] = SE.n

        @blk.gpsimd
        def _(gpsimd):
            W(gpsimd, "gpsimd", CC, CC.n)
            mainview = agout.rearrange("h p (j t) -> (h p j) t", t=2048)
            bndview = agout.rearrange("h p (r t) -> (h p r) t", t=128)
            for h in range(8):
                SV5.dma(gpsimd.indirect_dma_start(
                    out=vw[:, 2176 * h + 128:2176 * h + 2176], out_offset=None,
                    in_=mainview, in_offset=bass.IndirectOffsetOnAxis(ap=widxS[:, h:h + 1], axis=0)))
                SV5.dma(gpsimd.indirect_dma_start(
                    out=vw[:, 2176 * h:2176 * h + 128], out_offset=None,
                    in_=bndview, in_offset=bass.IndirectOffsetOnAxis(ap=widxS[:, 8 + h:8 + h + 1], axis=0)))
            lg5["vw"] = SV5.n

        @blk.vector
        def _(vector):
            W(vector, "vector", SV5, lg5["vw"])
            for h in range(7):
                vector.tensor_scalar(out=vw[:, 2176 * h:2176 * h + 128], in0=vw[:, 2176 * h:2176 * h + 128],
                                     scalar1=bscaleS[:, 0:1], scalar2=None, op0=ALU.mult)
            VE.inc(vector.tensor_scalar(out=vw[:, 2176 * 7:2176 * 7 + 128], in0=vw[:, 2176 * 7:2176 * 7 + 128],
                                        scalar1=bscaleS[:, 0:1], scalar2=None, op0=ALU.mult))
            ve5["vw"] = VE.n

        sg_hist = []
        ot_hist = []
        for t in range(4):
            obuf = 0

            for oc in range(8):
                sb = pb[0] if oc % 2 == 0 else pb[1]
                spar = oc % 2
                gidx5 = 8 * t + oc

                @blk.tensor
                def _(tensor, t=t, oc=oc, sb=sb, gidx5=gidx5):
                    if t == 0 and oc == 0:
                        W(tensor, "tensor", SE, se5[("wc", 7)])
                        W(tensor, "tensor", VE, ve5["vw"])
                    if gidx5 >= 2:
                        W(tensor, "tensor", VE, sg_hist[gidx5 - 2])
                    for h in range(7):
                        tensor.matmul(sb[:, 0:512], wcb[:, 1024 * h + 128 * oc:1024 * h + 128 * oc + 128],
                                      vw[:, 2176 * h + 65 + 512 * t:2176 * h + 65 + 512 * t + 512],
                                      start=(h == 0), stop=False)
                    TE.inc(tensor.matmul(sb[:, 0:512], wcb[:, 7 * 1024 + 128 * oc:7 * 1024 + 128 * oc + 128],
                                         vw[:, 2176 * 7 + 65 + 512 * t:2176 * 7 + 65 + 512 * t + 512],
                                         start=False, stop=True))
                    te5[("s", t, oc)] = TE.n

                @blk.scalar
                def _(scalar, t=t, oc=oc, spar=spar, gidx5=gidx5):
                    W(scalar, "scalar", TE, te5[("s", t, oc)])
                    if gidx5 >= 2:
                        W(scalar, "scalar", VE, sg_hist[gidx5 - 2])
                    SE.inc(scalar.activation(out=gsl[:, 512 * spar:512 * spar + 512],
                                             in_=gacc[:, WTOK * oc + 512 * t:WTOK * oc + 512 * t + 512],
                                             func=AF.Silu))
                    se5[("g", t, oc)] = SE.n

                @blk.vector
                def _(vector, t=t, oc=oc, sb=sb, spar=spar, gidx5=gidx5):
                    W(vector, "vector", SE, se5[("g", t, oc)])
                    if gidx5 >= 2:
                        W(vector, "vector", TE, te5.get(("tr",) + divmod(gidx5 - 2, 8), 0))
                    VE.inc(vector.tensor_tensor(out=sgt[:, 512 * spar:512 * spar + 512],
                                                in0=sb[:, 0:512], in1=gsl[:, 512 * spar:512 * spar + 512],
                                                op=ALU.mult))
                    ve5[("sg", t, oc)] = VE.n
                    sg_hist.append(VE.n)

                @blk.tensor
                def _(tensor, t=t, oc=oc, spar=spar, gidx5=gidx5):
                    W(tensor, "tensor", VE, ve5[("sg", t, oc)])
                    tb = pb[2] if oc % 2 == 0 else pb[3]
                    if gidx5 >= 2:
                        W(tensor, "tensor", VE, ot_hist[gidx5 - 2])
                    for i in range(3):
                        tensor.transpose(tb[:, 128 * i:128 * i + 128], sgt[:, 512 * spar + 128 * i:512 * spar + 128 * i + 128], idf[:])
                    TE.inc(tensor.transpose(tb[:, 384:512], sgt[:, 512 * spar + 384:512 * spar + 512], idf[:]))
                    te5[("tr", t, oc)] = TE.n

                @blk.vector
                def _(vector, t=t, oc=oc, obuf=obuf, gidx5=gidx5):
                    W(vector, "vector", TE, te5[("tr", t, oc)])
                    tb = pb[2] if oc % 2 == 0 else pb[3]
                    if oc == 0:
                        W(vector, "vector", SO, so5.get(("tot", t - 1), 0))
                    for i in range(3):
                        vector.tensor_copy(otok[:, 1024 * i + 128 * oc:1024 * i + 128 * oc + 128],
                                           tb[:, 128 * i:128 * i + 128])
                    VE.inc(vector.tensor_copy(otok[:, 3072 + 128 * oc:3072 + 128 * oc + 128],
                                              tb[:, 384:512]))
                    ve5[("ot", t, oc)] = VE.n
                    ot_hist.append(VE.n)

            for i in range(4):
                @blk.sync
                def _(sync, t=t, i=i):
                    gx = 4 * t + i
                    if gx >= 2:
                        pt, pi = divmod(gx - 2, 4)
                        W(sync, "sync", VE, ve5.get(("add", pt, pi), 0))
                    LX[i % 2].dma(sync.dma_start(out=xsl5[:, 1024 * (i % 2):1024 * (i % 2) + 1024],
                                                 in_=x_slice[512 * t + 128 * i:512 * t + 128 * i + 128, :]))
                    ld5[("x", t, i)] = LX[i % 2].n

                @blk.vector
                def _(vector, t=t, i=i):
                    W(vector, "vector", VE, ve5[("ot", t, 7)])
                    W(vector, "vector", LX[i % 2], ld5[("x", t, i)])
                    VE.inc(vector.tensor_tensor(out=otok[:, 1024 * i:1024 * i + 1024],
                                                in0=otok[:, 1024 * i:1024 * i + 1024],
                                                in1=xsl5[:, 1024 * (i % 2):1024 * (i % 2) + 1024], op=ALU.add))
                    ve5[("add", t, i)] = VE.n

                @blk.sync
                def _(sync, t=t, i=i):
                    W(sync, "sync", VE, ve5[("add", t, i)])
                    SO.dma(sync.dma_start(out=out[512 * t + 128 * i:512 * t + 128 * i + 128, :],
                                          in_=otok[:, 1024 * i:1024 * i + 1024]))
                    so5[("out", t, i)] = SO.n
                    so5[("tot", t)] = SO.n

        @blk.sync
        def _(sync):
            sync.wait_ge(SO.h, SO.n)

        p5.close()

    st.close()
    return nc


_CACHE = {}


def _build_cached():
    key = tuple(sorted(DEBUG))
    if key not in _CACHE:
        _CACHE[key] = build_program()
    return _CACHE[key]


def kernel(L0_attn_seq, L0_attn_store_norm_g, L0_attn_retrieve_norm_g, L0_attn_w_kv,
           L0_attn_w_q, L0_attn_w_step, L0_attn_w_mom, L0_attn_w_decay, L0_attn_w_gate,
           L0_attn_w_combine, L0_attn_mh_gamma, L0_attn_mem_w0, L0_attn_mem_w1,
           L0_ffn_Wg, L0_ffn_W1, L0_ffn_W2):
    nc = _build_cached()
    f32 = np.float32
    x = np.ascontiguousarray(np.asarray(L0_attn_seq, dtype=f32).reshape(NTOK, D))
    w_kv = np.asarray(L0_attn_w_kv, dtype=f32)
    w_q = np.asarray(L0_attn_w_q, dtype=f32)
    w_step = np.asarray(L0_attn_w_step, dtype=f32)
    w_mom = np.asarray(L0_attn_w_mom, dtype=f32)
    w_decay = np.asarray(L0_attn_w_decay, dtype=f32)
    w_gate = np.asarray(L0_attn_w_gate, dtype=f32)
    w_comb = np.ascontiguousarray(np.asarray(L0_attn_w_combine, dtype=f32))
    mh_gamma = np.asarray(L0_attn_mh_gamma, dtype=f32)
    mem_w0 = np.ascontiguousarray(np.asarray(L0_attn_mem_w0, dtype=f32))
    mem_w1 = np.ascontiguousarray(np.asarray(L0_attn_mem_w1, dtype=f32))
    wg = np.ascontiguousarray(np.asarray(L0_ffn_Wg, dtype=f32))
    w1 = np.ascontiguousarray(np.asarray(L0_ffn_W1, dtype=f32))
    w2 = np.ascontiguousarray(np.asarray(L0_ffn_W2, dtype=f32))
    sgain = np.ascontiguousarray(np.asarray(L0_attn_store_norm_g, dtype=f32).reshape(8, 128))
    rgain = np.ascontiguousarray(np.asarray(L0_attn_retrieve_norm_g, dtype=f32).reshape(8, 128))

    idb = np.eye(128, dtype=ml_dtypes.bfloat16)
    idf = np.eye(128, dtype=np.float32)
    oseg = np.zeros((128, 2), dtype=ml_dtypes.bfloat16)
    oseg[0:64, 0] = ml_dtypes.bfloat16(1.0 / 64)
    oseg[64:128, 1] = ml_dtypes.bfloat16(1.0 / 64)
    cvc = np.zeros((128, 4), dtype=f32)
    cvc[:, 0] = EPS_LN
    cvc[:, 1] = EPS_RMS
    cvc[:, 2] = EPS_RMS * EPS_LN
    zf = np.zeros((128, 128), dtype=f32)

    in_maps = []
    for i in range(8):
        widx = np.zeros((128, 16), dtype=np.uint32)
        pr = np.arange(128, dtype=np.uint32)
        for h in range(8):
            widx[:, h] = (h * 128 + pr) * 8 + i
            widx[:, 8 + h] = (h * 128 + pr) * 128 + (16 * i - 1 if i > 0 else 0)
        bscale = np.full((128, 1), 1.0 if i > 0 else 0.0, dtype=f32)
        m = {
            "x_full": x,
            "x_slice": np.ascontiguousarray(x[WTOK * i:WTOK * (i + 1)]),
            "wk_h": np.ascontiguousarray(w_kv[:, 128 * i:128 * i + 128]),
            "wv_h": np.ascontiguousarray(w_kv[:, 1024 + 128 * i:1024 + 128 * i + 128]),
            "wq_h": np.ascontiguousarray(w_q[:, 128 * i:128 * i + 128]),
            "w4_h": np.ascontiguousarray(np.stack([w_step[:, i], w_gate[:, i]], axis=1)),
            "wmd_h": np.ascontiguousarray(np.stack([w_mom[:, i], w_decay[:, i]], axis=1)),
            "sgain": sgain, "rgain": rgain,
            "mw0": mem_w0, "mw1": mem_w1,
            "gamma_row": np.ascontiguousarray(mh_gamma[i].reshape(1, 128)),
            "wg_in": wg, "w1_in": w1, "w2_in": w2, "wc_in": w_comb,
            "idb_in": idb, "idf_in": idf, "oseg_in": oseg, "cv_in": cvc,
            "widx_in": widx, "bscale_in": bscale, "zf_in": zf,
        }
        in_maps.append(m)

    res = run_bass_kernel_spmd(nc, in_maps, core_ids=list(range(8)))
    outs = [res.results[i]["out"] for i in range(8)]
    full = np.concatenate(outs, axis=0).reshape(1, NTOK, D).astype(np.float32)
    if DEBUG:
        kernel._debug_results = res
    return full


# version 07
# speedup vs baseline: 1.1775x; 1.1775x over previous
import contextlib
import numpy as np
import ml_dtypes
import concourse.bass as bass
import concourse.mybir as mybir
from concourse.bass_utils import run_bass_kernel_spmd

F32 = mybir.dt.float32
BF16 = mybir.dt.bfloat16
U32 = mybir.dt.uint32
AF = mybir.ActivationFunctionType
ALU = mybir.AluOpType
AX = mybir.AxisListType

H = 8
DH = 128
CH = 64
D = 1024
NTOK = 16384
NCHUNK = NTOK // CH
NPAIR = NCHUNK // 2
WTOK = NTOK // 8
EPS_LN = 1e-5
EPS_RMS = 1e-6
LR_EFF = 0.01 * 2.0 / DH

DEBUG = []


class Sem:
    def __init__(self, stack, nc, name):
        self.h = stack.enter_context(nc.semaphore(name))
        self.name = name
        self.n = 0

    def inc(self, instr, k=1):
        instr.then_inc(self.h, k)
        self.n += k
        return instr

    def dma(self, instr):
        return self.inc(instr, 16)


_WAITED = {}


def W(eng, ename, sem, val):
    if val is None or val <= 0:
        return
    key = (ename, sem.name)
    if _WAITED.get(key, -1) >= val:
        return
    _WAITED[key] = val
    eng.wait_ge(sem.h, val)


def build_program():
    _WAITED.clear()
    nc = bass.Bass("TRN2", target_bir_lowering=False)

    x_full = nc.dram_tensor("x_full", [NTOK, D], F32, kind="ExternalInput").ap()
    x_slice = nc.dram_tensor("x_slice", [WTOK, D], F32, kind="ExternalInput").ap()
    wk_h = nc.dram_tensor("wk_h", [D, DH], F32, kind="ExternalInput").ap()
    wv_h = nc.dram_tensor("wv_h", [D, DH], F32, kind="ExternalInput").ap()
    wq_h = nc.dram_tensor("wq_h", [D, DH], F32, kind="ExternalInput").ap()
    w4_h = nc.dram_tensor("w4_h", [D, 2], F32, kind="ExternalInput").ap()
    wmd_h = nc.dram_tensor("wmd_h", [D, 2], F32, kind="ExternalInput").ap()
    sgain_in = nc.dram_tensor("sgain", [8, 128], F32, kind="ExternalInput").ap()
    rgain_in = nc.dram_tensor("rgain", [8, 128], F32, kind="ExternalInput").ap()
    mw0_in = nc.dram_tensor("mw0", [128, 128], F32, kind="ExternalInput").ap()
    mw1_in = nc.dram_tensor("mw1", [128, 128], F32, kind="ExternalInput").ap()
    gamma_in = nc.dram_tensor("gamma_row", [1, 128], F32, kind="ExternalInput").ap()
    wg_in = nc.dram_tensor("wg_in", [D, H], F32, kind="ExternalInput").ap()
    w1_in = nc.dram_tensor("w1_in", [8, D, 4096], F32, kind="ExternalInput").ap()
    w2_in = nc.dram_tensor("w2_in", [8, 4096, D], F32, kind="ExternalInput").ap()
    wc_in = nc.dram_tensor("wc_in", [D, D], F32, kind="ExternalInput").ap()
    idb_in = nc.dram_tensor("idb_in", [128, 128], BF16, kind="ExternalInput").ap()
    idf_in = nc.dram_tensor("idf_in", [128, 128], F32, kind="ExternalInput").ap()
    oseg_in = nc.dram_tensor("oseg_in", [128, 2], BF16, kind="ExternalInput").ap()
    cv_in = nc.dram_tensor("cv_in", [128, 4], F32, kind="ExternalInput").ap()
    widx_in = nc.dram_tensor("widx_in", [128, 16], U32, kind="ExternalInput").ap()
    bscale_in = nc.dram_tensor("bscale_in", [128, 1], F32, kind="ExternalInput").ap()
    zf_in = nc.dram_tensor("zf_in", [128, 128], F32, kind="ExternalInput").ap()
    out = nc.dram_tensor("out", [WTOK, D], F32, kind="ExternalOutput").ap()

    def scratch(name, shape, dtype):
        if name in DEBUG:
            return nc.dram_tensor(name, shape, dtype, kind="ExternalOutput").ap()
        return nc.dram_tensor(name, shape, dtype).ap()

    kTd = scratch("kTd", [128, NTOK], BF16)
    vTd = scratch("vTd", [128, NTOK], BF16)
    rTd = scratch("rTd", [128, NTOK + 64], BF16)
    gswd = scratch("gswd", [128, 128], F32)
    lrd = scratch("lrd", [1, NTOK], F32)
    gaterd = scratch("gaterd", [8, WTOK], BF16)
    agin_t = nc.dram_tensor("agin", [128, NTOK], BF16)
    agout_t = nc.dram_tensor("agout", [8, 128, NTOK], BF16)
    agin = agin_t.ap()
    agout = agout_t.ap()
    dbg = {}
    for nm, shape, dt in [
        ("agin_d", [128, NTOK], BF16),
        ("momb_d", [128, 256], F32), ("decb_d", [128, 256], F32),
        ("cm_d", [2, 128, 1024], F32),
        ("gacc_d", [8, 128, WTOK], F32),
        ("xnT_d", [8, 128, WTOK], BF16),
    ]:
        if nm in DEBUG:
            dbg[nm] = nc.dram_tensor(nm, shape, dt, kind="ExternalOutput").ap()

    st = contextlib.ExitStack()
    E = st.enter_context

    LD = Sem(st, nc, "LD")
    LG = Sem(st, nc, "LG")
    LX = [Sem(st, nc, "LX0"), Sem(st, nc, "LX1")]
    SK = Sem(st, nc, "SK")
    LW = [Sem(st, nc, "LW0"), Sem(st, nc, "LW1")]
    KV = [Sem(st, nc, "KV0"), Sem(st, nc, "KV1")]
    RW = [Sem(st, nc, "RW0"), Sem(st, nc, "RW1")]
    SA = [Sem(st, nc, "SA0"), Sem(st, nc, "SA1")]
    SC2 = Sem(st, nc, "SC2")
    SPK = Sem(st, nc, "SPK")
    SG3 = Sem(st, nc, "SG3")
    SG4 = Sem(st, nc, "SG4")
    SV5 = Sem(st, nc, "SV5")
    TE = Sem(st, nc, "TE")
    VE = Sem(st, nc, "VE")
    SE = Sem(st, nc, "SE")
    CC = Sem(st, nc, "CC")
    SO = Sem(st, nc, "SO")
    FN = Sem(st, nc, "FN")

    pb = [E(nc.psum_tensor(f"pb{i}", [128, 512], F32)) for i in range(6)]
    pbt = [E(nc.psum_tensor(f"pbt{i}", [128, 1024], BF16)) for i in range(2)]

    idb = E(nc.sbuf_tensor("idb", [128, 128], BF16))
    idf = E(nc.sbuf_tensor("idf", [128, 128], F32))
    oseg = E(nc.sbuf_tensor("oseg", [128, 2], BF16))
    cv = E(nc.sbuf_tensor("cv", [128, 4], F32))
    ones1 = E(nc.sbuf_tensor("ones1", [1, 128], BF16))
    onescol = E(nc.sbuf_tensor("onescol", [128, 1], BF16))
    onesf = E(nc.sbuf_tensor("onesf", [1, 128], F32))
    onec1f = E(nc.sbuf_tensor("onec1f", [1, 1], F32))
    mw0f = E(nc.sbuf_tensor("mw0f", [128, 128], F32))
    mw1f = E(nc.sbuf_tensor("mw1f", [128, 128], F32))
    w0b = E(nc.sbuf_tensor("w0b", [128, 128], BF16))
    w1b = E(nc.sbuf_tensor("w1b", [128, 128], BF16))
    w1Tb = E(nc.sbuf_tensor("w1Tb", [128, 128], BF16))
    wkb = E(nc.sbuf_tensor("wkb", [128, 8 * 128], BF16))
    wvb = E(nc.sbuf_tensor("wvb", [128, 8 * 128], BF16))
    wqb = E(nc.sbuf_tensor("wqb", [128, 8 * 128], BF16))
    w4b = E(nc.sbuf_tensor("w4b", [128, 8 * 2], BF16))
    wmdb = E(nc.sbuf_tensor("wmdb", [128, 8 * 2], BF16))
    gp1row = E(nc.sbuf_tensor("gp1row", [1, 128], F32))
    momb = E(nc.sbuf_tensor("momb", [128, 256], F32))
    decb = E(nc.sbuf_tensor("decb", [128, 256], F32))
    M0 = E(nc.sbuf_tensor("M0", [128, 128], F32))
    M1 = E(nc.sbuf_tensor("M1", [128, 128], F32))
    U0 = E(nc.sbuf_tensor("U0", [128, 128], F32))
    U1 = E(nc.sbuf_tensor("U1", [128, 128], F32))
    xnT = E(nc.sbuf_tensor("xnT", [128, 8 * WTOK], BF16))
    gacc = E(nc.sbuf_tensor("gacc", [128, 8 * WTOK], F32))
    widxS = E(nc.sbuf_tensor("widxS", [128, 16], U32))
    bscaleS = E(nc.sbuf_tensor("bscaleS", [128, 1], F32))
    zrow = E(nc.sbuf_tensor("zrow", [128, 64], BF16))

    with nc.Block() as blk:

        def fence():
            base = FN.n
            FN.n = base + 5
            tgt = FN.n

            @blk.sync
            def _(sync):
                for s in [LD, SO, SK, LX[0], LX[1], LW[0], LW[1], KV[0], KV[1]]:
                    W(sync, "sync", s, s.n)
                sync.sem_inc(FN.h, 1)
                sync.wait_ge(FN.h, tgt)

            @blk.gpsimd
            def _(gpsimd):
                for s in [LG, SC2, SPK, SG3, SG4, SV5, RW[0], RW[1], SA[0], SA[1]]:
                    W(gpsimd, "gpsimd", s, s.n)
                gpsimd.sem_inc(FN.h, 1)
                gpsimd.wait_ge(FN.h, tgt)

            @blk.tensor
            def _(tensor):
                tensor.wait_ge(TE.h, TE.n)
                tensor.sem_inc(FN.h, 1)
                tensor.wait_ge(FN.h, tgt)

            @blk.vector
            def _(vector):
                vector.wait_ge(VE.h, VE.n)
                vector.sem_inc(FN.h, 1)
                vector.wait_ge(FN.h, tgt)

            @blk.scalar
            def _(scalar):
                scalar.wait_ge(SE.h, SE.n)
                scalar.sem_inc(FN.h, 1)
                scalar.wait_ge(FN.h, tgt)

        p0 = contextlib.ExitStack()
        E0 = p0.enter_context
        wkst = E0(nc.sbuf_tensor("wkst", [128, 8 * 128], F32))
        wvst = E0(nc.sbuf_tensor("wvst", [128, 8 * 128], F32))
        wqst = E0(nc.sbuf_tensor("wqst", [128, 8 * 128], F32))
        w4st = E0(nc.sbuf_tensor("w4st", [128, 8 * 2], F32))
        wmdst = E0(nc.sbuf_tensor("wmdst", [128, 8 * 2], F32))
        sgainS = E0(nc.sbuf_tensor("sgainS", [128, 8], F32))
        rgainS = E0(nc.sbuf_tensor("rgainS", [128, 8], F32))
        grow0 = E0(nc.sbuf_tensor("grow0", [1, 128], F32))
        zf32 = E0(nc.sbuf_tensor("zf32", [128, 128], F32))

        @blk.sync
        def _(sync):
            for (dst, src) in [(idb, idb_in), (idf, idf_in), (oseg, oseg_in),
                               (cv, cv_in), (mw0f, mw0_in), (mw1f, mw1_in),
                               (widxS, widx_in), (bscaleS, bscale_in),
                               (grow0, gamma_in), (zf32, zf_in)]:
                LD.dma(sync.dma_start(out=dst[:], in_=src[:, :]))
            for k in range(8):
                LD.dma(sync.dma_start(out=wkst[:, 128 * k:128 * k + 128], in_=wk_h[128 * k:128 * k + 128, :]))
                LD.dma(sync.dma_start(out=wvst[:, 128 * k:128 * k + 128], in_=wv_h[128 * k:128 * k + 128, :]))
                LD.dma(sync.dma_start(out=wqst[:, 128 * k:128 * k + 128], in_=wq_h[128 * k:128 * k + 128, :]))
                LD.dma(sync.dma_start(out=w4st[:, 2 * k:2 * k + 2], in_=w4_h[128 * k:128 * k + 128, :]))
                LD.dma(sync.dma_start(out=wmdst[:, 2 * k:2 * k + 2], in_=wmd_h[128 * k:128 * k + 128, :]))
                LD.dma(sync.dma_start(out=sgainS[:, k:k + 1], in_=sgain_in[k:k + 1, :]))
                LD.dma(sync.dma_start(out=rgainS[:, k:k + 1], in_=rgain_in[k:k + 1, :]))

        @blk.vector
        def _(vector):
            W(vector, "vector", LD, LD.n)
            vector.memset(ones1[:], 1.0)
            vector.memset(onescol[:], 1.0)
            vector.memset(onesf[:], 1.0)
            vector.memset(onec1f[:], 1.0)
            vector.memset(M0[:], 0.0)
            vector.memset(M1[:], 0.0)
            vector.memset(U0[:], 0.0)
            vector.memset(U1[:], 0.0)
            vector.memset(zrow[:], 0.0)
            vector.memset(gacc[:], 0.0)
            for k in range(8):
                s = slice(128 * k, 128 * k + 128)
                vector.tensor_scalar(out=wkb[:, s], in0=wkst[:, s], scalar1=sgainS[:, k:k + 1],
                                     scalar2=None, op0=ALU.mult)
                vector.tensor_scalar(out=wvb[:, s], in0=wvst[:, s], scalar1=sgainS[:, k:k + 1],
                                     scalar2=None, op0=ALU.mult)
                vector.tensor_scalar(out=wqb[:, s], in0=wqst[:, s], scalar1=rgainS[:, k:k + 1],
                                     scalar2=None, op0=ALU.mult)
                vector.tensor_scalar(out=w4b[:, 2 * k:2 * k + 1], in0=w4st[:, 2 * k:2 * k + 1],
                                     scalar1=sgainS[:, k:k + 1], scalar2=None, op0=ALU.mult)
                vector.tensor_scalar(out=w4b[:, 2 * k + 1:2 * k + 2], in0=w4st[:, 2 * k + 1:2 * k + 2],
                                     scalar1=rgainS[:, k:k + 1], scalar2=None, op0=ALU.mult)
                vector.tensor_scalar(out=wmdb[:, 2 * k:2 * k + 2], in0=wmdst[:, 2 * k:2 * k + 2],
                                     scalar1=sgainS[:, k:k + 1], scalar2=None, op0=ALU.mult)
            vector.tensor_scalar(out=gp1row[:], in0=grow0[:], scalar1=1.0, scalar2=None, op0=ALU.add)
            vector.tensor_copy(w0b[:], mw0f[:])
            VE.inc(vector.tensor_copy(w1b[:], mw1f[:]))

        @blk.tensor
        def _(tensor):
            W(tensor, "tensor", VE, VE.n)
            TE.inc(tensor.transpose(pbt[0][:, 0:128], w1b[:], idb[:]))

        @blk.scalar
        def _(scalar):
            W(scalar, "scalar", TE, TE.n)
            SE.inc(scalar.copy(out=w1Tb[:], in_=pbt[0][:, 0:128]))

        @blk.gpsimd
        def _(gpsimd):
            W(gpsimd, "gpsimd", VE, VE.n)
            LG.dma(gpsimd.dma_start(out=rTd[:, NTOK:NTOK + 64], in_=zrow[:]))
            W(gpsimd, "gpsimd", LD, LD.n)
            LG.dma(gpsimd.dma_start(out=gswd[:, :], in_=zf32[:]))

        fence()
        p0.close()

        p1 = contextlib.ExitStack()
        E1 = p1.enter_context
        xst = E1(nc.sbuf_tensor("xst", [128, 2 * 1024], F32))
        xsq = E1(nc.sbuf_tensor("xsq", [128, 2 * 1024], F32))
        ytok = E1(nc.sbuf_tensor("ytok", [128, 2 * 1024], BF16))
        yTst = E1(nc.sbuf_tensor("yTst", [128, 8192], BF16))
        stats = E1(nc.sbuf_tensor("stats", [128, 2 * 16], F32))
        cm2 = E1(nc.sbuf_tensor("cm2", [2, 2 * 1024], F32))
        cmtok = E1(nc.sbuf_tensor("cmtok", [128, 2 * 1024], F32))
        kvr = E1(nc.sbuf_tensor("kvr", [128, 2 * 3 * 512], BF16))
        sgm = E1(nc.sbuf_tensor("sgm", [2, 2 * 512], F32))
        lrrow = E1(nc.sbuf_tensor("lrrow", [1, 2 * 512], F32))

        se_y = {}
        se_sq1 = {}
        ve_st = {}
        ve_ev = {}
        ve_cm = {}
        te_tc = {}
        te_pj = {}
        se_ev = {}
        ve_lr = {}
        ld_x = {}
        ld_kvr = {}
        lg_cm = {}
        lg_pk = {}


        for g in range(128):
            t, i = divmod(g, 4)
            par = g % 2
            so = 16 * par
            xv = xst[:, 1024 * par:1024 * par + 1024]
            xqv = xsq[:, 1024 * par:1024 * par + 1024]
            yv = ytok[:, 1024 * par:1024 * par + 1024]

            @blk.sync
            def _(sync, g=g, par=par, xv=xv):
                W(sync, "sync", VE, se_y.get(g - 2, 0))
                LX[par].dma(sync.dma_start(out=xv, in_=x_full[128 * g:128 * g + 128, :]))
                ld_x[g] = LX[par].n

            @blk.vector
            def _(vector, g=g, par=par, xv=xv, xqv=xqv, so=so):
                W(vector, "vector", LX[par], ld_x[g])
                W(vector, "vector", SE, se_sq1.get(g - 2, 0))
                vector.reduce_sum(stats[:, so + 0:so + 1], xv, axis=AX.X)
                vector.tensor_tensor(out=xqv, in0=xv, in1=xv, op=ALU.mult)
                vector.drain()
                vector.reduce_sum(stats[:, so + 1:so + 2], xqv, axis=AX.X)
                vector.tensor_scalar(out=stats[:, so + 2:so + 3], in0=stats[:, so + 0:so + 1],
                                     scalar1=1.0 / D, scalar2=None, op0=ALU.mult)
                vector.drain()
                vector.tensor_scalar(out=stats[:, so + 3:so + 4], in0=stats[:, so + 1:so + 2],
                                     scalar1=1.0 / D, scalar2=None, op0=ALU.mult)
                vector.drain()
                VE.inc(vector.scalar_tensor_tensor(out=stats[:, so + 4:so + 5], in0=stats[:, so + 2:so + 3],
                                                   scalar=stats[:, so + 2:so + 3], in1=stats[:, so + 3:so + 4],
                                                   op0=ALU.mult, op1=ALU.subtract))
                ve_st[(g, "var")] = VE.n

            @blk.scalar
            def _(scalar, g=g, so=so):
                W(scalar, "scalar", VE, ve_st[(g, "var")])
                SE.inc(scalar.activation(out=stats[:, so + 5:so + 6], in_=stats[:, so + 4:so + 5],
                                         func=AF.Sqrt, scale=-(1.0 + EPS_RMS), bias=cv[:, 2:3]))
                se_sq1[g] = SE.n

            @blk.vector
            def _(vector, g=g, so=so):
                W(vector, "vector", SE, se_sq1[g])
                vector.reciprocal(stats[:, so + 6:so + 7], stats[:, so + 5:so + 6])
                vector.drain()
                VE.inc(vector.scalar_tensor_tensor(out=stats[:, so + 7:so + 8], in0=stats[:, so + 2:so + 3],
                                                   scalar=-1.0, in1=stats[:, so + 6:so + 7],
                                                   op0=ALU.mult, op1=ALU.mult))
                ve_st[g] = VE.n

            @blk.vector
            def _(vector, g=g, xv=xv, yv=yv, so=so):
                W(vector, "vector", TE, te_tc.get(g - 2, 0))
                vector.drain()
                VE.inc(vector.tensor_scalar(out=yv, in0=xv, scalar1=stats[:, so + 6:so + 7],
                                            scalar2=stats[:, so + 7:so + 8], op0=ALU.mult, op1=ALU.add))
                se_y[g] = VE.n

            @blk.tensor
            def _(tensor, g=g, yv=yv, par=par):
                W(tensor, "tensor", VE, se_y[g])
                W(tensor, "tensor", VE, ve_ev.get(g - 2, 0))
                W(tensor, "tensor", VE, ve_cm.get(g - 2, 0))
                for k in range(7):
                    tensor.transpose(pbt[par][:, 128 * k:128 * k + 128], yv[:, 128 * k:128 * k + 128], idb[:])
                TE.inc(tensor.transpose(pbt[par][:, 896:1024], yv[:, 896:1024], idb[:]))
                tensor.matmul(pb[4][0:2, 0:512], oseg[:], yv[:, 0:512], start=True, stop=True)
                TE.inc(tensor.matmul(pb[5][0:2, 0:512], oseg[:], yv[:, 512:1024], start=True, stop=True))
                te_tc[g] = TE.n

            @blk.vector
            def _(vector, g=g, par=par, t=t, i=i):
                W(vector, "vector", TE, te_tc[g])
                buf = t % 2
                if t >= 2 and i == 0:
                    W(vector, "vector", TE, te_pj.get(t - 2, 0))
                dstv = yTst[:].rearrange("p (b k u c) -> p b k u c", b=2, k=8, u=4, c=128)[:, buf, :, i, :]
                srcv = pbt[par][:].rearrange("p (k c) -> p k c", k=8, c=128)
                vector.tensor_copy(dstv, srcv)
                W(vector, "vector", SC2, lg_cm.get(g - 2, 0))
                vector.tensor_copy(cm2[:, 1024 * par:1024 * par + 512], pb[4][0:2, 0:512])
                VE.inc(vector.tensor_copy(cm2[:, 1024 * par + 512:1024 * par + 1024], pb[5][0:2, 0:512]))
                ve_cm[g] = VE.n
                ve_ev[g] = VE.n

            @blk.gpsimd
            def _(gpsimd, g=g, par=par):
                W(gpsimd, "gpsimd", SC2, lg_cm.get(g - 1, 0))
                W(gpsimd, "gpsimd", VE, ve_cm[g])
                tix = g // 64
                row = (2 * g) % 128
                SC2.dma(gpsimd.dma_start(out=cmtok[row:row + 2, 1024 * tix:1024 * tix + 1024],
                                        in_=cm2[:, 1024 * par:1024 * par + 1024]))
                lg_cm[g] = SC2.n

            if i == 3:
                buf = t % 2
                pso = 512 * 0

                @blk.tensor
                def _(tensor, t=t, buf=buf, g=g):
                    W(tensor, "tensor", VE, ve_ev[g])
                    W(tensor, "tensor", SE, se_ev.get(t - 2, 0))
                    yview = yTst[:].rearrange("p (b k u) -> p b k u", b=2, k=8, u=512)
                    for k in range(8):
                        tensor.matmul(pb[0][:, 0:512], wkb[:, 128 * k:128 * k + 128],
                                      yview[:, buf, k, :], start=(k == 0), stop=(k == 7))
                    for k in range(8):
                        tensor.matmul(pb[1][:, 0:512], wvb[:, 128 * k:128 * k + 128],
                                      yview[:, buf, k, :], start=(k == 0), stop=(k == 7))
                    for k in range(8):
                        tensor.matmul(pb[2][:, 0:512], wqb[:, 128 * k:128 * k + 128],
                                      yview[:, buf, k, :], start=(k == 0), stop=(k == 7))
                    for k in range(7):
                        tensor.matmul(pb[3][0:2, 0:512], w4b[:, 2 * k:2 * k + 2],
                                      yview[:, buf, k, :], start=(k == 0), stop=False)
                    TE.inc(tensor.matmul(pb[3][0:2, 0:512], w4b[:, 14:16],
                                         yview[:, buf, 7, :], start=False, stop=True))
                    te_pj[t] = TE.n

                @blk.scalar
                def _(scalar, t=t, buf=buf):
                    W(scalar, "scalar", TE, te_pj[t])
                    W(scalar, "scalar", SK, ld_kvr.get(t - 2, 0))
                    W(scalar, "scalar", VE, ve_lr.get(t - 2, 0))
                    o = 3 * 512 * buf
                    scalar.copy(out=kvr[:, o:o + 512], in_=pb[0][:, 0:512])
                    scalar.copy(out=kvr[:, o + 512:o + 1024], in_=pb[1][:, 0:512])
                    scalar.copy(out=kvr[:, o + 1024:o + 1536], in_=pb[2][:, 0:512])
                    SE.inc(scalar.activation(out=sgm[:, 512 * buf:512 * buf + 512],
                                             in_=pb[3][0:2, 0:512], func=AF.Sigmoid))
                    se_ev[t] = SE.n

                @blk.vector
                def _(vector, t=t, buf=buf):
                    W(vector, "vector", SE, se_ev[t])
                    W(vector, "vector", SPK, lg_pk.get(t - 2, 0))
                    VE.inc(vector.tensor_scalar(out=lrrow[:, 512 * buf:512 * buf + 512],
                                                in0=sgm[0:1, 512 * buf:512 * buf + 512],
                                                scalar1=LR_EFF, scalar2=None, op0=ALU.mult))
                    ve_lr[t] = VE.n

                @blk.sync
                def _(sync, t=t, buf=buf):
                    W(sync, "sync", SK, ld_kvr.get(t - 1, 0))
                    W(sync, "sync", SE, se_ev[t])
                    o = 3 * 512 * buf
                    SK.dma(sync.dma_start(out=kTd[:, 512 * t:512 * t + 512], in_=kvr[:, o:o + 512]))
                    SK.dma(sync.dma_start(out=vTd[:, 512 * t:512 * t + 512], in_=kvr[:, o + 512:o + 1024]))
                    SK.dma(sync.dma_start(out=rTd[:, 512 * t:512 * t + 512], in_=kvr[:, o + 1024:o + 1536]))
                    ld_kvr[t] = SK.n

                @blk.gpsimd
                def _(gpsimd, t=t, buf=buf):
                    W(gpsimd, "gpsimd", SPK, lg_pk.get(t - 1, 0))
                    W(gpsimd, "gpsimd", VE, ve_lr[t])
                    SPK.dma(gpsimd.dma_start(out=lrd[0:1, 512 * t:512 * t + 512],
                                            in_=lrrow[0:1, 512 * buf:512 * buf + 512]))
                    gv = sgm[1:2, 512 * buf:512 * buf + 512]
                    if t > 0:
                        SPK.dma(gpsimd.dma_start(out=gswd[4 * t - 1:4 * t, 65:128], in_=gv[:, 0:63]))
                    SPK.dma(gpsimd.dma_start(out=gswd[4 * t:4 * t + 3, :], in_=gv[:, 63:447]))
                    SPK.dma(gpsimd.dma_start(out=gswd[4 * t + 3:4 * t + 4, 0:65], in_=gv[:, 447:512]))
                    lg_pk[t] = SPK.n

        fence()

        if "cm_d" in DEBUG:
            @blk.gpsimd
            def _(gpsimd):
                LG.dma(gpsimd.dma_start(out=dbg["cm_d"][0], in_=cmtok[:, 0:1024]))
                LG.dma(gpsimd.dma_start(out=dbg["cm_d"][1], in_=cmtok[:, 1024:2048]))
                gpsimd.wait_ge(LG.h, LG.n)

        cmbf = E1(nc.sbuf_tensor("cmbf", [128, 1024], BF16))
        cmT = E1(nc.sbuf_tensor("cmT", [128, 8 * 256], BF16))
        mrow = E1(nc.sbuf_tensor("mrow", [1, 256], F32))
        drow = E1(nc.sbuf_tensor("drow", [1, 256], F32))

        for half in range(2):
            @blk.scalar
            def _(scalar, half=half):
                if half == 1:
                    W(scalar, "scalar", VE, VE.n)
                SE.inc(scalar.copy(out=cmbf[:], in_=cmtok[:, 1024 * half:1024 * half + 1024]))

            @blk.tensor
            def _(tensor, half=half):
                W(tensor, "tensor", SE, SE.n)
                if half == 1:
                    W(tensor, "tensor", VE, VE.n)
                for k in range(7):
                    tensor.transpose(pbt[half][:, 128 * k:128 * k + 128],
                                     cmbf[:, 128 * k:128 * k + 128], idb[:])
                TE.inc(tensor.transpose(pbt[half][:, 896:1024], cmbf[:, 896:1024], idb[:]))

            @blk.vector
            def _(vector, half=half):
                W(vector, "vector", TE, TE.n)
                dstv = cmT[:].rearrange("p (k c) -> p k c", k=8, c=256)[:, :, 128 * half:128 * half + 128]
                VE.inc(vector.tensor_copy(dstv, pbt[half][:].rearrange("p (k c) -> p k c", k=8, c=128)))

        @blk.tensor
        def _(tensor):
            W(tensor, "tensor", VE, VE.n)
            for k in range(8):
                tensor.matmul(pb[0][0:1, 0:256], wmdb[:, 2 * k:2 * k + 1],
                              cmT[:, 256 * k:256 * k + 256], start=(k == 0), stop=(k == 7))
            for k in range(7):
                tensor.matmul(pb[3][0:1, 0:256], wmdb[:, 2 * k + 1:2 * k + 2],
                              cmT[:, 256 * k:256 * k + 256], start=(k == 0), stop=False)
            TE.inc(tensor.matmul(pb[3][0:1, 0:256], wmdb[:, 15:16],
                                 cmT[:, 1792:2048], start=False, stop=True))

        @blk.scalar
        def _(scalar):
            W(scalar, "scalar", TE, TE.n)
            scalar.activation(out=mrow[:], in_=pb[0][0:1, 0:256], func=AF.Sigmoid)
            SE.inc(scalar.activation(out=drow[:], in_=pb[3][0:1, 0:256], func=AF.Sigmoid))

        @blk.vector
        def _(vector):
            W(vector, "vector", SE, SE.n)
            VE.inc(vector.tensor_scalar(out=drow[:], in0=drow[:], scalar1=-1.0,
                                        scalar2=1.0, op0=ALU.mult, op1=ALU.add))

        @blk.tensor
        def _(tensor):
            W(tensor, "tensor", VE, VE.n)
            tensor.matmul(pb[1][:, 0:256], onesf[:], mrow[:], start=True, stop=True)
            TE.inc(tensor.matmul(pb[2][:, 0:256], onesf[:], drow[:], start=True, stop=True))

        @blk.vector
        def _(vector):
            W(vector, "vector", TE, TE.n)
            vector.tensor_copy(momb[:], pb[1][:, 0:256])
            VE.inc(vector.tensor_copy(decb[:], pb[2][:, 0:256]))

        fence()
        p1.close()

        if "momb_d" in DEBUG:
            @blk.gpsimd
            def _(gpsimd):
                LG.dma(gpsimd.dma_start(out=dbg["momb_d"][:, :], in_=momb[:]))
                LG.dma(gpsimd.dma_start(out=dbg["decb_d"][:, :], in_=decb[:]))
                gpsimd.wait_ge(LG.h, LG.n)

        p2 = contextlib.ExitStack()
        E2 = p2.enter_context
        kp = E2(nc.sbuf_tensor("kp", [128, 8 * 128], BF16))
        vp = E2(nc.sbuf_tensor("vp", [128, 8 * 128], BF16))
        rp = E2(nc.sbuf_tensor("rp", [128, 8 * 128], BF16))
        grow2 = E2(nc.sbuf_tensor("grow2", [1, 8 * 128], F32))
        lrow2 = E2(nc.sbuf_tensor("lrow2", [1, 8 * 128], F32))
        a1fS = E2(nc.sbuf_tensor("a1fS", [128, 2 * 128], BF16))
        a1tS = E2(nc.sbuf_tensor("a1tS", [128, 2 * 128], BF16))
        sigS = E2(nc.sbuf_tensor("sigS", [128, 2 * 128], BF16))
        dpfS = E2(nc.sbuf_tensor("dpfS", [128, 2 * 128], BF16))
        dptS = E2(nc.sbuf_tensor("dptS", [128, 2 * 128], BF16))
        dzS = E2(nc.sbuf_tensor("dzS", [128, 2 * 128], BF16))
        ktS = E2(nc.sbuf_tensor("ktS", [128, 2 * 128], BF16))
        vtS = E2(nc.sbuf_tensor("vtS", [128, 2 * 128], BF16))
        tmpa = E2(nc.sbuf_tensor("tmpa", [128, 2 * 128], F32))
        tmpb = E2(nc.sbuf_tensor("tmpb", [128, 2 * 128], F32))
        w0cS = E2(nc.sbuf_tensor("w0cS", [128, 2 * 128], BF16))
        w1cS = E2(nc.sbuf_tensor("w1cS", [128, 2 * 128], BF16))
        s1S = E2(nc.sbuf_tensor("s1S", [128, 2 * 128], BF16))
        sqS = E2(nc.sbuf_tensor("sqS", [128, 2 * 128], BF16))
        r0S = E2(nc.sbuf_tensor("r0S", [1, 2 * 128], F32))
        r2S = E2(nc.sbuf_tensor("r2S", [1, 2 * 128], F32))
        valsS = E2(nc.sbuf_tensor("valsS", [128, 2 * 128], BF16))

        ld_pr = {}
        lg_gt = {}
        se_a1 = {}
        se_srow = {}
        se_s1r = {}
        se_sq = {}
        ve_dp = {}
        ve_dpf = {}
        ve_dz = {}
        ve_r2 = {}
        ve_vals = {}
        ve_scanw = {}
        te_z = {}
        te_m = {}
        te_dpt = {}
        te_da = {}
        te_g = {}
        te_r = {}
        te_ssq = {}
        te_sc = {}
        lg_v = {}

        def emit_kv_group(G):
            lane = G % 2
            off = 512 * (G % 2)

            @blk.sync
            def _(sync, G=G, lane=lane, off=off):
                if G >= 2:
                    W(sync, "sync", TE, te_r[(4 * G - 5, 1, 0)])
                KV[lane].dma(sync.dma_start(out=kp[:, off:off + 512], in_=kTd[:, 512 * G:512 * G + 512]))
                KV[lane].dma(sync.dma_start(out=vp[:, off:off + 512], in_=vTd[:, 512 * G:512 * G + 512]))
                KV[lane].dma(sync.dma_start(out=rp[:, off:off + 512], in_=rTd[:, 512 * G + 63:512 * G + 63 + 512]))
                for pp in range(4 * G, min(4 * G + 4, NPAIR)):
                    ld_pr[pp] = (lane, KV[lane].n)

            @blk.gpsimd
            def _(gpsimd, G=G, lane=lane, off=off):
                if G >= 2:
                    W(gpsimd, "gpsimd", VE, ve_r2.get(4 * G - 5, 0))
                RW[lane].dma(gpsimd.dma_start(out=grow2[0:1, off:off + 512], in_=gswd[4 * G:4 * G + 4, :]))
                RW[lane].dma(gpsimd.dma_start(out=lrow2[0:1, off:off + 512], in_=lrd[0:1, 512 * G:512 * G + 512]))
                for pp in range(4 * G, min(4 * G + 4, NPAIR)):
                    lg_gt[pp] = (lane, RW[lane].n)

        emit_kv_group(0)
        emit_kv_group(1)

        for p in range(NPAIR):
            par = p % 2
            o = 128 * par
            s_ = slice(o, o + 128)
            ko = 128 * (p % 8)
            ks_ = slice(ko, ko + 128)
            zb = pb[0] if par == 0 else pb[1]
            wb = pb[2] if par == 0 else pb[3]
            gb0, gb1 = pb[4], pb[5]

            if p % 4 == 0 and p >= 4 and (p // 4 + 1) * 4 < NPAIR + 4:
                G = p // 4 + 1
                if 4 * G < NPAIR:
                    emit_kv_group(G)

            @blk.tensor
            def _(tensor, p=p, ks_=ks_, zb=zb):
                lane, val = ld_pr[p]
                W(tensor, "tensor", KV[lane], val)
                W(tensor, "tensor", SE, se_srow.get(p - 2, 0))
                W(tensor, "tensor", VE, ve_vals.get(p - 2, 0))
                tensor.matmul(zb[:, 0:128], kp[:, ks_], w0b[:], start=True, stop=True)
                TE.inc(tensor.matmul(zb[:, 128:256], w0b[:], kp[:, ks_], start=True, stop=True))
                te_z[p] = TE.n

            @blk.scalar
            def _(scalar, p=p, s_=s_, zb=zb):
                W(scalar, "scalar", TE, te_z[p])
                scalar.activation(out=a1fS[:, s_], in_=zb[:, 128:256], func=AF.Silu)
                scalar.activation(out=a1tS[:, s_], in_=zb[:, 0:128], func=AF.Silu)
                SE.inc(scalar.activation(out=sigS[:, s_], in_=zb[:, 0:128], func=AF.Sigmoid))
                se_a1[p] = SE.n

            @blk.tensor
            def _(tensor, p=p, s_=s_, ks_=ks_, ko=ko, zb=zb, wb=wb, par=par):
                W(tensor, "tensor", SE, se_a1[p])
                W(tensor, "tensor", VE, ve_dpf.get(p - 2, 0))
                lane, val = lg_gt[p]
                W(tensor, "tensor", RW[lane], val)
                W(tensor, "tensor", VE, ve_vals.get(p - 2, 0))
                tensor.matmul(zb[:, 384:512], a1fS[:, s_], w1b[:], start=True, stop=True)
                tensor.matmul(wb[:, 128:129], lrow2[0:1, ko:ko + 128], onec1f[:], start=True, stop=True)
                tensor.transpose(pbt[par][:, 0:128], kp[:, ks_], idb[:])
                TE.inc(tensor.transpose(pbt[par][:, 128:256], vp[:, ks_], idb[:]))
                te_m[p] = TE.n

            @blk.vector
            def _(vector, p=p, s_=s_, zb=zb, wb=wb, par=par):
                W(vector, "vector", TE, te_m[p])
                vector.tensor_copy(ktS[:, s_], pbt[par][:, 0:128])
                vector.tensor_copy(vtS[:, s_], pbt[par][:, 128:256])
                vector.drain()
                vector.tensor_tensor(out=tmpb[:, s_], in0=zb[:, 384:512], in1=vtS[:, s_], op=ALU.subtract)
                vector.drain()
                VE.inc(vector.tensor_scalar(out=dptS[:, s_], in0=tmpb[:, s_], scalar1=wb[:, 128:129],
                                            scalar2=None, op0=ALU.mult))
                ve_dp[p] = VE.n

            @blk.tensor
            def _(tensor, p=p, s_=s_, par=par):
                W(tensor, "tensor", VE, ve_dp[p])
                TE.inc(tensor.transpose(pbt[par][:, 256:384], dptS[:, s_], idb[:]))
                te_dpt[p] = TE.n

            @blk.vector
            def _(vector, p=p, s_=s_, par=par):
                W(vector, "vector", TE, te_dpt[p])
                VE.inc(vector.tensor_copy(dpfS[:, s_], pbt[par][:, 256:384]))
                ve_dpf[p] = VE.n

            @blk.tensor
            def _(tensor, p=p, s_=s_, wb=wb):
                W(tensor, "tensor", VE, ve_dpf[p])
                W(tensor, "tensor", VE, ve_vals.get(p - 2, 0))
                TE.inc(tensor.matmul(wb[:, 256:384], dpfS[:, s_], w1Tb[:], start=True, stop=True))
                te_da[p] = TE.n

            @blk.vector
            def _(vector, p=p, s_=s_, wb=wb):
                W(vector, "vector", TE, te_da[p])
                vector.scalar_tensor_tensor(out=tmpa[:, s_], in0=a1tS[:, s_], scalar=-1.0,
                                            in1=sigS[:, s_], op0=ALU.mult, op1=ALU.mult)
                vector.tensor_tensor(out=tmpb[:, s_], in0=a1tS[:, s_], in1=sigS[:, s_], op=ALU.add)
                vector.drain()
                vector.tensor_tensor(out=tmpa[:, s_], in0=tmpa[:, s_], in1=tmpb[:, s_], op=ALU.add)
                vector.drain()
                VE.inc(vector.tensor_tensor(out=dzS[:, s_], in0=wb[:, 256:384], in1=tmpa[:, s_], op=ALU.mult))
                ve_dz[p] = VE.n

            @blk.tensor
            def _(tensor, p=p, s_=s_):
                W(tensor, "tensor", VE, ve_dz[p])
                W(tensor, "tensor", VE, ve_vals.get(p - 1, 0))
                tensor.matmul(gb0[:, 0:128], a1tS[0:64, s_], dptS[0:64, s_], start=True, stop=True)
                tensor.matmul(gb0[:, 128:256], ktS[0:64, s_], dzS[0:64, s_], start=True, stop=True)
                tensor.matmul(gb1[:, 0:128], a1tS[64:128, s_], dptS[64:128, s_], start=True, stop=True)
                TE.inc(tensor.matmul(gb1[:, 128:256], ktS[64:128, s_], dzS[64:128, s_], start=True, stop=True))
                te_g[p] = TE.n

            for ci in range(2):
                c = 2 * p + ci
                gbank = gb0 if ci == 0 else gb1

                @blk.vector
                def _(vector, p=p, c=c, ci=ci, gbank=gbank):
                    if ci == 0:
                        W(vector, "vector", TE, te_g[p])
                    vector.drain()
                    vector.scalar_tensor_tensor(out=M1[:], in0=M1[:], scalar=momb[:, c:c + 1],
                                                in1=gbank[:, 0:128], op0=ALU.mult, op1=ALU.subtract)
                    vector.scalar_tensor_tensor(out=M0[:], in0=M0[:], scalar=momb[:, c:c + 1],
                                                in1=gbank[:, 128:256], op0=ALU.mult, op1=ALU.subtract)
                    vector.drain()
                    vector.scalar_tensor_tensor(out=U1[:], in0=U1[:], scalar=decb[:, c:c + 1],
                                                in1=M1[:], op0=ALU.mult, op1=ALU.add)
                    vector.scalar_tensor_tensor(out=U0[:], in0=U0[:], scalar=decb[:, c:c + 1],
                                                in1=M0[:], op0=ALU.mult, op1=ALU.add)
                    vector.drain()
                    vector.tensor_tensor(out=w0cS[:, 128 * ci:128 * ci + 128], in0=U0[:], in1=mw0f[:], op=ALU.add)
                    VE.inc(vector.tensor_tensor(out=w1cS[:, 128 * ci:128 * ci + 128], in0=U1[:], in1=mw1f[:], op=ALU.add))
                    ve_scanw[(p, ci)] = VE.n

                @blk.tensor
                def _(tensor, p=p, ci=ci, ko=ko, wb=wb):
                    W(tensor, "tensor", VE, ve_scanw[(p, ci)])
                    TE.inc(tensor.matmul(wb[:, 384 + 64 * ci:384 + 64 * ci + 64],
                                         w0cS[:, 128 * ci:128 * ci + 128], rp[:, ko + 64 * ci:ko + 64 * ci + 64],
                                         start=True, stop=True))
                    te_r[(p, ci, 0)] = TE.n

            @blk.scalar
            def _(scalar, p=p, o=o, wb=wb):
                for ci in range(2):
                    W(scalar, "scalar", TE, te_r[(p, ci, 0)])
                    SE.inc(scalar.activation(out=s1S[:, o + 64 * ci:o + 64 * ci + 64],
                                             in_=wb[:, 384 + 64 * ci:384 + 64 * ci + 64], func=AF.Silu))
                    se_s1r[(p, ci)] = SE.n

            for ci in range(2):
                gbank = gb0 if ci == 0 else gb1

                @blk.tensor
                def _(tensor, p=p, ci=ci, o=o, gbank=gbank):
                    W(tensor, "tensor", SE, se_s1r[(p, ci)])
                    TE.inc(tensor.matmul(gbank[:, 256 + 64 * ci:256 + 64 * ci + 64],
                                         w1cS[:, 128 * ci:128 * ci + 128],
                                         s1S[:, o + 64 * ci:o + 64 * ci + 64], start=True, stop=True))
                    te_r[(p, ci, 1)] = TE.n

            @blk.vector
            def _(vector, p=p, s_=s_, o=o):
                W(vector, "vector", TE, te_r[(p, 1, 1)])
                vector.tensor_copy(tmpb[:, o:o + 64], gb0[:, 256:320])
                vector.tensor_copy(tmpb[:, o + 64:o + 128], gb1[:, 320:384])
                vector.drain()
                VE.inc(vector.tensor_tensor(out=s1S[:, s_], in0=tmpb[:, s_],
                                            in1=tmpb[:, s_], op=ALU.mult))
                se_sq[p] = VE.n

            @blk.tensor
            def _(tensor, p=p, s_=s_, zb=zb):
                W(tensor, "tensor", VE, se_sq[p])
                TE.inc(tensor.matmul(zb[0:1, 0:128], onescol[:], s1S[:, s_], start=True, stop=True))
                te_ssq[p] = TE.n

            @blk.scalar
            def _(scalar, p=p, par=par, zb=zb):
                W(scalar, "scalar", TE, te_ssq[p])
                SE.inc(scalar.activation(out=r0S[0:1, 128 * par:128 * par + 128], in_=zb[0:1, 0:128],
                                         func=AF.Sqrt, scale=1.0 / DH, bias=cv[0:1, 1:2]))
                se_srow[p] = SE.n

            @blk.vector
            def _(vector, p=p, par=par, ko=ko):
                W(vector, "vector", SE, se_srow[p])
                lane, val = lg_gt[p]
                W(vector, "vector", RW[lane], val)
                vector.reciprocal(r2S[0:1, 128 * par:128 * par + 128], r0S[0:1, 128 * par:128 * par + 128])
                vector.drain()
                VE.inc(vector.tensor_tensor(out=r2S[0:1, 128 * par:128 * par + 128],
                                            in0=r2S[0:1, 128 * par:128 * par + 128],
                                            in1=grow2[0:1, ko:ko + 128], op=ALU.mult))
                ve_r2[p] = VE.n

            @blk.tensor
            def _(tensor, p=p, par=par, wb=wb):
                W(tensor, "tensor", VE, ve_r2[p])
                TE.inc(tensor.matmul(wb[:, 384:512], gp1row[:], r2S[0:1, 128 * par:128 * par + 128],
                                     start=True, stop=True))
                te_sc[p] = TE.n

            @blk.vector
            def _(vector, p=p, s_=s_, o=o, wb=wb):
                W(vector, "vector", TE, te_sc[p])
                if p >= 2:
                    W(vector, "vector", SA[p % 2], lg_v.get(p - 2, 0))
                VE.inc(vector.tensor_tensor(out=valsS[:, s_], in0=tmpb[:, s_], in1=wb[:, 384:512], op=ALU.mult))
                ve_vals[p] = VE.n

            @blk.gpsimd
            def _(gpsimd, p=p, s_=s_):
                W(gpsimd, "gpsimd", VE, ve_vals[p])
                SA[p % 2].dma(gpsimd.dma_start(out=agin[:, 128 * p:128 * p + 128], in_=valsS[:, s_]))
                lg_v[p] = SA[p % 2].n

        @blk.gpsimd
        def _(gpsimd):
            W(gpsimd, "gpsimd", SA[0], SA[0].n)
            W(gpsimd, "gpsimd", SA[1], SA[1].n)
            W(gpsimd, "gpsimd", LG, LG.n)
            CC.inc(gpsimd.collective_compute(
                "AllGather", ALU.bypass,
                replica_groups=[list(range(8))],
                ins=[agin_t.ap()],
                outs=[agout_t.ap()],
            ))

        if "agin_d" in DEBUG:
            @blk.gpsimd
            def _(gpsimd):
                LG.dma(gpsimd.dma_start(out=dbg["agin_d"][:, :], in_=agin[:, :]))
                gpsimd.wait_ge(LG.h, LG.n)

        fence()
        p2.close()

        p3 = contextlib.ExitStack()
        E3 = p3.enter_context
        xst3 = E3(nc.sbuf_tensor("xst3", [128, 2 * 1024], F32))
        xsq3 = E3(nc.sbuf_tensor("xsq3", [128, 2 * 1024], F32))
        xntok = E3(nc.sbuf_tensor("xntok", [128, 2 * 1024], BF16))
        st3 = E3(nc.sbuf_tensor("st3", [128, 2 * 16], F32))
        mx8 = E3(nc.sbuf_tensor("mx8", [128, 8], F32))
        gtmp = E3(nc.sbuf_tensor("gtmp", [128, 32], F32))
        gatesS = E3(nc.sbuf_tensor("gatesS", [128, 8], F32))
        gst3 = E3(nc.sbuf_tensor("gst3", [8, 2 * 128], BF16))
        wg32 = E3(nc.sbuf_tensor("wg32", [128, 64], F32))
        xnT32 = E3(nc.sbuf_tensor("xnT32", [128, 2 * 1024], F32))
        xn32 = E3(nc.sbuf_tensor("xn32", [128, 2 * 1024], F32))

        ld3 = {}
        se3 = {}
        ve3 = {}
        te3 = {}

        @blk.sync
        def _(sync):
            sync.wait_ge(LD.h, LD.n)
            for k in range(8):
                LD.dma(sync.dma_start(out=wg32[:, 8 * k:8 * k + 8], in_=wg_in[128 * k:128 * k + 128, :]))
            ld3["wg"] = LD.n

        for tc in range(16):
            par = tc % 2
            so = 16 * par
            xv = xst3[:, 1024 * par:1024 * par + 1024]
            xqv = xsq3[:, 1024 * par:1024 * par + 1024]
            yv = xntok[:, 1024 * par:1024 * par + 1024]

            @blk.sync
            def _(sync, tc=tc, par=par, xv=xv):
                W(sync, "sync", VE, se3.get((tc - 2, "y"), 0))
                LX[par].dma(sync.dma_start(out=xv, in_=x_slice[128 * tc:128 * tc + 128, :]))
                ld3[tc] = LX[par].n

            @blk.vector
            def _(vector, tc=tc, par=par, xv=xv, xqv=xqv, so=so):
                W(vector, "vector", LX[par], ld3[tc])
                W(vector, "vector", SE, se3.get((tc - 2, "sqrt"), 0))
                vector.reduce_sum(st3[:, so + 0:so + 1], xv, axis=AX.X)
                vector.tensor_tensor(out=xqv, in0=xv, in1=xv, op=ALU.mult)
                vector.drain()
                vector.reduce_sum(st3[:, so + 1:so + 2], xqv, axis=AX.X)
                vector.tensor_scalar(out=st3[:, so + 2:so + 3], in0=st3[:, so + 0:so + 1],
                                     scalar1=1.0 / D, scalar2=None, op0=ALU.mult)
                vector.drain()
                vector.tensor_scalar(out=st3[:, so + 3:so + 4], in0=st3[:, so + 1:so + 2],
                                     scalar1=1.0 / D, scalar2=None, op0=ALU.mult)
                vector.drain()
                VE.inc(vector.scalar_tensor_tensor(out=st3[:, so + 4:so + 5], in0=st3[:, so + 2:so + 3],
                                                   scalar=st3[:, so + 2:so + 3], in1=st3[:, so + 3:so + 4],
                                                   op0=ALU.mult, op1=ALU.subtract))
                ve3[(tc, "var")] = VE.n

            @blk.scalar
            def _(scalar, tc=tc, so=so):
                W(scalar, "scalar", VE, ve3[(tc, "var")])
                SE.inc(scalar.activation(out=st3[:, so + 5:so + 6], in_=st3[:, so + 4:so + 5],
                                         func=AF.Sqrt, scale=-1.0, bias=cv[:, 0:1]))
                se3[(tc, "sqrt")] = SE.n

            @blk.vector
            def _(vector, tc=tc, so=so):
                W(vector, "vector", SE, SE.n)
                vector.reciprocal(st3[:, so + 6:so + 7], st3[:, so + 5:so + 6])
                vector.drain()
                VE.inc(vector.scalar_tensor_tensor(out=st3[:, so + 7:so + 8], in0=st3[:, so + 2:so + 3],
                                                   scalar=-1.0, in1=st3[:, so + 6:so + 7],
                                                   op0=ALU.mult, op1=ALU.mult))
                ve3[(tc, "st")] = VE.n

            @blk.vector
            def _(vector, tc=tc, par=par, xv=xv, yv=yv, so=so):
                W(vector, "vector", TE, te3.get((tc - 2, "tr"), 0))
                vector.drain()
                vector.tensor_scalar(out=xn32[:, 1024 * par:1024 * par + 1024], in0=xv,
                                     scalar1=st3[:, so + 6:so + 7], scalar2=st3[:, so + 7:so + 8],
                                     op0=ALU.mult, op1=ALU.add)
                VE.inc(vector.tensor_scalar(out=yv, in0=xv, scalar1=st3[:, so + 6:so + 7],
                                            scalar2=st3[:, so + 7:so + 8], op0=ALU.mult, op1=ALU.add))
                se3[(tc, "y")] = VE.n

            for half in range(2):
                @blk.tensor
                def _(tensor, tc=tc, par=par, half=half):
                    if half == 0:
                        W(tensor, "tensor", VE, se3[(tc, "y")])
                    W(tensor, "tensor", VE, ve3.get((tc - 2, "ev", half), 0))
                    xsrc = xn32[:, 1024 * par:1024 * par + 1024]
                    hb5 = pb[2] if half == 0 else pb[3]
                    for k in range(3):
                        tensor.transpose(hb5[:, 128 * k:128 * k + 128],
                                         xsrc[:, 512 * half + 128 * k:512 * half + 128 * k + 128], idf[:])
                    TE.inc(tensor.transpose(hb5[:, 384:512], xsrc[:, 512 * half + 384:512 * half + 512], idf[:]))
                    te3[(tc, "tr", half)] = TE.n

                @blk.vector
                def _(vector, tc=tc, par=par, half=half):
                    W(vector, "vector", TE, te3[(tc, "tr", half)])
                    hb5 = pb[2] if half == 0 else pb[3]
                    dstv = xnT[:].rearrange("p (k w) -> p k w", k=8, w=WTOK)[:, 4 * half:4 * half + 4, 128 * tc:128 * tc + 128]
                    vector.tensor_copy(dstv, hb5[:].rearrange("p (k c) -> p k c", k=4, c=128))
                    dst32 = xnT32[:].rearrange("p (b k c) -> p b k c", b=2, k=4, c=128)[:, par, :, :] \
                        if False else xnT32[:, 1024 * par + 512 * half:1024 * par + 512 * half + 512]
                    VE.inc(vector.tensor_copy(dst32, hb5[:, 0:512]))
                    ve3[(tc, "ev", half)] = VE.n
            te3[(tc, "tr")] = te3[(tc, "tr", 1)]
            ve3[(tc, "ev")] = ve3[(tc, "ev", 1)]

            @blk.tensor
            def _(tensor, tc=tc, par=par):
                W(tensor, "tensor", LD, ld3["wg"])
                W(tensor, "tensor", VE, ve3[(tc, "ev")])
                W(tensor, "tensor", VE, ve3.get((tc - 1, "g"), 0))
                for k in range(7):
                    tensor.matmul(pb[0][:, 0:8], xnT32[:, 1024 * par + 128 * k:1024 * par + 128 * k + 128],
                                  wg32[:, 8 * k:8 * k + 8], start=(k == 0), stop=False)
                TE.inc(tensor.matmul(pb[0][:, 0:8], xnT32[:, 1024 * par + 896:1024 * par + 1024],
                                     wg32[:, 56:64], start=False, stop=True))
                te3[(tc, "lg")] = TE.n

            @blk.vector
            def _(vector, tc=tc):
                W(vector, "vector", TE, te3[(tc, "lg")])
                VE.inc(vector.max(mx8[:], pb[0][:, 0:8]))
                ve3[(tc, "mx")] = VE.n

            @blk.vector
            def _(vector, tc=tc):
                vector.drain()
                VE.inc(vector.tensor_tensor(out=gtmp[:, 10:11], in0=mx8[:, 0:1], in1=mx8[:, 1:2], op=ALU.subtract))
                ve3[(tc, "d")] = VE.n

            @blk.scalar
            def _(scalar, tc=tc):
                W(scalar, "scalar", VE, ve3[(tc, "d")])
                SE.inc(scalar.activation(out=gtmp[:, 11:12], in_=gtmp[:, 10:11], func=AF.Sigmoid))
                se3[(tc, "w1")] = SE.n

            @blk.vector
            def _(vector, tc=tc):
                W(vector, "vector", SE, se3[(tc, "w1")])
                vector.tensor_scalar(out=gtmp[:, 12:13], in0=gtmp[:, 11:12], scalar1=-1.0, scalar2=1.0,
                                     op0=ALU.mult, op1=ALU.add)
                vector.tensor_scalar(out=gtmp[:, 16:24], in0=pb[0][:, 0:8], scalar1=mx8[:, 0:1],
                                     scalar2=None, op0=ALU.is_equal)
                vector.tensor_scalar(out=gtmp[:, 24:32], in0=pb[0][:, 0:8], scalar1=mx8[:, 1:2],
                                     scalar2=None, op0=ALU.is_equal)
                vector.drain()
                vector.tensor_scalar(out=gtmp[:, 16:24], in0=gtmp[:, 16:24], scalar1=gtmp[:, 11:12],
                                     scalar2=None, op0=ALU.mult)
                vector.tensor_scalar(out=gtmp[:, 24:32], in0=gtmp[:, 24:32], scalar1=gtmp[:, 12:13],
                                     scalar2=None, op0=ALU.mult)
                vector.drain()
                VE.inc(vector.tensor_tensor(out=gatesS[:], in0=gtmp[:, 16:24], in1=gtmp[:, 24:32], op=ALU.add))
                ve3[(tc, "g")] = VE.n

            @blk.tensor
            def _(tensor, tc=tc):
                W(tensor, "tensor", VE, ve3[(tc, "g")])
                TE.inc(tensor.transpose(pb[1][0:8, 0:128], gatesS[:], idf[:]))
                te3[(tc, "gt")] = TE.n

            @blk.vector
            def _(vector, tc=tc, par=par):
                W(vector, "vector", TE, te3[(tc, "gt")])
                W(vector, "vector", SG3, ve3.get((tc - 2, "grdma"), 0))
                VE.inc(vector.tensor_copy(gst3[:, 128 * par:128 * par + 128], pb[1][0:8, 0:128]))
                ve3[(tc, "gr")] = VE.n

            @blk.gpsimd
            def _(gpsimd, tc=tc, par=par):
                W(gpsimd, "gpsimd", SG3, ve3.get((tc - 1, "grdma"), 0))
                W(gpsimd, "gpsimd", VE, ve3[(tc, "gr")])
                SG3.dma(gpsimd.dma_start(out=gaterd[:, 128 * tc:128 * tc + 128],
                                        in_=gst3[0:8, 128 * par:128 * par + 128]))
                ve3[(tc, "grdma")] = SG3.n

        fence()
        p3.close()

        if "xnT_d" in DEBUG:
            @blk.gpsimd
            def _(gpsimd):
                for k in range(8):
                    LG.dma(gpsimd.dma_start(out=dbg["xnT_d"][k], in_=xnT[:, WTOK * k:WTOK * k + WTOK]))
                gpsimd.wait_ge(LG.h, LG.n)

        p4 = contextlib.ExitStack()
        E4 = p4.enter_context
        wst4 = E4(nc.sbuf_tensor("wst4", [128, 2 * 1024], F32))
        w1q = E4(nc.sbuf_tensor("w1q", [128, 2 * 8 * 1024], BF16))
        w2q = E4(nc.sbuf_tensor("w2q", [128, 2 * 8 * 1024], BF16))
        hbuf = E4(nc.sbuf_tensor("hbuf", [128, 2 * 8 * 512], BF16))
        gbc = E4(nc.sbuf_tensor("gbc", [128, WTOK], BF16))
        grow4 = E4(nc.sbuf_tensor("grow4", [1, WTOK], BF16))

        ld4 = {}
        cv4 = {}
        te4 = {}
        ve4 = {}
        se4 = {}
        h_ve = []
        acc_ve = []
        l2_te = []
        gmul_last = {}

        l2done = {}

        xnv = xnT[:].rearrange("p (k w) -> p k w", k=8, w=WTOK)

        def stage_step(eq, kk):
            if eq >= 32:
                return
            e, q = divmod(eq, 4)
            wo = 8192 * (eq % 2)
            hh = kk % 2

            def _dma(eng, ename):
                gk = 16 * eq + kk
                if gk >= 2:
                    peq, pkk = divmod(gk - 2, 16)
                    W(eng, ename, VE, cv4.get((peq, pkk), 0))
                if kk < 8:
                    src = w1_in[e, 128 * kk:128 * kk + 128, 1024 * q:1024 * q + 1024]
                else:
                    c2 = kk - 8
                    src = w2_in[e, 1024 * q + 128 * c2:1024 * q + 128 * c2 + 128, 0:1024]
                LW[hh].dma(eng.dma_start(out=wst4[:, 1024 * hh:1024 * hh + 1024], in_=src))
                ld4[(eq, kk)] = LW[hh].n

            if hh == 0:
                @blk.sync
                def _(sync):
                    _dma(sync, "sync")
            else:
                @blk.gpsimd
                def _(gpsimd):
                    _dma(gpsimd, "gpsimd")

            @blk.vector
            def _(vector):
                W(vector, "vector", LW[hh], ld4[(eq, kk)])
                if kk == 0:
                    W(vector, "vector", TE, te4.get((eq - 2, "done"), 0))
                dst = w1q if kk < 8 else w2q
                ki = kk % 8
                VE.inc(vector.tensor_copy(dst[:, wo + 1024 * ki:wo + 1024 * ki + 1024],
                                          wst4[:, 1024 * hh:1024 * hh + 1024]))
                cv4[(eq, kk)] = VE.n

        def emit_gbc(eq, e):
            @blk.gpsimd
            def _(gpsimd):
                W(gpsimd, "gpsimd", SG4, ld4.get((eq - 4, "grow"), 0))
                if e > 0:
                    W(gpsimd, "gpsimd", TE, te4.get((4 * e - 1, "gbcmm"), 0))
                SG4.dma(gpsimd.dma_start(out=grow4[:], in_=gaterd[e:e + 1, :]))
                ld4[(eq, "grow")] = SG4.n

            for tt in range(4):
                @blk.tensor
                def _(tensor, tt=tt):
                    W(tensor, "tensor", SG4, ld4[(eq, "grow")])
                    if e > 0:
                        W(tensor, "tensor", VE, gmul_last.get(e - 1, 0))
                    W(tensor, "tensor", VE, ve4.get((eq, "gbc", tt - 1), 0))
                    if h_ve:
                        W(tensor, "tensor", VE, h_ve[-1])
                    TE.inc(tensor.matmul(pb[4][:, 0:512], ones1[:],
                                         grow4[0:1, 512 * tt:512 * tt + 512], start=True, stop=True))
                    te4[(eq, "gbc", tt)] = TE.n
                    te4[(eq + 3, "gbcmm")] = TE.n

                @blk.vector
                def _(vector, tt=tt):
                    W(vector, "vector", TE, te4[(eq, "gbc", tt)])
                    VE.inc(vector.tensor_copy(gbc[:, 512 * tt:512 * tt + 512], pb[4][:, 0:512]))
                    ve4[(eq, "gbc", tt)] = VE.n
            ve4[(eq, "gbc")] = VE.n

        L1BANKS = [pb[0], pb[1], pb[5], pb[4]]

        def emit_L1(eq, tt):
            e = eq // 4
            wo = 8192 * (eq % 2)
            ho = 4096 * (tt % 2)
            for grp in range(2):
                ms = range(4 * grp, 4 * grp + 4)
                for m in ms:
                    gm = 32 * eq + 8 * tt + m
                    hb = L1BANKS[gm % 4]

                    @blk.tensor
                    def _(tensor, m=m, hb=hb, gm=gm):
                        if m == 0 and tt == 0:
                            W(tensor, "tensor", VE, cv4[(eq, 7)])
                            W(tensor, "tensor", VE, ve4[(4 * (eq // 4), "gbc")])
                        if gm >= 4:
                            W(tensor, "tensor", VE, h_ve[gm - 4])
                        for k in range(7):
                            tensor.matmul(hb[:, 0:512],
                                          w1q[:, wo + 1024 * k + 128 * m:wo + 1024 * k + 128 * m + 128],
                                          xnv[:, k, 512 * tt:512 * tt + 512], start=(k == 0), stop=False)
                        TE.inc(tensor.matmul(hb[:, 0:512],
                                             w1q[:, wo + 7 * 1024 + 128 * m:wo + 7 * 1024 + 128 * m + 128],
                                             xnv[:, 7, 512 * tt:512 * tt + 512], start=False, stop=True))
                        te4[(eq, tt, m)] = TE.n

                @blk.scalar
                def _(scalar, ms=ms):
                    for m in ms:
                        gm = 32 * eq + 8 * tt + m
                        W(scalar, "scalar", TE, te4[(eq, tt, m)])
                        SE.inc(scalar.activation(out=L1BANKS[gm % 4][:, 0:512],
                                                 in_=L1BANKS[gm % 4][:, 0:512], func=AF.Silu))
                        se4[(eq, tt, m)] = SE.n

                @blk.vector
                def _(vector, ms=ms, ho=ho):
                    for m in ms:
                        gm = 32 * eq + 8 * tt + m
                        W(vector, "vector", SE, se4[(eq, tt, m)])
                        if m == 0:
                            prev = (eq, tt - 2) if tt >= 2 else (eq - 1, tt + 2)
                            W(vector, "vector", TE, l2done.get(prev, 0))
                        VE.inc(vector.tensor_tensor(out=hbuf[:, ho + 512 * m:ho + 512 * m + 512],
                                                    in0=L1BANKS[gm % 4][:, 0:512],
                                                    in1=gbc[:, 512 * tt:512 * tt + 512],
                                                    op=ALU.mult))
                        ve4[(eq, tt, m)] = VE.n
                        h_ve.append(VE.n)
                        if m == 7:
                            gmul_last[e] = VE.n

        def emit_L2(eq, tt):
            wo = 8192 * (eq % 2)
            ho = 4096 * (tt % 2)
            for oc in range(8):
                go = 32 * eq + 8 * tt + oc
                ob = pb[2] if go % 2 == 0 else pb[3]

                @blk.tensor
                def _(tensor, oc=oc, ob=ob, go=go):
                    if tt == 0 and oc == 0:
                        W(tensor, "tensor", VE, cv4[(eq, 15)])
                    W(tensor, "tensor", VE, ve4[(eq, tt, 7)])
                    if go >= 2:
                        W(tensor, "tensor", VE, acc_ve[go - 2])
                    for m in range(7):
                        tensor.matmul(ob[:, 0:512],
                                      w2q[:, wo + 1024 * m + 128 * oc:wo + 1024 * m + 128 * oc + 128],
                                      hbuf[:, ho + 512 * m:ho + 512 * m + 512], start=(m == 0), stop=False)
                    TE.inc(tensor.matmul(ob[:, 0:512],
                                         w2q[:, wo + 7 * 1024 + 128 * oc:wo + 7 * 1024 + 128 * oc + 128],
                                         hbuf[:, ho + 512 * 7:ho + 512 * 7 + 512], start=False, stop=True))
                    te4[(eq, tt, "l2", oc)] = TE.n

                @blk.vector
                def _(vector, oc=oc, ob=ob):
                    W(vector, "vector", TE, te4[(eq, tt, "l2", oc)])
                    VE.inc(vector.tensor_tensor(
                        out=gacc[:, WTOK * oc + 512 * tt:WTOK * oc + 512 * tt + 512],
                        in0=gacc[:, WTOK * oc + 512 * tt:WTOK * oc + 512 * tt + 512],
                        in1=ob[:, 0:512], op=ALU.add))
                    acc_ve.append(VE.n)
            l2done[(eq, tt)] = TE.n
            if tt == 3:
                te4[(eq, "done")] = TE.n

        for kk in range(16):
            stage_step(0, kk)

        SEQ = [("L1", 0), ("L1", 1), ("L2", 0), ("L1", 2), ("L2", 1), ("L1", 3), ("L2", 2), ("L2", 3)]
        for eq in range(32):
            e, q = divmod(eq, 4)
            if q == 0:
                emit_gbc(eq, e)
            for bi, (kind, tt) in enumerate(SEQ):
                if kind == "L1":
                    emit_L1(eq, tt)
                else:
                    emit_L2(eq, tt)
                stage_step(eq + 1, 2 * bi)
                stage_step(eq + 1, 2 * bi + 1)

        fence()
        p4.close()

        if "gacc_d" in DEBUG:
            @blk.gpsimd
            def _(gpsimd):
                for oc in range(8):
                    LG.dma(gpsimd.dma_start(out=dbg["gacc_d"][oc], in_=gacc[:, WTOK * oc:WTOK * oc + WTOK]))
                gpsimd.wait_ge(LG.h, LG.n)

        p5 = contextlib.ExitStack()
        E5 = p5.enter_context
        wcst = E5(nc.sbuf_tensor("wcst", [128, 1024], F32))
        wcb = E5(nc.sbuf_tensor("wcb", [128, 8 * 1024], BF16))
        vw = E5(nc.sbuf_tensor("vw", [128, 8 * 2176], BF16))
        sgt = E5(nc.sbuf_tensor("sgt", [128, 2 * 512], F32))
        gsl = E5(nc.sbuf_tensor("gsl", [128, 2 * 512], F32))
        otok = E5(nc.sbuf_tensor("otok", [128, 4 * 1024], F32))
        xsl5 = E5(nc.sbuf_tensor("xsl5", [128, 2 * 1024], F32))

        ld5 = {}
        se5 = {}
        ve5 = {}
        te5 = {}
        lg5 = {}
        so5 = {}

        for k in range(8):
            @blk.sync
            def _(sync, k=k):
                if k == 0:
                    sync.wait_ge(LD.h, LD.n)
                W(sync, "sync", SE, se5.get(("wc", k - 1), 0))
                LD.dma(sync.dma_start(out=wcst[:], in_=wc_in[128 * k:128 * k + 128, :]))
                ld5[("wc", k)] = LD.n

            @blk.scalar
            def _(scalar, k=k):
                W(scalar, "scalar", LD, ld5[("wc", k)])
                SE.inc(scalar.copy(out=wcb[:, 1024 * k:1024 * k + 1024], in_=wcst[:]))
                se5[("wc", k)] = SE.n

        @blk.gpsimd
        def _(gpsimd):
            W(gpsimd, "gpsimd", CC, CC.n)
            mainview = agout.rearrange("h p (j t) -> (h p j) t", t=2048)
            bndview = agout.rearrange("h p (r t) -> (h p r) t", t=128)
            for h in range(8):
                SV5.dma(gpsimd.indirect_dma_start(
                    out=vw[:, 2176 * h + 128:2176 * h + 2176], out_offset=None,
                    in_=mainview, in_offset=bass.IndirectOffsetOnAxis(ap=widxS[:, h:h + 1], axis=0)))
                SV5.dma(gpsimd.indirect_dma_start(
                    out=vw[:, 2176 * h:2176 * h + 128], out_offset=None,
                    in_=bndview, in_offset=bass.IndirectOffsetOnAxis(ap=widxS[:, 8 + h:8 + h + 1], axis=0)))
            lg5["vw"] = SV5.n

        @blk.vector
        def _(vector):
            W(vector, "vector", SV5, lg5["vw"])
            for h in range(7):
                vector.tensor_scalar(out=vw[:, 2176 * h:2176 * h + 128], in0=vw[:, 2176 * h:2176 * h + 128],
                                     scalar1=bscaleS[:, 0:1], scalar2=None, op0=ALU.mult)
            VE.inc(vector.tensor_scalar(out=vw[:, 2176 * 7:2176 * 7 + 128], in0=vw[:, 2176 * 7:2176 * 7 + 128],
                                        scalar1=bscaleS[:, 0:1], scalar2=None, op0=ALU.mult))
            ve5["vw"] = VE.n

        sg_hist = []
        ot_hist = []
        for t in range(4):
            obuf = 0

            for oc in range(8):
                sb = pb[0] if oc % 2 == 0 else pb[1]
                spar = oc % 2
                gidx5 = 8 * t + oc

                @blk.tensor
                def _(tensor, t=t, oc=oc, sb=sb, gidx5=gidx5):
                    if t == 0 and oc == 0:
                        W(tensor, "tensor", SE, se5[("wc", 7)])
                        W(tensor, "tensor", VE, ve5["vw"])
                    if gidx5 >= 2:
                        W(tensor, "tensor", VE, sg_hist[gidx5 - 2])
                    for h in range(7):
                        tensor.matmul(sb[:, 0:512], wcb[:, 1024 * h + 128 * oc:1024 * h + 128 * oc + 128],
                                      vw[:, 2176 * h + 65 + 512 * t:2176 * h + 65 + 512 * t + 512],
                                      start=(h == 0), stop=False)
                    TE.inc(tensor.matmul(sb[:, 0:512], wcb[:, 7 * 1024 + 128 * oc:7 * 1024 + 128 * oc + 128],
                                         vw[:, 2176 * 7 + 65 + 512 * t:2176 * 7 + 65 + 512 * t + 512],
                                         start=False, stop=True))
                    te5[("s", t, oc)] = TE.n

                @blk.scalar
                def _(scalar, t=t, oc=oc, spar=spar, gidx5=gidx5):
                    W(scalar, "scalar", TE, te5[("s", t, oc)])
                    if gidx5 >= 2:
                        W(scalar, "scalar", VE, sg_hist[gidx5 - 2])
                    SE.inc(scalar.activation(out=gsl[:, 512 * spar:512 * spar + 512],
                                             in_=gacc[:, WTOK * oc + 512 * t:WTOK * oc + 512 * t + 512],
                                             func=AF.Silu))
                    se5[("g", t, oc)] = SE.n

                @blk.vector
                def _(vector, t=t, oc=oc, sb=sb, spar=spar, gidx5=gidx5):
                    W(vector, "vector", SE, se5[("g", t, oc)])
                    if gidx5 >= 2:
                        W(vector, "vector", TE, te5.get(("tr",) + divmod(gidx5 - 2, 8), 0))
                    VE.inc(vector.tensor_tensor(out=sgt[:, 512 * spar:512 * spar + 512],
                                                in0=sb[:, 0:512], in1=gsl[:, 512 * spar:512 * spar + 512],
                                                op=ALU.mult))
                    ve5[("sg", t, oc)] = VE.n
                    sg_hist.append(VE.n)

                @blk.tensor
                def _(tensor, t=t, oc=oc, spar=spar, gidx5=gidx5):
                    W(tensor, "tensor", VE, ve5[("sg", t, oc)])
                    tb = pb[2] if oc % 2 == 0 else pb[3]
                    if gidx5 >= 2:
                        W(tensor, "tensor", VE, ot_hist[gidx5 - 2])
                    for i in range(3):
                        tensor.transpose(tb[:, 128 * i:128 * i + 128], sgt[:, 512 * spar + 128 * i:512 * spar + 128 * i + 128], idf[:])
                    TE.inc(tensor.transpose(tb[:, 384:512], sgt[:, 512 * spar + 384:512 * spar + 512], idf[:]))
                    te5[("tr", t, oc)] = TE.n

                @blk.vector
                def _(vector, t=t, oc=oc, obuf=obuf, gidx5=gidx5):
                    W(vector, "vector", TE, te5[("tr", t, oc)])
                    tb = pb[2] if oc % 2 == 0 else pb[3]
                    if oc == 0:
                        W(vector, "vector", SO, so5.get(("tot", t - 1), 0))
                    for i in range(3):
                        vector.tensor_copy(otok[:, 1024 * i + 128 * oc:1024 * i + 128 * oc + 128],
                                           tb[:, 128 * i:128 * i + 128])
                    VE.inc(vector.tensor_copy(otok[:, 3072 + 128 * oc:3072 + 128 * oc + 128],
                                              tb[:, 384:512]))
                    ve5[("ot", t, oc)] = VE.n
                    ot_hist.append(VE.n)

            for i in range(4):
                @blk.sync
                def _(sync, t=t, i=i):
                    gx = 4 * t + i
                    if gx >= 2:
                        pt, pi = divmod(gx - 2, 4)
                        W(sync, "sync", VE, ve5.get(("add", pt, pi), 0))
                    LX[i % 2].dma(sync.dma_start(out=xsl5[:, 1024 * (i % 2):1024 * (i % 2) + 1024],
                                                 in_=x_slice[512 * t + 128 * i:512 * t + 128 * i + 128, :]))
                    ld5[("x", t, i)] = LX[i % 2].n

                @blk.vector
                def _(vector, t=t, i=i):
                    W(vector, "vector", VE, ve5[("ot", t, 7)])
                    W(vector, "vector", LX[i % 2], ld5[("x", t, i)])
                    VE.inc(vector.tensor_tensor(out=otok[:, 1024 * i:1024 * i + 1024],
                                                in0=otok[:, 1024 * i:1024 * i + 1024],
                                                in1=xsl5[:, 1024 * (i % 2):1024 * (i % 2) + 1024], op=ALU.add))
                    ve5[("add", t, i)] = VE.n

                @blk.sync
                def _(sync, t=t, i=i):
                    W(sync, "sync", VE, ve5[("add", t, i)])
                    SO.dma(sync.dma_start(out=out[512 * t + 128 * i:512 * t + 128 * i + 128, :],
                                          in_=otok[:, 1024 * i:1024 * i + 1024]))
                    so5[("out", t, i)] = SO.n
                    so5[("tot", t)] = SO.n

        @blk.sync
        def _(sync):
            sync.wait_ge(SO.h, SO.n)

        p5.close()

    st.close()
    return nc


_CACHE = {}


def _build_cached():
    key = tuple(sorted(DEBUG))
    if key not in _CACHE:
        _CACHE[key] = build_program()
    return _CACHE[key]


def kernel(L0_attn_seq, L0_attn_store_norm_g, L0_attn_retrieve_norm_g, L0_attn_w_kv,
           L0_attn_w_q, L0_attn_w_step, L0_attn_w_mom, L0_attn_w_decay, L0_attn_w_gate,
           L0_attn_w_combine, L0_attn_mh_gamma, L0_attn_mem_w0, L0_attn_mem_w1,
           L0_ffn_Wg, L0_ffn_W1, L0_ffn_W2):
    nc = _build_cached()
    f32 = np.float32
    x = np.ascontiguousarray(np.asarray(L0_attn_seq, dtype=f32).reshape(NTOK, D))
    w_kv = np.asarray(L0_attn_w_kv, dtype=f32)
    w_q = np.asarray(L0_attn_w_q, dtype=f32)
    w_step = np.asarray(L0_attn_w_step, dtype=f32)
    w_mom = np.asarray(L0_attn_w_mom, dtype=f32)
    w_decay = np.asarray(L0_attn_w_decay, dtype=f32)
    w_gate = np.asarray(L0_attn_w_gate, dtype=f32)
    w_comb = np.ascontiguousarray(np.asarray(L0_attn_w_combine, dtype=f32))
    mh_gamma = np.asarray(L0_attn_mh_gamma, dtype=f32)
    mem_w0 = np.ascontiguousarray(np.asarray(L0_attn_mem_w0, dtype=f32))
    mem_w1 = np.ascontiguousarray(np.asarray(L0_attn_mem_w1, dtype=f32))
    wg = np.ascontiguousarray(np.asarray(L0_ffn_Wg, dtype=f32))
    w1 = np.ascontiguousarray(np.asarray(L0_ffn_W1, dtype=f32))
    w2 = np.ascontiguousarray(np.asarray(L0_ffn_W2, dtype=f32))
    sgain = np.ascontiguousarray(np.asarray(L0_attn_store_norm_g, dtype=f32).reshape(8, 128))
    rgain = np.ascontiguousarray(np.asarray(L0_attn_retrieve_norm_g, dtype=f32).reshape(8, 128))

    idb = np.eye(128, dtype=ml_dtypes.bfloat16)
    idf = np.eye(128, dtype=np.float32)
    oseg = np.zeros((128, 2), dtype=ml_dtypes.bfloat16)
    oseg[0:64, 0] = ml_dtypes.bfloat16(1.0 / 64)
    oseg[64:128, 1] = ml_dtypes.bfloat16(1.0 / 64)
    cvc = np.zeros((128, 4), dtype=f32)
    cvc[:, 0] = EPS_LN
    cvc[:, 1] = EPS_RMS
    cvc[:, 2] = EPS_RMS * EPS_LN
    zf = np.zeros((128, 128), dtype=f32)

    in_maps = []
    for i in range(8):
        widx = np.zeros((128, 16), dtype=np.uint32)
        pr = np.arange(128, dtype=np.uint32)
        for h in range(8):
            widx[:, h] = (h * 128 + pr) * 8 + i
            widx[:, 8 + h] = (h * 128 + pr) * 128 + (16 * i - 1 if i > 0 else 0)
        bscale = np.full((128, 1), 1.0 if i > 0 else 0.0, dtype=f32)
        m = {
            "x_full": x,
            "x_slice": np.ascontiguousarray(x[WTOK * i:WTOK * (i + 1)]),
            "wk_h": np.ascontiguousarray(w_kv[:, 128 * i:128 * i + 128]),
            "wv_h": np.ascontiguousarray(w_kv[:, 1024 + 128 * i:1024 + 128 * i + 128]),
            "wq_h": np.ascontiguousarray(w_q[:, 128 * i:128 * i + 128]),
            "w4_h": np.ascontiguousarray(np.stack([w_step[:, i], w_gate[:, i]], axis=1)),
            "wmd_h": np.ascontiguousarray(np.stack([w_mom[:, i], w_decay[:, i]], axis=1)),
            "sgain": sgain, "rgain": rgain,
            "mw0": mem_w0, "mw1": mem_w1,
            "gamma_row": np.ascontiguousarray(mh_gamma[i].reshape(1, 128)),
            "wg_in": wg, "w1_in": w1, "w2_in": w2, "wc_in": w_comb,
            "idb_in": idb, "idf_in": idf, "oseg_in": oseg, "cv_in": cvc,
            "widx_in": widx, "bscale_in": bscale, "zf_in": zf,
        }
        in_maps.append(m)

    res = run_bass_kernel_spmd(nc, in_maps, core_ids=list(range(8)))
    outs = [res.results[i]["out"] for i in range(8)]
    full = np.concatenate(outs, axis=0).reshape(1, NTOK, D).astype(np.float32)
    if DEBUG:
        kernel._debug_results = res
    return full


# version 08
# speedup vs baseline: 1.1779x; 1.0003x over previous
import contextlib
import numpy as np
import ml_dtypes
import concourse.bass as bass
import concourse.mybir as mybir
from concourse.bass_utils import run_bass_kernel_spmd

F32 = mybir.dt.float32
BF16 = mybir.dt.bfloat16
U32 = mybir.dt.uint32
AF = mybir.ActivationFunctionType
ALU = mybir.AluOpType
AX = mybir.AxisListType

H = 8
DH = 128
CH = 64
D = 1024
NTOK = 16384
NCHUNK = NTOK // CH
NPAIR = NCHUNK // 2
WTOK = NTOK // 8
EPS_LN = 1e-5
EPS_RMS = 1e-6
LR_EFF = 0.01 * 2.0 / DH

DEBUG = []


class Sem:
    def __init__(self, stack, nc, name):
        self.h = stack.enter_context(nc.semaphore(name))
        self.name = name
        self.n = 0

    def inc(self, instr, k=1):
        instr.then_inc(self.h, k)
        self.n += k
        return instr

    def dma(self, instr):
        return self.inc(instr, 16)


_WAITED = {}


def W(eng, ename, sem, val):
    if val is None or val <= 0:
        return
    key = (ename, sem.name)
    if _WAITED.get(key, -1) >= val:
        return
    _WAITED[key] = val
    eng.wait_ge(sem.h, val)


def build_program():
    _WAITED.clear()
    nc = bass.Bass("TRN2", target_bir_lowering=False)

    x_full = nc.dram_tensor("x_full", [NTOK, D], F32, kind="ExternalInput").ap()
    x_slice = nc.dram_tensor("x_slice", [WTOK, D], F32, kind="ExternalInput").ap()
    wk_h = nc.dram_tensor("wk_h", [D, DH], F32, kind="ExternalInput").ap()
    wv_h = nc.dram_tensor("wv_h", [D, DH], F32, kind="ExternalInput").ap()
    wq_h = nc.dram_tensor("wq_h", [D, DH], F32, kind="ExternalInput").ap()
    w4_h = nc.dram_tensor("w4_h", [D, 2], F32, kind="ExternalInput").ap()
    wmd_h = nc.dram_tensor("wmd_h", [D, 2], F32, kind="ExternalInput").ap()
    sgain_in = nc.dram_tensor("sgain", [8, 128], F32, kind="ExternalInput").ap()
    rgain_in = nc.dram_tensor("rgain", [8, 128], F32, kind="ExternalInput").ap()
    mw0_in = nc.dram_tensor("mw0", [128, 128], F32, kind="ExternalInput").ap()
    mw1_in = nc.dram_tensor("mw1", [128, 128], F32, kind="ExternalInput").ap()
    gamma_in = nc.dram_tensor("gamma_row", [1, 128], F32, kind="ExternalInput").ap()
    wg_in = nc.dram_tensor("wg_in", [D, H], F32, kind="ExternalInput").ap()
    w1_in = nc.dram_tensor("w1_in", [8, D, 4096], F32, kind="ExternalInput").ap()
    w2_in = nc.dram_tensor("w2_in", [8, 4096, D], F32, kind="ExternalInput").ap()
    wc_in = nc.dram_tensor("wc_in", [D, D], F32, kind="ExternalInput").ap()
    idb_in = nc.dram_tensor("idb_in", [128, 128], BF16, kind="ExternalInput").ap()
    idf_in = nc.dram_tensor("idf_in", [128, 128], F32, kind="ExternalInput").ap()
    oseg_in = nc.dram_tensor("oseg_in", [128, 2], BF16, kind="ExternalInput").ap()
    cv_in = nc.dram_tensor("cv_in", [128, 4], F32, kind="ExternalInput").ap()
    widx_in = nc.dram_tensor("widx_in", [128, 16], U32, kind="ExternalInput").ap()
    bscale_in = nc.dram_tensor("bscale_in", [128, 1], F32, kind="ExternalInput").ap()
    zf_in = nc.dram_tensor("zf_in", [128, 128], F32, kind="ExternalInput").ap()
    out = nc.dram_tensor("out", [WTOK, D], F32, kind="ExternalOutput").ap()

    def scratch(name, shape, dtype):
        if name in DEBUG:
            return nc.dram_tensor(name, shape, dtype, kind="ExternalOutput").ap()
        return nc.dram_tensor(name, shape, dtype).ap()

    kTd = scratch("kTd", [128, NTOK], BF16)
    vTd = scratch("vTd", [128, NTOK], BF16)
    rTd = scratch("rTd", [128, NTOK + 64], BF16)
    gswd = scratch("gswd", [128, 128], F32)
    lrd = scratch("lrd", [1, NTOK], F32)
    gaterd = scratch("gaterd", [8, WTOK], BF16)
    agin_t = nc.dram_tensor("agin", [128, NTOK], BF16)
    agout_t = nc.dram_tensor("agout", [8, 128, NTOK], BF16)
    agin = agin_t.ap()
    agout = agout_t.ap()
    dbg = {}
    for nm, shape, dt in [
        ("agin_d", [128, NTOK], BF16),
        ("momb_d", [128, 256], F32), ("decb_d", [128, 256], F32),
        ("cm_d", [2, 128, 1024], F32),
        ("gacc_d", [8, 128, WTOK], F32),
        ("xnT_d", [8, 128, WTOK], BF16),
    ]:
        if nm in DEBUG:
            dbg[nm] = nc.dram_tensor(nm, shape, dt, kind="ExternalOutput").ap()

    st = contextlib.ExitStack()
    E = st.enter_context

    LD = Sem(st, nc, "LD")
    LG = Sem(st, nc, "LG")
    LX = [Sem(st, nc, "LX0"), Sem(st, nc, "LX1")]
    SK = Sem(st, nc, "SK")
    LW = [Sem(st, nc, "LW0"), Sem(st, nc, "LW1")]
    KV = [Sem(st, nc, "KV0"), Sem(st, nc, "KV1")]
    RW = [Sem(st, nc, "RW0"), Sem(st, nc, "RW1")]
    SA = [Sem(st, nc, "SA0"), Sem(st, nc, "SA1")]
    SC2 = Sem(st, nc, "SC2")
    SPK = Sem(st, nc, "SPK")
    SG3 = Sem(st, nc, "SG3")
    SG4 = Sem(st, nc, "SG4")
    SV5 = Sem(st, nc, "SV5")
    TE = Sem(st, nc, "TE")
    VE = Sem(st, nc, "VE")
    SE = Sem(st, nc, "SE")
    CC = Sem(st, nc, "CC")
    SO = Sem(st, nc, "SO")
    FN = Sem(st, nc, "FN")

    pb = [E(nc.psum_tensor(f"pb{i}", [128, 512], F32)) for i in range(6)]
    pbt = [E(nc.psum_tensor(f"pbt{i}", [128, 1024], BF16)) for i in range(2)]

    idb = E(nc.sbuf_tensor("idb", [128, 128], BF16))
    idf = E(nc.sbuf_tensor("idf", [128, 128], F32))
    oseg = E(nc.sbuf_tensor("oseg", [128, 2], BF16))
    cv = E(nc.sbuf_tensor("cv", [128, 4], F32))
    ones1 = E(nc.sbuf_tensor("ones1", [1, 128], BF16))
    onescol = E(nc.sbuf_tensor("onescol", [128, 1], BF16))
    onesf = E(nc.sbuf_tensor("onesf", [1, 128], F32))
    onec1f = E(nc.sbuf_tensor("onec1f", [1, 1], F32))
    mw0f = E(nc.sbuf_tensor("mw0f", [128, 128], F32))
    mw1f = E(nc.sbuf_tensor("mw1f", [128, 128], F32))
    w0b = E(nc.sbuf_tensor("w0b", [128, 128], BF16))
    w1b = E(nc.sbuf_tensor("w1b", [128, 128], BF16))
    w1Tb = E(nc.sbuf_tensor("w1Tb", [128, 128], BF16))
    wkb = E(nc.sbuf_tensor("wkb", [128, 8 * 128], BF16))
    wvb = E(nc.sbuf_tensor("wvb", [128, 8 * 128], BF16))
    wqb = E(nc.sbuf_tensor("wqb", [128, 8 * 128], BF16))
    w4b = E(nc.sbuf_tensor("w4b", [128, 8 * 2], BF16))
    wmdb = E(nc.sbuf_tensor("wmdb", [128, 8 * 2], BF16))
    gp1row = E(nc.sbuf_tensor("gp1row", [1, 128], F32))
    momb = E(nc.sbuf_tensor("momb", [128, 256], F32))
    decb = E(nc.sbuf_tensor("decb", [128, 256], F32))
    M0 = E(nc.sbuf_tensor("M0", [128, 128], F32))
    M1 = E(nc.sbuf_tensor("M1", [128, 128], F32))
    U0 = E(nc.sbuf_tensor("U0", [128, 128], F32))
    U1 = E(nc.sbuf_tensor("U1", [128, 128], F32))
    xnT = E(nc.sbuf_tensor("xnT", [128, 8 * WTOK], BF16))
    gacc = E(nc.sbuf_tensor("gacc", [128, 8 * WTOK], F32))
    widxS = E(nc.sbuf_tensor("widxS", [128, 16], U32))
    bscaleS = E(nc.sbuf_tensor("bscaleS", [128, 1], F32))
    zrow = E(nc.sbuf_tensor("zrow", [128, 64], BF16))

    with nc.Block() as blk:

        def fence():
            base = FN.n
            FN.n = base + 5
            tgt = FN.n

            @blk.sync
            def _(sync):
                for s in [LD, SO, SK, LX[0], LX[1], LW[0], LW[1], KV[0], KV[1]]:
                    W(sync, "sync", s, s.n)
                sync.sem_inc(FN.h, 1)
                sync.wait_ge(FN.h, tgt)

            @blk.gpsimd
            def _(gpsimd):
                for s in [LG, SC2, SPK, SG3, SG4, SV5, RW[0], RW[1], SA[0], SA[1]]:
                    W(gpsimd, "gpsimd", s, s.n)
                gpsimd.sem_inc(FN.h, 1)
                gpsimd.wait_ge(FN.h, tgt)

            @blk.tensor
            def _(tensor):
                tensor.wait_ge(TE.h, TE.n)
                tensor.sem_inc(FN.h, 1)
                tensor.wait_ge(FN.h, tgt)

            @blk.vector
            def _(vector):
                vector.wait_ge(VE.h, VE.n)
                vector.sem_inc(FN.h, 1)
                vector.wait_ge(FN.h, tgt)

            @blk.scalar
            def _(scalar):
                scalar.wait_ge(SE.h, SE.n)
                scalar.sem_inc(FN.h, 1)
                scalar.wait_ge(FN.h, tgt)

        p0 = contextlib.ExitStack()
        E0 = p0.enter_context
        wkst = E0(nc.sbuf_tensor("wkst", [128, 8 * 128], F32))
        wvst = E0(nc.sbuf_tensor("wvst", [128, 8 * 128], F32))
        wqst = E0(nc.sbuf_tensor("wqst", [128, 8 * 128], F32))
        w4st = E0(nc.sbuf_tensor("w4st", [128, 8 * 2], F32))
        wmdst = E0(nc.sbuf_tensor("wmdst", [128, 8 * 2], F32))
        sgainS = E0(nc.sbuf_tensor("sgainS", [128, 8], F32))
        rgainS = E0(nc.sbuf_tensor("rgainS", [128, 8], F32))
        grow0 = E0(nc.sbuf_tensor("grow0", [1, 128], F32))
        zf32 = E0(nc.sbuf_tensor("zf32", [128, 128], F32))

        @blk.sync
        def _(sync):
            for (dst, src) in [(idb, idb_in), (idf, idf_in), (oseg, oseg_in),
                               (cv, cv_in), (mw0f, mw0_in), (mw1f, mw1_in),
                               (widxS, widx_in), (bscaleS, bscale_in),
                               (grow0, gamma_in), (zf32, zf_in)]:
                LD.dma(sync.dma_start(out=dst[:], in_=src[:, :]))
            for k in range(8):
                LD.dma(sync.dma_start(out=wkst[:, 128 * k:128 * k + 128], in_=wk_h[128 * k:128 * k + 128, :]))
                LD.dma(sync.dma_start(out=wvst[:, 128 * k:128 * k + 128], in_=wv_h[128 * k:128 * k + 128, :]))
                LD.dma(sync.dma_start(out=wqst[:, 128 * k:128 * k + 128], in_=wq_h[128 * k:128 * k + 128, :]))
                LD.dma(sync.dma_start(out=w4st[:, 2 * k:2 * k + 2], in_=w4_h[128 * k:128 * k + 128, :]))
                LD.dma(sync.dma_start(out=wmdst[:, 2 * k:2 * k + 2], in_=wmd_h[128 * k:128 * k + 128, :]))
                LD.dma(sync.dma_start(out=sgainS[:, k:k + 1], in_=sgain_in[k:k + 1, :]))
                LD.dma(sync.dma_start(out=rgainS[:, k:k + 1], in_=rgain_in[k:k + 1, :]))

        @blk.vector
        def _(vector):
            W(vector, "vector", LD, LD.n)
            vector.memset(ones1[:], 1.0)
            vector.memset(onescol[:], 1.0)
            vector.memset(onesf[:], 1.0)
            vector.memset(onec1f[:], 1.0)
            vector.memset(M0[:], 0.0)
            vector.memset(M1[:], 0.0)
            vector.memset(U0[:], 0.0)
            vector.memset(U1[:], 0.0)
            vector.memset(zrow[:], 0.0)
            vector.memset(gacc[:], 0.0)
            for k in range(8):
                s = slice(128 * k, 128 * k + 128)
                vector.tensor_scalar(out=wkb[:, s], in0=wkst[:, s], scalar1=sgainS[:, k:k + 1],
                                     scalar2=None, op0=ALU.mult)
                vector.tensor_scalar(out=wvb[:, s], in0=wvst[:, s], scalar1=sgainS[:, k:k + 1],
                                     scalar2=None, op0=ALU.mult)
                vector.tensor_scalar(out=wqb[:, s], in0=wqst[:, s], scalar1=rgainS[:, k:k + 1],
                                     scalar2=None, op0=ALU.mult)
                vector.tensor_scalar(out=w4b[:, 2 * k:2 * k + 1], in0=w4st[:, 2 * k:2 * k + 1],
                                     scalar1=sgainS[:, k:k + 1], scalar2=None, op0=ALU.mult)
                vector.tensor_scalar(out=w4b[:, 2 * k + 1:2 * k + 2], in0=w4st[:, 2 * k + 1:2 * k + 2],
                                     scalar1=rgainS[:, k:k + 1], scalar2=None, op0=ALU.mult)
                vector.tensor_scalar(out=wmdb[:, 2 * k:2 * k + 2], in0=wmdst[:, 2 * k:2 * k + 2],
                                     scalar1=sgainS[:, k:k + 1], scalar2=None, op0=ALU.mult)
            vector.tensor_scalar(out=gp1row[:], in0=grow0[:], scalar1=1.0, scalar2=None, op0=ALU.add)
            vector.tensor_copy(w0b[:], mw0f[:])
            VE.inc(vector.tensor_copy(w1b[:], mw1f[:]))

        @blk.tensor
        def _(tensor):
            W(tensor, "tensor", VE, VE.n)
            TE.inc(tensor.transpose(pbt[0][:, 0:128], w1b[:], idb[:]))

        @blk.scalar
        def _(scalar):
            W(scalar, "scalar", TE, TE.n)
            SE.inc(scalar.copy(out=w1Tb[:], in_=pbt[0][:, 0:128]))

        @blk.gpsimd
        def _(gpsimd):
            W(gpsimd, "gpsimd", VE, VE.n)
            LG.dma(gpsimd.dma_start(out=rTd[:, NTOK:NTOK + 64], in_=zrow[:]))
            W(gpsimd, "gpsimd", LD, LD.n)
            LG.dma(gpsimd.dma_start(out=gswd[:, :], in_=zf32[:]))

        fence()
        p0.close()

        p1 = contextlib.ExitStack()
        E1 = p1.enter_context
        xst = E1(nc.sbuf_tensor("xst", [128, 2 * 1024], F32))
        xsq = E1(nc.sbuf_tensor("xsq", [128, 2 * 1024], F32))
        ytok = E1(nc.sbuf_tensor("ytok", [128, 2 * 1024], BF16))
        yTst = E1(nc.sbuf_tensor("yTst", [128, 8192], BF16))
        stats = E1(nc.sbuf_tensor("stats", [128, 2 * 16], F32))
        cm2 = E1(nc.sbuf_tensor("cm2", [2, 2 * 1024], F32))
        cmtok = E1(nc.sbuf_tensor("cmtok", [128, 2 * 1024], F32))
        kvr = E1(nc.sbuf_tensor("kvr", [128, 2 * 3 * 512], BF16))
        sgm = E1(nc.sbuf_tensor("sgm", [2, 2 * 512], F32))
        lrrow = E1(nc.sbuf_tensor("lrrow", [1, 2 * 512], F32))

        se_y = {}
        se_sq1 = {}
        ve_st = {}
        ve_ev = {}
        ve_cm = {}
        te_tc = {}
        te_pj = {}
        se_ev = {}
        ve_lr = {}
        ld_x = {}
        ld_kvr = {}
        lg_cm = {}
        lg_pk = {}


        for g in range(128):
            t, i = divmod(g, 4)
            par = g % 2
            so = 16 * par
            xv = xst[:, 1024 * par:1024 * par + 1024]
            xqv = xsq[:, 1024 * par:1024 * par + 1024]
            yv = ytok[:, 1024 * par:1024 * par + 1024]

            @blk.sync
            def _(sync, g=g, par=par, xv=xv):
                W(sync, "sync", VE, se_y.get(g - 2, 0))
                LX[par].dma(sync.dma_start(out=xv, in_=x_full[128 * g:128 * g + 128, :]))
                ld_x[g] = LX[par].n

            @blk.vector
            def _(vector, g=g, par=par, xv=xv, xqv=xqv, so=so):
                W(vector, "vector", LX[par], ld_x[g])
                W(vector, "vector", SE, se_sq1.get(g - 2, 0))
                vector.reduce_sum(stats[:, so + 0:so + 1], xv, axis=AX.X)
                vector.tensor_tensor(out=xqv, in0=xv, in1=xv, op=ALU.mult)
                vector.drain()
                vector.reduce_sum(stats[:, so + 1:so + 2], xqv, axis=AX.X)
                vector.tensor_scalar(out=stats[:, so + 2:so + 3], in0=stats[:, so + 0:so + 1],
                                     scalar1=1.0 / D, scalar2=None, op0=ALU.mult)
                vector.drain()
                vector.tensor_scalar(out=stats[:, so + 3:so + 4], in0=stats[:, so + 1:so + 2],
                                     scalar1=1.0 / D, scalar2=None, op0=ALU.mult)
                vector.drain()
                VE.inc(vector.scalar_tensor_tensor(out=stats[:, so + 4:so + 5], in0=stats[:, so + 2:so + 3],
                                                   scalar=stats[:, so + 2:so + 3], in1=stats[:, so + 3:so + 4],
                                                   op0=ALU.mult, op1=ALU.subtract))
                ve_st[(g, "var")] = VE.n

            @blk.scalar
            def _(scalar, g=g, so=so):
                W(scalar, "scalar", VE, ve_st[(g, "var")])
                SE.inc(scalar.activation(out=stats[:, so + 5:so + 6], in_=stats[:, so + 4:so + 5],
                                         func=AF.Sqrt, scale=-(1.0 + EPS_RMS), bias=cv[:, 2:3]))
                se_sq1[g] = SE.n

            @blk.vector
            def _(vector, g=g, xv=xv, yv=yv, so=so):
                W(vector, "vector", SE, se_sq1[g])
                W(vector, "vector", TE, te_tc.get(g - 2, 0))
                vector.reciprocal(stats[:, so + 6:so + 7], stats[:, so + 5:so + 6])
                vector.drain()
                vector.scalar_tensor_tensor(out=stats[:, so + 7:so + 8], in0=stats[:, so + 2:so + 3],
                                            scalar=-1.0, in1=stats[:, so + 6:so + 7],
                                            op0=ALU.mult, op1=ALU.mult)
                ve_st[g] = VE.n
                vector.drain()
                VE.inc(vector.tensor_scalar(out=yv, in0=xv, scalar1=stats[:, so + 6:so + 7],
                                            scalar2=stats[:, so + 7:so + 8], op0=ALU.mult, op1=ALU.add))
                se_y[g] = VE.n

            @blk.tensor
            def _(tensor, g=g, yv=yv, par=par):
                W(tensor, "tensor", VE, se_y[g])
                W(tensor, "tensor", VE, ve_ev.get(g - 2, 0))
                W(tensor, "tensor", VE, ve_cm.get(g - 2, 0))
                for k in range(7):
                    tensor.transpose(pbt[par][:, 128 * k:128 * k + 128], yv[:, 128 * k:128 * k + 128], idb[:])
                TE.inc(tensor.transpose(pbt[par][:, 896:1024], yv[:, 896:1024], idb[:]))
                tensor.matmul(pb[4][0:2, 0:512], oseg[:], yv[:, 0:512], start=True, stop=True)
                TE.inc(tensor.matmul(pb[5][0:2, 0:512], oseg[:], yv[:, 512:1024], start=True, stop=True))
                te_tc[g] = TE.n

            @blk.vector
            def _(vector, g=g, par=par, t=t, i=i):
                W(vector, "vector", TE, te_tc[g])
                buf = t % 2
                if t >= 2 and i == 0:
                    W(vector, "vector", TE, te_pj.get(t - 2, 0))
                dstv = yTst[:].rearrange("p (b k u c) -> p b k u c", b=2, k=8, u=4, c=128)[:, buf, :, i, :]
                srcv = pbt[par][:].rearrange("p (k c) -> p k c", k=8, c=128)
                vector.tensor_copy(dstv, srcv)
                W(vector, "vector", SC2, lg_cm.get(g - 2, 0))
                vector.tensor_copy(cm2[:, 1024 * par:1024 * par + 512], pb[4][0:2, 0:512])
                VE.inc(vector.tensor_copy(cm2[:, 1024 * par + 512:1024 * par + 1024], pb[5][0:2, 0:512]))
                ve_cm[g] = VE.n
                ve_ev[g] = VE.n

            @blk.gpsimd
            def _(gpsimd, g=g, par=par):
                W(gpsimd, "gpsimd", SC2, lg_cm.get(g - 1, 0))
                W(gpsimd, "gpsimd", VE, ve_cm[g])
                tix = g // 64
                row = (2 * g) % 128
                SC2.dma(gpsimd.dma_start(out=cmtok[row:row + 2, 1024 * tix:1024 * tix + 1024],
                                        in_=cm2[:, 1024 * par:1024 * par + 1024]))
                lg_cm[g] = SC2.n

            if i == 3:
                buf = t % 2
                pso = 512 * 0

                @blk.tensor
                def _(tensor, t=t, buf=buf, g=g):
                    W(tensor, "tensor", VE, ve_ev[g])
                    W(tensor, "tensor", SE, se_ev.get(t - 2, 0))
                    yview = yTst[:].rearrange("p (b k u) -> p b k u", b=2, k=8, u=512)
                    for k in range(8):
                        tensor.matmul(pb[0][:, 0:512], wkb[:, 128 * k:128 * k + 128],
                                      yview[:, buf, k, :], start=(k == 0), stop=(k == 7))
                    for k in range(8):
                        tensor.matmul(pb[1][:, 0:512], wvb[:, 128 * k:128 * k + 128],
                                      yview[:, buf, k, :], start=(k == 0), stop=(k == 7))
                    for k in range(8):
                        tensor.matmul(pb[2][:, 0:512], wqb[:, 128 * k:128 * k + 128],
                                      yview[:, buf, k, :], start=(k == 0), stop=(k == 7))
                    for k in range(7):
                        tensor.matmul(pb[3][0:2, 0:512], w4b[:, 2 * k:2 * k + 2],
                                      yview[:, buf, k, :], start=(k == 0), stop=False)
                    TE.inc(tensor.matmul(pb[3][0:2, 0:512], w4b[:, 14:16],
                                         yview[:, buf, 7, :], start=False, stop=True))
                    te_pj[t] = TE.n

                @blk.scalar
                def _(scalar, t=t, buf=buf):
                    W(scalar, "scalar", TE, te_pj[t])
                    W(scalar, "scalar", SK, ld_kvr.get(t - 2, 0))
                    W(scalar, "scalar", VE, ve_lr.get(t - 2, 0))
                    o = 3 * 512 * buf
                    scalar.copy(out=kvr[:, o:o + 512], in_=pb[0][:, 0:512])
                    scalar.copy(out=kvr[:, o + 512:o + 1024], in_=pb[1][:, 0:512])
                    scalar.copy(out=kvr[:, o + 1024:o + 1536], in_=pb[2][:, 0:512])
                    SE.inc(scalar.activation(out=sgm[:, 512 * buf:512 * buf + 512],
                                             in_=pb[3][0:2, 0:512], func=AF.Sigmoid))
                    se_ev[t] = SE.n

                @blk.vector
                def _(vector, t=t, buf=buf):
                    W(vector, "vector", SE, se_ev[t])
                    W(vector, "vector", SPK, lg_pk.get(t - 2, 0))
                    VE.inc(vector.tensor_scalar(out=lrrow[:, 512 * buf:512 * buf + 512],
                                                in0=sgm[0:1, 512 * buf:512 * buf + 512],
                                                scalar1=LR_EFF, scalar2=None, op0=ALU.mult))
                    ve_lr[t] = VE.n

                @blk.sync
                def _(sync, t=t, buf=buf):
                    W(sync, "sync", SK, ld_kvr.get(t - 1, 0))
                    W(sync, "sync", SE, se_ev[t])
                    o = 3 * 512 * buf
                    SK.dma(sync.dma_start(out=kTd[:, 512 * t:512 * t + 512], in_=kvr[:, o:o + 512]))
                    SK.dma(sync.dma_start(out=vTd[:, 512 * t:512 * t + 512], in_=kvr[:, o + 512:o + 1024]))
                    SK.dma(sync.dma_start(out=rTd[:, 512 * t:512 * t + 512], in_=kvr[:, o + 1024:o + 1536]))
                    ld_kvr[t] = SK.n

                @blk.gpsimd
                def _(gpsimd, t=t, buf=buf):
                    W(gpsimd, "gpsimd", SPK, lg_pk.get(t - 1, 0))
                    W(gpsimd, "gpsimd", VE, ve_lr[t])
                    SPK.dma(gpsimd.dma_start(out=lrd[0:1, 512 * t:512 * t + 512],
                                            in_=lrrow[0:1, 512 * buf:512 * buf + 512]))
                    gv = sgm[1:2, 512 * buf:512 * buf + 512]
                    if t > 0:
                        SPK.dma(gpsimd.dma_start(out=gswd[4 * t - 1:4 * t, 65:128], in_=gv[:, 0:63]))
                    SPK.dma(gpsimd.dma_start(out=gswd[4 * t:4 * t + 3, :], in_=gv[:, 63:447]))
                    SPK.dma(gpsimd.dma_start(out=gswd[4 * t + 3:4 * t + 4, 0:65], in_=gv[:, 447:512]))
                    lg_pk[t] = SPK.n

        fence()

        if "cm_d" in DEBUG:
            @blk.gpsimd
            def _(gpsimd):
                LG.dma(gpsimd.dma_start(out=dbg["cm_d"][0], in_=cmtok[:, 0:1024]))
                LG.dma(gpsimd.dma_start(out=dbg["cm_d"][1], in_=cmtok[:, 1024:2048]))
                gpsimd.wait_ge(LG.h, LG.n)

        cmbf = E1(nc.sbuf_tensor("cmbf", [128, 1024], BF16))
        cmT = E1(nc.sbuf_tensor("cmT", [128, 8 * 256], BF16))
        mrow = E1(nc.sbuf_tensor("mrow", [1, 256], F32))
        drow = E1(nc.sbuf_tensor("drow", [1, 256], F32))

        for half in range(2):
            @blk.scalar
            def _(scalar, half=half):
                if half == 1:
                    W(scalar, "scalar", VE, VE.n)
                SE.inc(scalar.copy(out=cmbf[:], in_=cmtok[:, 1024 * half:1024 * half + 1024]))

            @blk.tensor
            def _(tensor, half=half):
                W(tensor, "tensor", SE, SE.n)
                if half == 1:
                    W(tensor, "tensor", VE, VE.n)
                for k in range(7):
                    tensor.transpose(pbt[half][:, 128 * k:128 * k + 128],
                                     cmbf[:, 128 * k:128 * k + 128], idb[:])
                TE.inc(tensor.transpose(pbt[half][:, 896:1024], cmbf[:, 896:1024], idb[:]))

            @blk.vector
            def _(vector, half=half):
                W(vector, "vector", TE, TE.n)
                dstv = cmT[:].rearrange("p (k c) -> p k c", k=8, c=256)[:, :, 128 * half:128 * half + 128]
                VE.inc(vector.tensor_copy(dstv, pbt[half][:].rearrange("p (k c) -> p k c", k=8, c=128)))

        @blk.tensor
        def _(tensor):
            W(tensor, "tensor", VE, VE.n)
            for k in range(8):
                tensor.matmul(pb[0][0:1, 0:256], wmdb[:, 2 * k:2 * k + 1],
                              cmT[:, 256 * k:256 * k + 256], start=(k == 0), stop=(k == 7))
            for k in range(7):
                tensor.matmul(pb[3][0:1, 0:256], wmdb[:, 2 * k + 1:2 * k + 2],
                              cmT[:, 256 * k:256 * k + 256], start=(k == 0), stop=False)
            TE.inc(tensor.matmul(pb[3][0:1, 0:256], wmdb[:, 15:16],
                                 cmT[:, 1792:2048], start=False, stop=True))

        @blk.scalar
        def _(scalar):
            W(scalar, "scalar", TE, TE.n)
            scalar.activation(out=mrow[:], in_=pb[0][0:1, 0:256], func=AF.Sigmoid)
            SE.inc(scalar.activation(out=drow[:], in_=pb[3][0:1, 0:256], func=AF.Sigmoid))

        @blk.vector
        def _(vector):
            W(vector, "vector", SE, SE.n)
            VE.inc(vector.tensor_scalar(out=drow[:], in0=drow[:], scalar1=-1.0,
                                        scalar2=1.0, op0=ALU.mult, op1=ALU.add))

        @blk.tensor
        def _(tensor):
            W(tensor, "tensor", VE, VE.n)
            tensor.matmul(pb[1][:, 0:256], onesf[:], mrow[:], start=True, stop=True)
            TE.inc(tensor.matmul(pb[2][:, 0:256], onesf[:], drow[:], start=True, stop=True))

        @blk.vector
        def _(vector):
            W(vector, "vector", TE, TE.n)
            vector.tensor_copy(momb[:], pb[1][:, 0:256])
            VE.inc(vector.tensor_copy(decb[:], pb[2][:, 0:256]))

        fence()
        p1.close()

        if "momb_d" in DEBUG:
            @blk.gpsimd
            def _(gpsimd):
                LG.dma(gpsimd.dma_start(out=dbg["momb_d"][:, :], in_=momb[:]))
                LG.dma(gpsimd.dma_start(out=dbg["decb_d"][:, :], in_=decb[:]))
                gpsimd.wait_ge(LG.h, LG.n)

        p2 = contextlib.ExitStack()
        E2 = p2.enter_context
        kp = E2(nc.sbuf_tensor("kp", [128, 8 * 128], BF16))
        vp = E2(nc.sbuf_tensor("vp", [128, 8 * 128], BF16))
        rp = E2(nc.sbuf_tensor("rp", [128, 8 * 128], BF16))
        grow2 = E2(nc.sbuf_tensor("grow2", [1, 8 * 128], F32))
        lrow2 = E2(nc.sbuf_tensor("lrow2", [1, 8 * 128], F32))
        a1fS = E2(nc.sbuf_tensor("a1fS", [128, 2 * 128], BF16))
        a1tS = E2(nc.sbuf_tensor("a1tS", [128, 2 * 128], BF16))
        sigS = E2(nc.sbuf_tensor("sigS", [128, 2 * 128], BF16))
        dpfS = E2(nc.sbuf_tensor("dpfS", [128, 2 * 128], BF16))
        dptS = E2(nc.sbuf_tensor("dptS", [128, 2 * 128], BF16))
        dzS = E2(nc.sbuf_tensor("dzS", [128, 2 * 128], BF16))
        ktS = E2(nc.sbuf_tensor("ktS", [128, 2 * 128], BF16))
        vtS = E2(nc.sbuf_tensor("vtS", [128, 2 * 128], BF16))
        tmpa = E2(nc.sbuf_tensor("tmpa", [128, 2 * 128], F32))
        tmpb = E2(nc.sbuf_tensor("tmpb", [128, 2 * 128], F32))
        w0cS = E2(nc.sbuf_tensor("w0cS", [128, 2 * 128], BF16))
        w1cS = E2(nc.sbuf_tensor("w1cS", [128, 2 * 128], BF16))
        s1S = E2(nc.sbuf_tensor("s1S", [128, 2 * 128], BF16))
        sqS = E2(nc.sbuf_tensor("sqS", [128, 2 * 128], BF16))
        r0S = E2(nc.sbuf_tensor("r0S", [1, 2 * 128], F32))
        r2S = E2(nc.sbuf_tensor("r2S", [1, 2 * 128], F32))
        valsS = E2(nc.sbuf_tensor("valsS", [128, 2 * 128], BF16))

        ld_pr = {}
        lg_gt = {}
        se_a1 = {}
        se_srow = {}
        se_s1r = {}
        se_sq = {}
        ve_dp = {}
        ve_dpf = {}
        ve_dz = {}
        ve_r2 = {}
        ve_vals = {}
        ve_scanw = {}
        te_z = {}
        te_m = {}
        te_dpt = {}
        te_da = {}
        te_g = {}
        te_r = {}
        te_ssq = {}
        te_sc = {}
        lg_v = {}

        def emit_kv_group(G):
            lane = G % 2
            off = 512 * (G % 2)

            @blk.sync
            def _(sync, G=G, lane=lane, off=off):
                if G >= 2:
                    W(sync, "sync", TE, te_r[(4 * G - 5, 1, 0)])
                KV[lane].dma(sync.dma_start(out=kp[:, off:off + 512], in_=kTd[:, 512 * G:512 * G + 512]))
                KV[lane].dma(sync.dma_start(out=vp[:, off:off + 512], in_=vTd[:, 512 * G:512 * G + 512]))
                KV[lane].dma(sync.dma_start(out=rp[:, off:off + 512], in_=rTd[:, 512 * G + 63:512 * G + 63 + 512]))
                for pp in range(4 * G, min(4 * G + 4, NPAIR)):
                    ld_pr[pp] = (lane, KV[lane].n)

            @blk.gpsimd
            def _(gpsimd, G=G, lane=lane, off=off):
                if G >= 2:
                    W(gpsimd, "gpsimd", VE, ve_r2.get(4 * G - 5, 0))
                RW[lane].dma(gpsimd.dma_start(out=grow2[0:1, off:off + 512], in_=gswd[4 * G:4 * G + 4, :]))
                RW[lane].dma(gpsimd.dma_start(out=lrow2[0:1, off:off + 512], in_=lrd[0:1, 512 * G:512 * G + 512]))
                for pp in range(4 * G, min(4 * G + 4, NPAIR)):
                    lg_gt[pp] = (lane, RW[lane].n)

        emit_kv_group(0)
        emit_kv_group(1)

        for p in range(NPAIR):
            par = p % 2
            o = 128 * par
            s_ = slice(o, o + 128)
            ko = 128 * (p % 8)
            ks_ = slice(ko, ko + 128)
            zb = pb[0] if par == 0 else pb[1]
            wb = pb[2] if par == 0 else pb[3]
            gb0, gb1 = pb[4], pb[5]

            if p % 4 == 0 and p >= 4 and (p // 4 + 1) * 4 < NPAIR + 4:
                G = p // 4 + 1
                if 4 * G < NPAIR:
                    emit_kv_group(G)

            @blk.tensor
            def _(tensor, p=p, ks_=ks_, zb=zb):
                lane, val = ld_pr[p]
                W(tensor, "tensor", KV[lane], val)
                W(tensor, "tensor", SE, se_srow.get(p - 2, 0))
                W(tensor, "tensor", VE, ve_vals.get(p - 2, 0))
                tensor.matmul(zb[:, 0:128], kp[:, ks_], w0b[:], start=True, stop=True)
                TE.inc(tensor.matmul(zb[:, 128:256], w0b[:], kp[:, ks_], start=True, stop=True))
                te_z[p] = TE.n

            @blk.scalar
            def _(scalar, p=p, s_=s_, zb=zb):
                W(scalar, "scalar", TE, te_z[p])
                scalar.activation(out=a1fS[:, s_], in_=zb[:, 128:256], func=AF.Silu)
                scalar.activation(out=a1tS[:, s_], in_=zb[:, 0:128], func=AF.Silu)
                SE.inc(scalar.activation(out=sigS[:, s_], in_=zb[:, 0:128], func=AF.Sigmoid))
                se_a1[p] = SE.n

            @blk.tensor
            def _(tensor, p=p, s_=s_, ks_=ks_, ko=ko, zb=zb, wb=wb, par=par):
                W(tensor, "tensor", SE, se_a1[p])
                W(tensor, "tensor", VE, ve_dpf.get(p - 2, 0))
                lane, val = lg_gt[p]
                W(tensor, "tensor", RW[lane], val)
                W(tensor, "tensor", VE, ve_vals.get(p - 2, 0))
                tensor.matmul(zb[:, 384:512], a1fS[:, s_], w1b[:], start=True, stop=True)
                tensor.matmul(wb[:, 128:129], lrow2[0:1, ko:ko + 128], onec1f[:], start=True, stop=True)
                tensor.transpose(pbt[par][:, 0:128], kp[:, ks_], idb[:])
                TE.inc(tensor.transpose(pbt[par][:, 128:256], vp[:, ks_], idb[:]))
                te_m[p] = TE.n

            @blk.vector
            def _(vector, p=p, s_=s_, zb=zb, wb=wb, par=par):
                W(vector, "vector", TE, te_m[p])
                vector.tensor_copy(ktS[:, s_], pbt[par][:, 0:128])
                vector.tensor_copy(vtS[:, s_], pbt[par][:, 128:256])
                vector.drain()
                vector.tensor_tensor(out=tmpb[:, s_], in0=zb[:, 384:512], in1=vtS[:, s_], op=ALU.subtract)
                vector.drain()
                VE.inc(vector.tensor_scalar(out=dptS[:, s_], in0=tmpb[:, s_], scalar1=wb[:, 128:129],
                                            scalar2=None, op0=ALU.mult))
                ve_dp[p] = VE.n

            @blk.tensor
            def _(tensor, p=p, s_=s_, par=par):
                W(tensor, "tensor", VE, ve_dp[p])
                TE.inc(tensor.transpose(pbt[par][:, 256:384], dptS[:, s_], idb[:]))
                te_dpt[p] = TE.n

            @blk.vector
            def _(vector, p=p, s_=s_, par=par):
                W(vector, "vector", TE, te_dpt[p])
                VE.inc(vector.tensor_copy(dpfS[:, s_], pbt[par][:, 256:384]))
                ve_dpf[p] = VE.n

            @blk.tensor
            def _(tensor, p=p, s_=s_, wb=wb):
                W(tensor, "tensor", VE, ve_dpf[p])
                W(tensor, "tensor", VE, ve_vals.get(p - 2, 0))
                TE.inc(tensor.matmul(wb[:, 256:384], dpfS[:, s_], w1Tb[:], start=True, stop=True))
                te_da[p] = TE.n

            @blk.vector
            def _(vector, p=p, s_=s_, wb=wb):
                W(vector, "vector", TE, te_da[p])
                vector.scalar_tensor_tensor(out=tmpa[:, s_], in0=a1tS[:, s_], scalar=-1.0,
                                            in1=sigS[:, s_], op0=ALU.mult, op1=ALU.mult)
                vector.tensor_tensor(out=tmpb[:, s_], in0=a1tS[:, s_], in1=sigS[:, s_], op=ALU.add)
                vector.drain()
                vector.tensor_tensor(out=tmpa[:, s_], in0=tmpa[:, s_], in1=tmpb[:, s_], op=ALU.add)
                vector.drain()
                VE.inc(vector.tensor_tensor(out=dzS[:, s_], in0=wb[:, 256:384], in1=tmpa[:, s_], op=ALU.mult))
                ve_dz[p] = VE.n

            @blk.tensor
            def _(tensor, p=p, s_=s_):
                W(tensor, "tensor", VE, ve_dz[p])
                W(tensor, "tensor", VE, ve_vals.get(p - 1, 0))
                tensor.matmul(gb0[:, 0:128], a1tS[0:64, s_], dptS[0:64, s_], start=True, stop=True)
                tensor.matmul(gb0[:, 128:256], ktS[0:64, s_], dzS[0:64, s_], start=True, stop=True)
                tensor.matmul(gb1[:, 0:128], a1tS[64:128, s_], dptS[64:128, s_], start=True, stop=True)
                TE.inc(tensor.matmul(gb1[:, 128:256], ktS[64:128, s_], dzS[64:128, s_], start=True, stop=True))
                te_g[p] = TE.n

            for ci in range(2):
                c = 2 * p + ci
                gbank = gb0 if ci == 0 else gb1

                @blk.vector
                def _(vector, p=p, c=c, ci=ci, gbank=gbank):
                    if ci == 0:
                        W(vector, "vector", TE, te_g[p])
                    vector.drain()
                    vector.scalar_tensor_tensor(out=M1[:], in0=M1[:], scalar=momb[:, c:c + 1],
                                                in1=gbank[:, 0:128], op0=ALU.mult, op1=ALU.subtract)
                    vector.scalar_tensor_tensor(out=M0[:], in0=M0[:], scalar=momb[:, c:c + 1],
                                                in1=gbank[:, 128:256], op0=ALU.mult, op1=ALU.subtract)
                    vector.drain()
                    vector.scalar_tensor_tensor(out=U1[:], in0=U1[:], scalar=decb[:, c:c + 1],
                                                in1=M1[:], op0=ALU.mult, op1=ALU.add)
                    vector.scalar_tensor_tensor(out=U0[:], in0=U0[:], scalar=decb[:, c:c + 1],
                                                in1=M0[:], op0=ALU.mult, op1=ALU.add)
                    vector.drain()
                    vector.tensor_tensor(out=w0cS[:, 128 * ci:128 * ci + 128], in0=U0[:], in1=mw0f[:], op=ALU.add)
                    VE.inc(vector.tensor_tensor(out=w1cS[:, 128 * ci:128 * ci + 128], in0=U1[:], in1=mw1f[:], op=ALU.add))
                    ve_scanw[(p, ci)] = VE.n

                @blk.tensor
                def _(tensor, p=p, ci=ci, ko=ko, wb=wb):
                    W(tensor, "tensor", VE, ve_scanw[(p, ci)])
                    TE.inc(tensor.matmul(wb[:, 384 + 64 * ci:384 + 64 * ci + 64],
                                         w0cS[:, 128 * ci:128 * ci + 128], rp[:, ko + 64 * ci:ko + 64 * ci + 64],
                                         start=True, stop=True))
                    te_r[(p, ci, 0)] = TE.n

            @blk.scalar
            def _(scalar, p=p, o=o, wb=wb):
                for ci in range(2):
                    W(scalar, "scalar", TE, te_r[(p, ci, 0)])
                    SE.inc(scalar.activation(out=s1S[:, o + 64 * ci:o + 64 * ci + 64],
                                             in_=wb[:, 384 + 64 * ci:384 + 64 * ci + 64], func=AF.Silu))
                    se_s1r[(p, ci)] = SE.n

            for ci in range(2):
                gbank = gb0 if ci == 0 else gb1

                @blk.tensor
                def _(tensor, p=p, ci=ci, o=o, gbank=gbank):
                    W(tensor, "tensor", SE, se_s1r[(p, ci)])
                    TE.inc(tensor.matmul(gbank[:, 256 + 64 * ci:256 + 64 * ci + 64],
                                         w1cS[:, 128 * ci:128 * ci + 128],
                                         s1S[:, o + 64 * ci:o + 64 * ci + 64], start=True, stop=True))
                    te_r[(p, ci, 1)] = TE.n

            @blk.vector
            def _(vector, p=p, s_=s_, o=o):
                W(vector, "vector", TE, te_r[(p, 1, 1)])
                vector.tensor_copy(tmpb[:, o:o + 64], gb0[:, 256:320])
                vector.tensor_copy(tmpb[:, o + 64:o + 128], gb1[:, 320:384])
                vector.drain()
                VE.inc(vector.tensor_tensor(out=s1S[:, s_], in0=tmpb[:, s_],
                                            in1=tmpb[:, s_], op=ALU.mult))
                se_sq[p] = VE.n

            @blk.tensor
            def _(tensor, p=p, s_=s_, zb=zb):
                W(tensor, "tensor", VE, se_sq[p])
                TE.inc(tensor.matmul(zb[0:1, 0:128], onescol[:], s1S[:, s_], start=True, stop=True))
                te_ssq[p] = TE.n

            @blk.scalar
            def _(scalar, p=p, par=par, zb=zb):
                W(scalar, "scalar", TE, te_ssq[p])
                SE.inc(scalar.activation(out=r0S[0:1, 128 * par:128 * par + 128], in_=zb[0:1, 0:128],
                                         func=AF.Sqrt, scale=1.0 / DH, bias=cv[0:1, 1:2]))
                se_srow[p] = SE.n

            @blk.vector
            def _(vector, p=p, par=par, ko=ko):
                W(vector, "vector", SE, se_srow[p])
                lane, val = lg_gt[p]
                W(vector, "vector", RW[lane], val)
                vector.reciprocal(r2S[0:1, 128 * par:128 * par + 128], r0S[0:1, 128 * par:128 * par + 128])
                vector.drain()
                VE.inc(vector.tensor_tensor(out=r2S[0:1, 128 * par:128 * par + 128],
                                            in0=r2S[0:1, 128 * par:128 * par + 128],
                                            in1=grow2[0:1, ko:ko + 128], op=ALU.mult))
                ve_r2[p] = VE.n

            @blk.tensor
            def _(tensor, p=p, par=par, wb=wb):
                W(tensor, "tensor", VE, ve_r2[p])
                TE.inc(tensor.matmul(wb[:, 384:512], gp1row[:], r2S[0:1, 128 * par:128 * par + 128],
                                     start=True, stop=True))
                te_sc[p] = TE.n

            @blk.vector
            def _(vector, p=p, s_=s_, o=o, wb=wb):
                W(vector, "vector", TE, te_sc[p])
                if p >= 2:
                    W(vector, "vector", SA[p % 2], lg_v.get(p - 2, 0))
                VE.inc(vector.tensor_tensor(out=valsS[:, s_], in0=tmpb[:, s_], in1=wb[:, 384:512], op=ALU.mult))
                ve_vals[p] = VE.n

            @blk.gpsimd
            def _(gpsimd, p=p, s_=s_):
                W(gpsimd, "gpsimd", VE, ve_vals[p])
                SA[p % 2].dma(gpsimd.dma_start(out=agin[:, 128 * p:128 * p + 128], in_=valsS[:, s_]))
                lg_v[p] = SA[p % 2].n

        @blk.gpsimd
        def _(gpsimd):
            W(gpsimd, "gpsimd", SA[0], SA[0].n)
            W(gpsimd, "gpsimd", SA[1], SA[1].n)
            W(gpsimd, "gpsimd", LG, LG.n)
            CC.inc(gpsimd.collective_compute(
                "AllGather", ALU.bypass,
                replica_groups=[list(range(8))],
                ins=[agin_t.ap()],
                outs=[agout_t.ap()],
            ))

        if "agin_d" in DEBUG:
            @blk.gpsimd
            def _(gpsimd):
                LG.dma(gpsimd.dma_start(out=dbg["agin_d"][:, :], in_=agin[:, :]))
                gpsimd.wait_ge(LG.h, LG.n)

        fence()
        p2.close()

        p3 = contextlib.ExitStack()
        E3 = p3.enter_context
        xst3 = E3(nc.sbuf_tensor("xst3", [128, 2 * 1024], F32))
        xsq3 = E3(nc.sbuf_tensor("xsq3", [128, 2 * 1024], F32))
        xntok = E3(nc.sbuf_tensor("xntok", [128, 2 * 1024], BF16))
        st3 = E3(nc.sbuf_tensor("st3", [128, 2 * 16], F32))
        mx8 = E3(nc.sbuf_tensor("mx8", [128, 8], F32))
        gtmp = E3(nc.sbuf_tensor("gtmp", [128, 32], F32))
        gatesS = E3(nc.sbuf_tensor("gatesS", [128, 8], F32))
        gst3 = E3(nc.sbuf_tensor("gst3", [8, 2 * 128], BF16))
        wg32 = E3(nc.sbuf_tensor("wg32", [128, 64], F32))
        xnT32 = E3(nc.sbuf_tensor("xnT32", [128, 2 * 1024], F32))
        xn32 = E3(nc.sbuf_tensor("xn32", [128, 2 * 1024], F32))

        ld3 = {}
        se3 = {}
        ve3 = {}
        te3 = {}

        @blk.sync
        def _(sync):
            sync.wait_ge(LD.h, LD.n)
            for k in range(8):
                LD.dma(sync.dma_start(out=wg32[:, 8 * k:8 * k + 8], in_=wg_in[128 * k:128 * k + 128, :]))
            ld3["wg"] = LD.n

        for tc in range(16):
            par = tc % 2
            so = 16 * par
            xv = xst3[:, 1024 * par:1024 * par + 1024]
            xqv = xsq3[:, 1024 * par:1024 * par + 1024]
            yv = xntok[:, 1024 * par:1024 * par + 1024]

            @blk.sync
            def _(sync, tc=tc, par=par, xv=xv):
                W(sync, "sync", VE, se3.get((tc - 2, "y"), 0))
                LX[par].dma(sync.dma_start(out=xv, in_=x_slice[128 * tc:128 * tc + 128, :]))
                ld3[tc] = LX[par].n

            @blk.vector
            def _(vector, tc=tc, par=par, xv=xv, xqv=xqv, so=so):
                W(vector, "vector", LX[par], ld3[tc])
                W(vector, "vector", SE, se3.get((tc - 2, "sqrt"), 0))
                vector.reduce_sum(st3[:, so + 0:so + 1], xv, axis=AX.X)
                vector.tensor_tensor(out=xqv, in0=xv, in1=xv, op=ALU.mult)
                vector.drain()
                vector.reduce_sum(st3[:, so + 1:so + 2], xqv, axis=AX.X)
                vector.tensor_scalar(out=st3[:, so + 2:so + 3], in0=st3[:, so + 0:so + 1],
                                     scalar1=1.0 / D, scalar2=None, op0=ALU.mult)
                vector.drain()
                vector.tensor_scalar(out=st3[:, so + 3:so + 4], in0=st3[:, so + 1:so + 2],
                                     scalar1=1.0 / D, scalar2=None, op0=ALU.mult)
                vector.drain()
                VE.inc(vector.scalar_tensor_tensor(out=st3[:, so + 4:so + 5], in0=st3[:, so + 2:so + 3],
                                                   scalar=st3[:, so + 2:so + 3], in1=st3[:, so + 3:so + 4],
                                                   op0=ALU.mult, op1=ALU.subtract))
                ve3[(tc, "var")] = VE.n

            @blk.scalar
            def _(scalar, tc=tc, so=so):
                W(scalar, "scalar", VE, ve3[(tc, "var")])
                SE.inc(scalar.activation(out=st3[:, so + 5:so + 6], in_=st3[:, so + 4:so + 5],
                                         func=AF.Sqrt, scale=-1.0, bias=cv[:, 0:1]))
                se3[(tc, "sqrt")] = SE.n

            @blk.vector
            def _(vector, tc=tc, so=so):
                W(vector, "vector", SE, SE.n)
                vector.reciprocal(st3[:, so + 6:so + 7], st3[:, so + 5:so + 6])
                vector.drain()
                VE.inc(vector.scalar_tensor_tensor(out=st3[:, so + 7:so + 8], in0=st3[:, so + 2:so + 3],
                                                   scalar=-1.0, in1=st3[:, so + 6:so + 7],
                                                   op0=ALU.mult, op1=ALU.mult))
                ve3[(tc, "st")] = VE.n

            @blk.vector
            def _(vector, tc=tc, par=par, xv=xv, yv=yv, so=so):
                W(vector, "vector", TE, te3.get((tc - 2, "tr"), 0))
                vector.drain()
                vector.tensor_scalar(out=xn32[:, 1024 * par:1024 * par + 1024], in0=xv,
                                     scalar1=st3[:, so + 6:so + 7], scalar2=st3[:, so + 7:so + 8],
                                     op0=ALU.mult, op1=ALU.add)
                VE.inc(vector.tensor_scalar(out=yv, in0=xv, scalar1=st3[:, so + 6:so + 7],
                                            scalar2=st3[:, so + 7:so + 8], op0=ALU.mult, op1=ALU.add))
                se3[(tc, "y")] = VE.n

            for half in range(2):
                @blk.tensor
                def _(tensor, tc=tc, par=par, half=half):
                    if half == 0:
                        W(tensor, "tensor", VE, se3[(tc, "y")])
                    W(tensor, "tensor", VE, ve3.get((tc - 2, "ev", half), 0))
                    xsrc = xn32[:, 1024 * par:1024 * par + 1024]
                    hb5 = pb[2] if half == 0 else pb[3]
                    for k in range(3):
                        tensor.transpose(hb5[:, 128 * k:128 * k + 128],
                                         xsrc[:, 512 * half + 128 * k:512 * half + 128 * k + 128], idf[:])
                    TE.inc(tensor.transpose(hb5[:, 384:512], xsrc[:, 512 * half + 384:512 * half + 512], idf[:]))
                    te3[(tc, "tr", half)] = TE.n

                @blk.vector
                def _(vector, tc=tc, par=par, half=half):
                    W(vector, "vector", TE, te3[(tc, "tr", half)])
                    hb5 = pb[2] if half == 0 else pb[3]
                    dstv = xnT[:].rearrange("p (k w) -> p k w", k=8, w=WTOK)[:, 4 * half:4 * half + 4, 128 * tc:128 * tc + 128]
                    vector.tensor_copy(dstv, hb5[:].rearrange("p (k c) -> p k c", k=4, c=128))
                    dst32 = xnT32[:].rearrange("p (b k c) -> p b k c", b=2, k=4, c=128)[:, par, :, :] \
                        if False else xnT32[:, 1024 * par + 512 * half:1024 * par + 512 * half + 512]
                    VE.inc(vector.tensor_copy(dst32, hb5[:, 0:512]))
                    ve3[(tc, "ev", half)] = VE.n
            te3[(tc, "tr")] = te3[(tc, "tr", 1)]
            ve3[(tc, "ev")] = ve3[(tc, "ev", 1)]

            @blk.tensor
            def _(tensor, tc=tc, par=par):
                W(tensor, "tensor", LD, ld3["wg"])
                W(tensor, "tensor", VE, ve3[(tc, "ev")])
                W(tensor, "tensor", VE, ve3.get((tc - 1, "g"), 0))
                for k in range(7):
                    tensor.matmul(pb[0][:, 0:8], xnT32[:, 1024 * par + 128 * k:1024 * par + 128 * k + 128],
                                  wg32[:, 8 * k:8 * k + 8], start=(k == 0), stop=False)
                TE.inc(tensor.matmul(pb[0][:, 0:8], xnT32[:, 1024 * par + 896:1024 * par + 1024],
                                     wg32[:, 56:64], start=False, stop=True))
                te3[(tc, "lg")] = TE.n

            @blk.vector
            def _(vector, tc=tc):
                W(vector, "vector", TE, te3[(tc, "lg")])
                VE.inc(vector.max(mx8[:], pb[0][:, 0:8]))
                ve3[(tc, "mx")] = VE.n

            @blk.vector
            def _(vector, tc=tc):
                vector.drain()
                VE.inc(vector.tensor_tensor(out=gtmp[:, 10:11], in0=mx8[:, 0:1], in1=mx8[:, 1:2], op=ALU.subtract))
                ve3[(tc, "d")] = VE.n

            @blk.scalar
            def _(scalar, tc=tc):
                W(scalar, "scalar", VE, ve3[(tc, "d")])
                SE.inc(scalar.activation(out=gtmp[:, 11:12], in_=gtmp[:, 10:11], func=AF.Sigmoid))
                se3[(tc, "w1")] = SE.n

            @blk.vector
            def _(vector, tc=tc):
                W(vector, "vector", SE, se3[(tc, "w1")])
                vector.tensor_scalar(out=gtmp[:, 12:13], in0=gtmp[:, 11:12], scalar1=-1.0, scalar2=1.0,
                                     op0=ALU.mult, op1=ALU.add)
                vector.tensor_scalar(out=gtmp[:, 16:24], in0=pb[0][:, 0:8], scalar1=mx8[:, 0:1],
                                     scalar2=None, op0=ALU.is_equal)
                vector.tensor_scalar(out=gtmp[:, 24:32], in0=pb[0][:, 0:8], scalar1=mx8[:, 1:2],
                                     scalar2=None, op0=ALU.is_equal)
                vector.drain()
                vector.tensor_scalar(out=gtmp[:, 16:24], in0=gtmp[:, 16:24], scalar1=gtmp[:, 11:12],
                                     scalar2=None, op0=ALU.mult)
                vector.tensor_scalar(out=gtmp[:, 24:32], in0=gtmp[:, 24:32], scalar1=gtmp[:, 12:13],
                                     scalar2=None, op0=ALU.mult)
                vector.drain()
                VE.inc(vector.tensor_tensor(out=gatesS[:], in0=gtmp[:, 16:24], in1=gtmp[:, 24:32], op=ALU.add))
                ve3[(tc, "g")] = VE.n

            @blk.tensor
            def _(tensor, tc=tc):
                W(tensor, "tensor", VE, ve3[(tc, "g")])
                TE.inc(tensor.transpose(pb[1][0:8, 0:128], gatesS[:], idf[:]))
                te3[(tc, "gt")] = TE.n

            @blk.vector
            def _(vector, tc=tc, par=par):
                W(vector, "vector", TE, te3[(tc, "gt")])
                W(vector, "vector", SG3, ve3.get((tc - 2, "grdma"), 0))
                VE.inc(vector.tensor_copy(gst3[:, 128 * par:128 * par + 128], pb[1][0:8, 0:128]))
                ve3[(tc, "gr")] = VE.n

            @blk.gpsimd
            def _(gpsimd, tc=tc, par=par):
                W(gpsimd, "gpsimd", SG3, ve3.get((tc - 1, "grdma"), 0))
                W(gpsimd, "gpsimd", VE, ve3[(tc, "gr")])
                SG3.dma(gpsimd.dma_start(out=gaterd[:, 128 * tc:128 * tc + 128],
                                        in_=gst3[0:8, 128 * par:128 * par + 128]))
                ve3[(tc, "grdma")] = SG3.n

        fence()
        p3.close()

        if "xnT_d" in DEBUG:
            @blk.gpsimd
            def _(gpsimd):
                for k in range(8):
                    LG.dma(gpsimd.dma_start(out=dbg["xnT_d"][k], in_=xnT[:, WTOK * k:WTOK * k + WTOK]))
                gpsimd.wait_ge(LG.h, LG.n)

        p4 = contextlib.ExitStack()
        E4 = p4.enter_context
        wst4 = E4(nc.sbuf_tensor("wst4", [128, 2 * 1024], F32))
        w1q = E4(nc.sbuf_tensor("w1q", [128, 2 * 8 * 1024], BF16))
        w2q = E4(nc.sbuf_tensor("w2q", [128, 2 * 8 * 1024], BF16))
        hbuf = E4(nc.sbuf_tensor("hbuf", [128, 2 * 8 * 512], BF16))
        gbc = E4(nc.sbuf_tensor("gbc", [128, WTOK], BF16))
        grow4 = E4(nc.sbuf_tensor("grow4", [1, WTOK], BF16))

        ld4 = {}
        cv4 = {}
        te4 = {}
        ve4 = {}
        se4 = {}
        h_ve = []
        acc_ve = []
        l2_te = []
        gmul_last = {}

        l2done = {}

        xnv = xnT[:].rearrange("p (k w) -> p k w", k=8, w=WTOK)

        def stage_step(eq, kk):
            if eq >= 32:
                return
            e, q = divmod(eq, 4)
            wo = 8192 * (eq % 2)
            hh = kk % 2

            def _dma(eng, ename):
                gk = 16 * eq + kk
                if gk >= 2:
                    peq, pkk = divmod(gk - 2, 16)
                    W(eng, ename, VE, cv4.get((peq, pkk), 0))
                if kk < 8:
                    src = w1_in[e, 128 * kk:128 * kk + 128, 1024 * q:1024 * q + 1024]
                else:
                    c2 = kk - 8
                    src = w2_in[e, 1024 * q + 128 * c2:1024 * q + 128 * c2 + 128, 0:1024]
                LW[hh].dma(eng.dma_start(out=wst4[:, 1024 * hh:1024 * hh + 1024], in_=src))
                ld4[(eq, kk)] = LW[hh].n

            if hh == 0:
                @blk.sync
                def _(sync):
                    _dma(sync, "sync")
            else:
                @blk.gpsimd
                def _(gpsimd):
                    _dma(gpsimd, "gpsimd")

            @blk.vector
            def _(vector):
                W(vector, "vector", LW[hh], ld4[(eq, kk)])
                if kk == 0:
                    W(vector, "vector", TE, te4.get((eq - 2, "done"), 0))
                dst = w1q if kk < 8 else w2q
                ki = kk % 8
                VE.inc(vector.tensor_copy(dst[:, wo + 1024 * ki:wo + 1024 * ki + 1024],
                                          wst4[:, 1024 * hh:1024 * hh + 1024]))
                cv4[(eq, kk)] = VE.n

        def emit_gbc(eq, e):
            @blk.gpsimd
            def _(gpsimd):
                W(gpsimd, "gpsimd", SG4, ld4.get((eq - 4, "grow"), 0))
                if e > 0:
                    W(gpsimd, "gpsimd", TE, te4.get((4 * e - 1, "gbcmm"), 0))
                SG4.dma(gpsimd.dma_start(out=grow4[:], in_=gaterd[e:e + 1, :]))
                ld4[(eq, "grow")] = SG4.n

            for tt in range(4):
                @blk.tensor
                def _(tensor, tt=tt):
                    W(tensor, "tensor", SG4, ld4[(eq, "grow")])
                    if e > 0:
                        W(tensor, "tensor", VE, gmul_last.get(e - 1, 0))
                    W(tensor, "tensor", VE, ve4.get((eq, "gbc", tt - 1), 0))
                    if h_ve:
                        W(tensor, "tensor", VE, h_ve[-1])
                    TE.inc(tensor.matmul(pb[4][:, 0:512], ones1[:],
                                         grow4[0:1, 512 * tt:512 * tt + 512], start=True, stop=True))
                    te4[(eq, "gbc", tt)] = TE.n
                    te4[(eq + 3, "gbcmm")] = TE.n

                @blk.vector
                def _(vector, tt=tt):
                    W(vector, "vector", TE, te4[(eq, "gbc", tt)])
                    VE.inc(vector.tensor_copy(gbc[:, 512 * tt:512 * tt + 512], pb[4][:, 0:512]))
                    ve4[(eq, "gbc", tt)] = VE.n
            ve4[(eq, "gbc")] = VE.n

        L1BANKS = [pb[0], pb[1], pb[5], pb[4]]

        def emit_L1(eq, tt):
            e = eq // 4
            wo = 8192 * (eq % 2)
            ho = 4096 * (tt % 2)
            for grp in range(2):
                ms = range(4 * grp, 4 * grp + 4)
                for m in ms:
                    gm = 32 * eq + 8 * tt + m
                    hb = L1BANKS[gm % 4]

                    @blk.tensor
                    def _(tensor, m=m, hb=hb, gm=gm):
                        if m == 0 and tt == 0:
                            W(tensor, "tensor", VE, cv4[(eq, 7)])
                            W(tensor, "tensor", VE, ve4[(4 * (eq // 4), "gbc")])
                        if gm >= 4:
                            W(tensor, "tensor", VE, h_ve[gm - 4])
                        for k in range(7):
                            tensor.matmul(hb[:, 0:512],
                                          w1q[:, wo + 1024 * k + 128 * m:wo + 1024 * k + 128 * m + 128],
                                          xnv[:, k, 512 * tt:512 * tt + 512], start=(k == 0), stop=False)
                        TE.inc(tensor.matmul(hb[:, 0:512],
                                             w1q[:, wo + 7 * 1024 + 128 * m:wo + 7 * 1024 + 128 * m + 128],
                                             xnv[:, 7, 512 * tt:512 * tt + 512], start=False, stop=True))
                        te4[(eq, tt, m)] = TE.n

                @blk.scalar
                def _(scalar, ms=ms):
                    for m in ms:
                        gm = 32 * eq + 8 * tt + m
                        W(scalar, "scalar", TE, te4[(eq, tt, m)])
                        SE.inc(scalar.activation(out=L1BANKS[gm % 4][:, 0:512],
                                                 in_=L1BANKS[gm % 4][:, 0:512], func=AF.Silu))
                        se4[(eq, tt, m)] = SE.n

                @blk.vector
                def _(vector, ms=ms, ho=ho):
                    for m in ms:
                        gm = 32 * eq + 8 * tt + m
                        W(vector, "vector", SE, se4[(eq, tt, m)])
                        if m == 0:
                            prev = (eq, tt - 2) if tt >= 2 else (eq - 1, tt + 2)
                            W(vector, "vector", TE, l2done.get(prev, 0))
                        VE.inc(vector.tensor_tensor(out=hbuf[:, ho + 512 * m:ho + 512 * m + 512],
                                                    in0=L1BANKS[gm % 4][:, 0:512],
                                                    in1=gbc[:, 512 * tt:512 * tt + 512],
                                                    op=ALU.mult))
                        ve4[(eq, tt, m)] = VE.n
                        h_ve.append(VE.n)
                        if m == 7:
                            gmul_last[e] = VE.n

        def emit_L2(eq, tt):
            wo = 8192 * (eq % 2)
            ho = 4096 * (tt % 2)
            for oc in range(8):
                go = 32 * eq + 8 * tt + oc
                ob = pb[2] if go % 2 == 0 else pb[3]

                @blk.tensor
                def _(tensor, oc=oc, ob=ob, go=go):
                    if tt == 0 and oc == 0:
                        W(tensor, "tensor", VE, cv4[(eq, 15)])
                    W(tensor, "tensor", VE, ve4[(eq, tt, 7)])
                    if go >= 2:
                        W(tensor, "tensor", VE, acc_ve[go - 2])
                    for m in range(7):
                        tensor.matmul(ob[:, 0:512],
                                      w2q[:, wo + 1024 * m + 128 * oc:wo + 1024 * m + 128 * oc + 128],
                                      hbuf[:, ho + 512 * m:ho + 512 * m + 512], start=(m == 0), stop=False)
                    TE.inc(tensor.matmul(ob[:, 0:512],
                                         w2q[:, wo + 7 * 1024 + 128 * oc:wo + 7 * 1024 + 128 * oc + 128],
                                         hbuf[:, ho + 512 * 7:ho + 512 * 7 + 512], start=False, stop=True))
                    te4[(eq, tt, "l2", oc)] = TE.n

                @blk.vector
                def _(vector, oc=oc, ob=ob):
                    W(vector, "vector", TE, te4[(eq, tt, "l2", oc)])
                    VE.inc(vector.tensor_tensor(
                        out=gacc[:, WTOK * oc + 512 * tt:WTOK * oc + 512 * tt + 512],
                        in0=gacc[:, WTOK * oc + 512 * tt:WTOK * oc + 512 * tt + 512],
                        in1=ob[:, 0:512], op=ALU.add))
                    acc_ve.append(VE.n)
            l2done[(eq, tt)] = TE.n
            if tt == 3:
                te4[(eq, "done")] = TE.n

        for kk in range(16):
            stage_step(0, kk)

        SEQ = [("L1", 0), ("L1", 1), ("L2", 0), ("L1", 2), ("L2", 1), ("L1", 3), ("L2", 2), ("L2", 3)]
        for eq in range(32):
            e, q = divmod(eq, 4)
            if q == 0:
                emit_gbc(eq, e)
            for bi, (kind, tt) in enumerate(SEQ):
                if kind == "L1":
                    emit_L1(eq, tt)
                else:
                    emit_L2(eq, tt)
                stage_step(eq + 1, 2 * bi)
                stage_step(eq + 1, 2 * bi + 1)

        fence()
        p4.close()

        if "gacc_d" in DEBUG:
            @blk.gpsimd
            def _(gpsimd):
                for oc in range(8):
                    LG.dma(gpsimd.dma_start(out=dbg["gacc_d"][oc], in_=gacc[:, WTOK * oc:WTOK * oc + WTOK]))
                gpsimd.wait_ge(LG.h, LG.n)

        p5 = contextlib.ExitStack()
        E5 = p5.enter_context
        wcst = E5(nc.sbuf_tensor("wcst", [128, 1024], F32))
        wcb = E5(nc.sbuf_tensor("wcb", [128, 8 * 1024], BF16))
        vw = E5(nc.sbuf_tensor("vw", [128, 8 * 2176], BF16))
        sgt = E5(nc.sbuf_tensor("sgt", [128, 2 * 512], F32))
        gsl = E5(nc.sbuf_tensor("gsl", [128, 2 * 512], F32))
        otok = E5(nc.sbuf_tensor("otok", [128, 4 * 1024], F32))
        xsl5 = E5(nc.sbuf_tensor("xsl5", [128, 2 * 1024], F32))

        ld5 = {}
        se5 = {}
        ve5 = {}
        te5 = {}
        lg5 = {}
        so5 = {}

        for k in range(8):
            @blk.sync
            def _(sync, k=k):
                if k == 0:
                    sync.wait_ge(LD.h, LD.n)
                W(sync, "sync", SE, se5.get(("wc", k - 1), 0))
                LD.dma(sync.dma_start(out=wcst[:], in_=wc_in[128 * k:128 * k + 128, :]))
                ld5[("wc", k)] = LD.n

            @blk.scalar
            def _(scalar, k=k):
                W(scalar, "scalar", LD, ld5[("wc", k)])
                SE.inc(scalar.copy(out=wcb[:, 1024 * k:1024 * k + 1024], in_=wcst[:]))
                se5[("wc", k)] = SE.n

        @blk.gpsimd
        def _(gpsimd):
            W(gpsimd, "gpsimd", CC, CC.n)
            mainview = agout.rearrange("h p (j t) -> (h p j) t", t=2048)
            bndview = agout.rearrange("h p (r t) -> (h p r) t", t=128)
            for h in range(8):
                SV5.dma(gpsimd.indirect_dma_start(
                    out=vw[:, 2176 * h + 128:2176 * h + 2176], out_offset=None,
                    in_=mainview, in_offset=bass.IndirectOffsetOnAxis(ap=widxS[:, h:h + 1], axis=0)))
                SV5.dma(gpsimd.indirect_dma_start(
                    out=vw[:, 2176 * h:2176 * h + 128], out_offset=None,
                    in_=bndview, in_offset=bass.IndirectOffsetOnAxis(ap=widxS[:, 8 + h:8 + h + 1], axis=0)))
            lg5["vw"] = SV5.n

        @blk.vector
        def _(vector):
            W(vector, "vector", SV5, lg5["vw"])
            for h in range(7):
                vector.tensor_scalar(out=vw[:, 2176 * h:2176 * h + 128], in0=vw[:, 2176 * h:2176 * h + 128],
                                     scalar1=bscaleS[:, 0:1], scalar2=None, op0=ALU.mult)
            VE.inc(vector.tensor_scalar(out=vw[:, 2176 * 7:2176 * 7 + 128], in0=vw[:, 2176 * 7:2176 * 7 + 128],
                                        scalar1=bscaleS[:, 0:1], scalar2=None, op0=ALU.mult))
            ve5["vw"] = VE.n

        sg_hist = []
        ot_hist = []
        for t in range(4):
            obuf = 0

            for oc in range(8):
                sb = pb[0] if oc % 2 == 0 else pb[1]
                spar = oc % 2
                gidx5 = 8 * t + oc

                @blk.tensor
                def _(tensor, t=t, oc=oc, sb=sb, gidx5=gidx5):
                    if t == 0 and oc == 0:
                        W(tensor, "tensor", SE, se5[("wc", 7)])
                        W(tensor, "tensor", VE, ve5["vw"])
                    if gidx5 >= 2:
                        W(tensor, "tensor", VE, sg_hist[gidx5 - 2])
                    for h in range(7):
                        tensor.matmul(sb[:, 0:512], wcb[:, 1024 * h + 128 * oc:1024 * h + 128 * oc + 128],
                                      vw[:, 2176 * h + 65 + 512 * t:2176 * h + 65 + 512 * t + 512],
                                      start=(h == 0), stop=False)
                    TE.inc(tensor.matmul(sb[:, 0:512], wcb[:, 7 * 1024 + 128 * oc:7 * 1024 + 128 * oc + 128],
                                         vw[:, 2176 * 7 + 65 + 512 * t:2176 * 7 + 65 + 512 * t + 512],
                                         start=False, stop=True))
                    te5[("s", t, oc)] = TE.n

                @blk.scalar
                def _(scalar, t=t, oc=oc, spar=spar, gidx5=gidx5):
                    W(scalar, "scalar", TE, te5[("s", t, oc)])
                    if gidx5 >= 2:
                        W(scalar, "scalar", VE, sg_hist[gidx5 - 2])
                    SE.inc(scalar.activation(out=gsl[:, 512 * spar:512 * spar + 512],
                                             in_=gacc[:, WTOK * oc + 512 * t:WTOK * oc + 512 * t + 512],
                                             func=AF.Silu))
                    se5[("g", t, oc)] = SE.n

                @blk.vector
                def _(vector, t=t, oc=oc, sb=sb, spar=spar, gidx5=gidx5):
                    W(vector, "vector", SE, se5[("g", t, oc)])
                    if gidx5 >= 2:
                        W(vector, "vector", TE, te5.get(("tr",) + divmod(gidx5 - 2, 8), 0))
                    VE.inc(vector.tensor_tensor(out=sgt[:, 512 * spar:512 * spar + 512],
                                                in0=sb[:, 0:512], in1=gsl[:, 512 * spar:512 * spar + 512],
                                                op=ALU.mult))
                    ve5[("sg", t, oc)] = VE.n
                    sg_hist.append(VE.n)

                @blk.tensor
                def _(tensor, t=t, oc=oc, spar=spar, gidx5=gidx5):
                    W(tensor, "tensor", VE, ve5[("sg", t, oc)])
                    tb = pb[2] if oc % 2 == 0 else pb[3]
                    if gidx5 >= 2:
                        W(tensor, "tensor", VE, ot_hist[gidx5 - 2])
                    for i in range(3):
                        tensor.transpose(tb[:, 128 * i:128 * i + 128], sgt[:, 512 * spar + 128 * i:512 * spar + 128 * i + 128], idf[:])
                    TE.inc(tensor.transpose(tb[:, 384:512], sgt[:, 512 * spar + 384:512 * spar + 512], idf[:]))
                    te5[("tr", t, oc)] = TE.n

                @blk.vector
                def _(vector, t=t, oc=oc, obuf=obuf, gidx5=gidx5):
                    W(vector, "vector", TE, te5[("tr", t, oc)])
                    tb = pb[2] if oc % 2 == 0 else pb[3]
                    if oc == 0:
                        W(vector, "vector", SO, so5.get(("tot", t - 1), 0))
                    for i in range(3):
                        vector.tensor_copy(otok[:, 1024 * i + 128 * oc:1024 * i + 128 * oc + 128],
                                           tb[:, 128 * i:128 * i + 128])
                    VE.inc(vector.tensor_copy(otok[:, 3072 + 128 * oc:3072 + 128 * oc + 128],
                                              tb[:, 384:512]))
                    ve5[("ot", t, oc)] = VE.n
                    ot_hist.append(VE.n)

            for i in range(4):
                @blk.sync
                def _(sync, t=t, i=i):
                    gx = 4 * t + i
                    if gx >= 2:
                        pt, pi = divmod(gx - 2, 4)
                        W(sync, "sync", VE, ve5.get(("add", pt, pi), 0))
                    LX[i % 2].dma(sync.dma_start(out=xsl5[:, 1024 * (i % 2):1024 * (i % 2) + 1024],
                                                 in_=x_slice[512 * t + 128 * i:512 * t + 128 * i + 128, :]))
                    ld5[("x", t, i)] = LX[i % 2].n

                @blk.vector
                def _(vector, t=t, i=i):
                    W(vector, "vector", VE, ve5[("ot", t, 7)])
                    W(vector, "vector", LX[i % 2], ld5[("x", t, i)])
                    VE.inc(vector.tensor_tensor(out=otok[:, 1024 * i:1024 * i + 1024],
                                                in0=otok[:, 1024 * i:1024 * i + 1024],
                                                in1=xsl5[:, 1024 * (i % 2):1024 * (i % 2) + 1024], op=ALU.add))
                    ve5[("add", t, i)] = VE.n

                @blk.sync
                def _(sync, t=t, i=i):
                    W(sync, "sync", VE, ve5[("add", t, i)])
                    SO.dma(sync.dma_start(out=out[512 * t + 128 * i:512 * t + 128 * i + 128, :],
                                          in_=otok[:, 1024 * i:1024 * i + 1024]))
                    so5[("out", t, i)] = SO.n
                    so5[("tot", t)] = SO.n

        @blk.sync
        def _(sync):
            sync.wait_ge(SO.h, SO.n)

        p5.close()

    st.close()
    return nc


_CACHE = {}


def _build_cached():
    key = tuple(sorted(DEBUG))
    if key not in _CACHE:
        _CACHE[key] = build_program()
    return _CACHE[key]


def kernel(L0_attn_seq, L0_attn_store_norm_g, L0_attn_retrieve_norm_g, L0_attn_w_kv,
           L0_attn_w_q, L0_attn_w_step, L0_attn_w_mom, L0_attn_w_decay, L0_attn_w_gate,
           L0_attn_w_combine, L0_attn_mh_gamma, L0_attn_mem_w0, L0_attn_mem_w1,
           L0_ffn_Wg, L0_ffn_W1, L0_ffn_W2):
    nc = _build_cached()
    f32 = np.float32
    x = np.ascontiguousarray(np.asarray(L0_attn_seq, dtype=f32).reshape(NTOK, D))
    w_kv = np.asarray(L0_attn_w_kv, dtype=f32)
    w_q = np.asarray(L0_attn_w_q, dtype=f32)
    w_step = np.asarray(L0_attn_w_step, dtype=f32)
    w_mom = np.asarray(L0_attn_w_mom, dtype=f32)
    w_decay = np.asarray(L0_attn_w_decay, dtype=f32)
    w_gate = np.asarray(L0_attn_w_gate, dtype=f32)
    w_comb = np.ascontiguousarray(np.asarray(L0_attn_w_combine, dtype=f32))
    mh_gamma = np.asarray(L0_attn_mh_gamma, dtype=f32)
    mem_w0 = np.ascontiguousarray(np.asarray(L0_attn_mem_w0, dtype=f32))
    mem_w1 = np.ascontiguousarray(np.asarray(L0_attn_mem_w1, dtype=f32))
    wg = np.ascontiguousarray(np.asarray(L0_ffn_Wg, dtype=f32))
    w1 = np.ascontiguousarray(np.asarray(L0_ffn_W1, dtype=f32))
    w2 = np.ascontiguousarray(np.asarray(L0_ffn_W2, dtype=f32))
    sgain = np.ascontiguousarray(np.asarray(L0_attn_store_norm_g, dtype=f32).reshape(8, 128))
    rgain = np.ascontiguousarray(np.asarray(L0_attn_retrieve_norm_g, dtype=f32).reshape(8, 128))

    idb = np.eye(128, dtype=ml_dtypes.bfloat16)
    idf = np.eye(128, dtype=np.float32)
    oseg = np.zeros((128, 2), dtype=ml_dtypes.bfloat16)
    oseg[0:64, 0] = ml_dtypes.bfloat16(1.0 / 64)
    oseg[64:128, 1] = ml_dtypes.bfloat16(1.0 / 64)
    cvc = np.zeros((128, 4), dtype=f32)
    cvc[:, 0] = EPS_LN
    cvc[:, 1] = EPS_RMS
    cvc[:, 2] = EPS_RMS * EPS_LN
    zf = np.zeros((128, 128), dtype=f32)

    in_maps = []
    for i in range(8):
        widx = np.zeros((128, 16), dtype=np.uint32)
        pr = np.arange(128, dtype=np.uint32)
        for h in range(8):
            widx[:, h] = (h * 128 + pr) * 8 + i
            widx[:, 8 + h] = (h * 128 + pr) * 128 + (16 * i - 1 if i > 0 else 0)
        bscale = np.full((128, 1), 1.0 if i > 0 else 0.0, dtype=f32)
        m = {
            "x_full": x,
            "x_slice": np.ascontiguousarray(x[WTOK * i:WTOK * (i + 1)]),
            "wk_h": np.ascontiguousarray(w_kv[:, 128 * i:128 * i + 128]),
            "wv_h": np.ascontiguousarray(w_kv[:, 1024 + 128 * i:1024 + 128 * i + 128]),
            "wq_h": np.ascontiguousarray(w_q[:, 128 * i:128 * i + 128]),
            "w4_h": np.ascontiguousarray(np.stack([w_step[:, i], w_gate[:, i]], axis=1)),
            "wmd_h": np.ascontiguousarray(np.stack([w_mom[:, i], w_decay[:, i]], axis=1)),
            "sgain": sgain, "rgain": rgain,
            "mw0": mem_w0, "mw1": mem_w1,
            "gamma_row": np.ascontiguousarray(mh_gamma[i].reshape(1, 128)),
            "wg_in": wg, "w1_in": w1, "w2_in": w2, "wc_in": w_comb,
            "idb_in": idb, "idf_in": idf, "oseg_in": oseg, "cv_in": cvc,
            "widx_in": widx, "bscale_in": bscale, "zf_in": zf,
        }
        in_maps.append(m)

    res = run_bass_kernel_spmd(nc, in_maps, core_ids=list(range(8)))
    outs = [res.results[i]["out"] for i in range(8)]
    full = np.concatenate(outs, axis=0).reshape(1, NTOK, D).astype(np.float32)
    if DEBUG:
        kernel._debug_results = res
    return full


# version 09
# speedup vs baseline: 1.1830x; 1.0043x over previous
import contextlib
import numpy as np
import ml_dtypes
import concourse.bass as bass
import concourse.mybir as mybir
from concourse.bass_utils import run_bass_kernel_spmd

F32 = mybir.dt.float32
BF16 = mybir.dt.bfloat16
U32 = mybir.dt.uint32
AF = mybir.ActivationFunctionType
ALU = mybir.AluOpType
AX = mybir.AxisListType

H = 8
DH = 128
CH = 64
D = 1024
NTOK = 16384
NCHUNK = NTOK // CH
NPAIR = NCHUNK // 2
WTOK = NTOK // 8
EPS_LN = 1e-5
EPS_RMS = 1e-6
LR_EFF = 0.01 * 2.0 / DH

DEBUG = []


class Sem:
    def __init__(self, stack, nc, name):
        self.h = stack.enter_context(nc.semaphore(name))
        self.name = name
        self.n = 0

    def inc(self, instr, k=1):
        instr.then_inc(self.h, k)
        self.n += k
        return instr

    def dma(self, instr):
        return self.inc(instr, 16)


_WAITED = {}


def W(eng, ename, sem, val):
    if val is None or val <= 0:
        return
    key = (ename, sem.name)
    if _WAITED.get(key, -1) >= val:
        return
    _WAITED[key] = val
    eng.wait_ge(sem.h, val)


def build_program():
    _WAITED.clear()
    nc = bass.Bass("TRN2", target_bir_lowering=False)

    x_full = nc.dram_tensor("x_full", [NTOK, D], F32, kind="ExternalInput").ap()
    x_slice = nc.dram_tensor("x_slice", [WTOK, D], F32, kind="ExternalInput").ap()
    wk_h = nc.dram_tensor("wk_h", [D, DH], F32, kind="ExternalInput").ap()
    wv_h = nc.dram_tensor("wv_h", [D, DH], F32, kind="ExternalInput").ap()
    wq_h = nc.dram_tensor("wq_h", [D, DH], F32, kind="ExternalInput").ap()
    w4_h = nc.dram_tensor("w4_h", [D, 2], F32, kind="ExternalInput").ap()
    wmd_h = nc.dram_tensor("wmd_h", [D, 2], F32, kind="ExternalInput").ap()
    sgain_in = nc.dram_tensor("sgain", [8, 128], F32, kind="ExternalInput").ap()
    rgain_in = nc.dram_tensor("rgain", [8, 128], F32, kind="ExternalInput").ap()
    mw0_in = nc.dram_tensor("mw0", [128, 128], F32, kind="ExternalInput").ap()
    mw1_in = nc.dram_tensor("mw1", [128, 128], F32, kind="ExternalInput").ap()
    gamma_in = nc.dram_tensor("gamma_row", [1, 128], F32, kind="ExternalInput").ap()
    wg_in = nc.dram_tensor("wg_in", [D, H], F32, kind="ExternalInput").ap()
    w1_in = nc.dram_tensor("w1_in", [8, D, 4096], F32, kind="ExternalInput").ap()
    w2_in = nc.dram_tensor("w2_in", [8, 4096, D], F32, kind="ExternalInput").ap()
    wc_in = nc.dram_tensor("wc_in", [D, D], F32, kind="ExternalInput").ap()
    idb_in = nc.dram_tensor("idb_in", [128, 128], BF16, kind="ExternalInput").ap()
    idf_in = nc.dram_tensor("idf_in", [128, 128], F32, kind="ExternalInput").ap()
    oseg_in = nc.dram_tensor("oseg_in", [128, 2], BF16, kind="ExternalInput").ap()
    cv_in = nc.dram_tensor("cv_in", [128, 4], F32, kind="ExternalInput").ap()
    widx_in = nc.dram_tensor("widx_in", [128, 16], U32, kind="ExternalInput").ap()
    bscale_in = nc.dram_tensor("bscale_in", [128, 1], F32, kind="ExternalInput").ap()
    zf_in = nc.dram_tensor("zf_in", [128, 128], F32, kind="ExternalInput").ap()
    out = nc.dram_tensor("out", [WTOK, D], F32, kind="ExternalOutput").ap()

    def scratch(name, shape, dtype):
        if name in DEBUG:
            return nc.dram_tensor(name, shape, dtype, kind="ExternalOutput").ap()
        return nc.dram_tensor(name, shape, dtype).ap()

    kTd = scratch("kTd", [128, NTOK], BF16)
    vTd = scratch("vTd", [128, NTOK], BF16)
    rTd = scratch("rTd", [128, NTOK + 64], BF16)
    gswd = scratch("gswd", [128, 128], F32)
    lrd = scratch("lrd", [1, NTOK], F32)
    gaterd = scratch("gaterd", [8, WTOK], BF16)
    agin_t = nc.dram_tensor("agin", [128, NTOK], BF16)
    agout_t = nc.dram_tensor("agout", [8, 128, NTOK], BF16)
    agin = agin_t.ap()
    agout = agout_t.ap()
    dbg = {}
    for nm, shape, dt in [
        ("agin_d", [128, NTOK], BF16),
        ("momb_d", [128, 256], F32), ("decb_d", [128, 256], F32),
        ("cm_d", [2, 128, 1024], F32),
        ("gacc_d", [8, 128, WTOK], F32),
        ("xnT_d", [8, 128, WTOK], BF16),
    ]:
        if nm in DEBUG:
            dbg[nm] = nc.dram_tensor(nm, shape, dt, kind="ExternalOutput").ap()

    st = contextlib.ExitStack()
    E = st.enter_context

    LD = Sem(st, nc, "LD")
    LG = Sem(st, nc, "LG")
    LX = [Sem(st, nc, "LX0"), Sem(st, nc, "LX1")]
    SK = Sem(st, nc, "SK")
    LW = [Sem(st, nc, "LW0"), Sem(st, nc, "LW1")]
    KV = [Sem(st, nc, "KV0"), Sem(st, nc, "KV1")]
    RW = [Sem(st, nc, "RW0"), Sem(st, nc, "RW1")]
    SA = [Sem(st, nc, "SA0"), Sem(st, nc, "SA1")]
    SC2 = Sem(st, nc, "SC2")
    SPK = Sem(st, nc, "SPK")
    SG3 = Sem(st, nc, "SG3")
    SG4 = Sem(st, nc, "SG4")
    SV5 = Sem(st, nc, "SV5")
    TE = Sem(st, nc, "TE")
    VE = Sem(st, nc, "VE")
    SE = Sem(st, nc, "SE")
    CC = Sem(st, nc, "CC")
    SO = Sem(st, nc, "SO")
    FN = Sem(st, nc, "FN")

    pb = [E(nc.psum_tensor(f"pb{i}", [128, 512], F32)) for i in range(6)]
    pbt = [E(nc.psum_tensor(f"pbt{i}", [128, 1024], BF16)) for i in range(2)]

    idb = E(nc.sbuf_tensor("idb", [128, 128], BF16))
    idf = E(nc.sbuf_tensor("idf", [128, 128], F32))
    oseg = E(nc.sbuf_tensor("oseg", [128, 2], BF16))
    cv = E(nc.sbuf_tensor("cv", [128, 4], F32))
    ones1 = E(nc.sbuf_tensor("ones1", [1, 128], BF16))
    onescol = E(nc.sbuf_tensor("onescol", [128, 1], BF16))
    onesf = E(nc.sbuf_tensor("onesf", [1, 128], F32))
    onec1f = E(nc.sbuf_tensor("onec1f", [1, 1], F32))
    mw0f = E(nc.sbuf_tensor("mw0f", [128, 128], F32))
    mw1f = E(nc.sbuf_tensor("mw1f", [128, 128], F32))
    w0b = E(nc.sbuf_tensor("w0b", [128, 128], BF16))
    w1b = E(nc.sbuf_tensor("w1b", [128, 128], BF16))
    w1Tb = E(nc.sbuf_tensor("w1Tb", [128, 128], BF16))
    wkb = E(nc.sbuf_tensor("wkb", [128, 8 * 128], BF16))
    wvb = E(nc.sbuf_tensor("wvb", [128, 8 * 128], BF16))
    wqb = E(nc.sbuf_tensor("wqb", [128, 8 * 128], BF16))
    w4b = E(nc.sbuf_tensor("w4b", [128, 8 * 2], BF16))
    wmdb = E(nc.sbuf_tensor("wmdb", [128, 8 * 2], BF16))
    gp1row = E(nc.sbuf_tensor("gp1row", [1, 128], F32))
    momb = E(nc.sbuf_tensor("momb", [128, 256], F32))
    decb = E(nc.sbuf_tensor("decb", [128, 256], F32))
    M0 = E(nc.sbuf_tensor("M0", [128, 128], F32))
    M1 = E(nc.sbuf_tensor("M1", [128, 128], F32))
    U0 = E(nc.sbuf_tensor("U0", [128, 128], F32))
    U1 = E(nc.sbuf_tensor("U1", [128, 128], F32))
    xnT = E(nc.sbuf_tensor("xnT", [128, 8 * WTOK], BF16))
    gacc = E(nc.sbuf_tensor("gacc", [128, 8 * WTOK], F32))
    widxS = E(nc.sbuf_tensor("widxS", [128, 16], U32))
    bscaleS = E(nc.sbuf_tensor("bscaleS", [128, 1], F32))
    zrow = E(nc.sbuf_tensor("zrow", [128, 64], BF16))

    with nc.Block() as blk:

        def fence():
            base = FN.n
            FN.n = base + 5
            tgt = FN.n

            @blk.sync
            def _(sync):
                for s in [LD, SO, SK, LX[0], LX[1], LW[0], LW[1], KV[0], KV[1]]:
                    W(sync, "sync", s, s.n)
                sync.sem_inc(FN.h, 1)
                sync.wait_ge(FN.h, tgt)

            @blk.gpsimd
            def _(gpsimd):
                for s in [LG, SC2, SPK, SG3, SG4, SV5, RW[0], RW[1], SA[0], SA[1]]:
                    W(gpsimd, "gpsimd", s, s.n)
                gpsimd.sem_inc(FN.h, 1)
                gpsimd.wait_ge(FN.h, tgt)

            @blk.tensor
            def _(tensor):
                tensor.wait_ge(TE.h, TE.n)
                tensor.sem_inc(FN.h, 1)
                tensor.wait_ge(FN.h, tgt)

            @blk.vector
            def _(vector):
                vector.wait_ge(VE.h, VE.n)
                vector.sem_inc(FN.h, 1)
                vector.wait_ge(FN.h, tgt)

            @blk.scalar
            def _(scalar):
                scalar.wait_ge(SE.h, SE.n)
                scalar.sem_inc(FN.h, 1)
                scalar.wait_ge(FN.h, tgt)

        p0 = contextlib.ExitStack()
        E0 = p0.enter_context
        wkst = E0(nc.sbuf_tensor("wkst", [128, 8 * 128], F32))
        wvst = E0(nc.sbuf_tensor("wvst", [128, 8 * 128], F32))
        wqst = E0(nc.sbuf_tensor("wqst", [128, 8 * 128], F32))
        w4st = E0(nc.sbuf_tensor("w4st", [128, 8 * 2], F32))
        wmdst = E0(nc.sbuf_tensor("wmdst", [128, 8 * 2], F32))
        sgainS = E0(nc.sbuf_tensor("sgainS", [128, 8], F32))
        rgainS = E0(nc.sbuf_tensor("rgainS", [128, 8], F32))
        grow0 = E0(nc.sbuf_tensor("grow0", [1, 128], F32))
        zf32 = E0(nc.sbuf_tensor("zf32", [128, 128], F32))

        @blk.sync
        def _(sync):
            for (dst, src) in [(idb, idb_in), (idf, idf_in), (oseg, oseg_in),
                               (cv, cv_in), (mw0f, mw0_in), (mw1f, mw1_in),
                               (widxS, widx_in), (bscaleS, bscale_in),
                               (grow0, gamma_in), (zf32, zf_in)]:
                LD.dma(sync.dma_start(out=dst[:], in_=src[:, :]))
            for k in range(8):
                LD.dma(sync.dma_start(out=wkst[:, 128 * k:128 * k + 128], in_=wk_h[128 * k:128 * k + 128, :]))
                LD.dma(sync.dma_start(out=wvst[:, 128 * k:128 * k + 128], in_=wv_h[128 * k:128 * k + 128, :]))
                LD.dma(sync.dma_start(out=wqst[:, 128 * k:128 * k + 128], in_=wq_h[128 * k:128 * k + 128, :]))
                LD.dma(sync.dma_start(out=w4st[:, 2 * k:2 * k + 2], in_=w4_h[128 * k:128 * k + 128, :]))
                LD.dma(sync.dma_start(out=wmdst[:, 2 * k:2 * k + 2], in_=wmd_h[128 * k:128 * k + 128, :]))
                LD.dma(sync.dma_start(out=sgainS[:, k:k + 1], in_=sgain_in[k:k + 1, :]))
                LD.dma(sync.dma_start(out=rgainS[:, k:k + 1], in_=rgain_in[k:k + 1, :]))

        @blk.vector
        def _(vector):
            W(vector, "vector", LD, LD.n)
            vector.memset(ones1[:], 1.0)
            vector.memset(onescol[:], 1.0)
            vector.memset(onesf[:], 1.0)
            vector.memset(onec1f[:], 1.0)
            vector.memset(M0[:], 0.0)
            vector.memset(M1[:], 0.0)
            vector.memset(U0[:], 0.0)
            vector.memset(U1[:], 0.0)
            vector.memset(zrow[:], 0.0)
            vector.memset(gacc[:], 0.0)
            for k in range(8):
                s = slice(128 * k, 128 * k + 128)
                vector.tensor_scalar(out=wkb[:, s], in0=wkst[:, s], scalar1=sgainS[:, k:k + 1],
                                     scalar2=None, op0=ALU.mult)
                vector.tensor_scalar(out=wvb[:, s], in0=wvst[:, s], scalar1=sgainS[:, k:k + 1],
                                     scalar2=None, op0=ALU.mult)
                vector.tensor_scalar(out=wqb[:, s], in0=wqst[:, s], scalar1=rgainS[:, k:k + 1],
                                     scalar2=None, op0=ALU.mult)
                vector.tensor_scalar(out=w4b[:, 2 * k:2 * k + 1], in0=w4st[:, 2 * k:2 * k + 1],
                                     scalar1=sgainS[:, k:k + 1], scalar2=None, op0=ALU.mult)
                vector.tensor_scalar(out=w4b[:, 2 * k + 1:2 * k + 2], in0=w4st[:, 2 * k + 1:2 * k + 2],
                                     scalar1=rgainS[:, k:k + 1], scalar2=None, op0=ALU.mult)
                vector.tensor_scalar(out=wmdb[:, 2 * k:2 * k + 2], in0=wmdst[:, 2 * k:2 * k + 2],
                                     scalar1=sgainS[:, k:k + 1], scalar2=None, op0=ALU.mult)
            vector.tensor_scalar(out=gp1row[:], in0=grow0[:], scalar1=1.0, scalar2=None, op0=ALU.add)
            vector.tensor_copy(w0b[:], mw0f[:])
            VE.inc(vector.tensor_copy(w1b[:], mw1f[:]))

        @blk.tensor
        def _(tensor):
            W(tensor, "tensor", VE, VE.n)
            TE.inc(tensor.transpose(pbt[0][:, 0:128], w1b[:], idb[:]))

        @blk.scalar
        def _(scalar):
            W(scalar, "scalar", TE, TE.n)
            SE.inc(scalar.copy(out=w1Tb[:], in_=pbt[0][:, 0:128]))

        @blk.gpsimd
        def _(gpsimd):
            W(gpsimd, "gpsimd", VE, VE.n)
            LG.dma(gpsimd.dma_start(out=rTd[:, NTOK:NTOK + 64], in_=zrow[:]))
            W(gpsimd, "gpsimd", LD, LD.n)
            LG.dma(gpsimd.dma_start(out=gswd[:, :], in_=zf32[:]))

        fence()
        p0.close()

        p1 = contextlib.ExitStack()
        E1 = p1.enter_context
        xst = E1(nc.sbuf_tensor("xst", [128, 2 * 1024], F32))
        xsq = E1(nc.sbuf_tensor("xsq", [128, 2 * 1024], F32))
        ytok = E1(nc.sbuf_tensor("ytok", [128, 2 * 1024], BF16))
        yTst = E1(nc.sbuf_tensor("yTst", [128, 8192], BF16))
        stats = E1(nc.sbuf_tensor("stats", [128, 2 * 16], F32))
        cm2 = E1(nc.sbuf_tensor("cm2", [2, 2 * 1024], F32))
        cmtok = E1(nc.sbuf_tensor("cmtok", [128, 2 * 1024], F32))
        kvr = E1(nc.sbuf_tensor("kvr", [128, 2 * 3 * 512], BF16))
        sgm = E1(nc.sbuf_tensor("sgm", [2, 2 * 512], F32))
        lrrow = E1(nc.sbuf_tensor("lrrow", [1, 2 * 512], F32))

        se_y = {}
        se_sq1 = {}
        ve_st = {}
        ve_ev = {}
        ve_cm = {}
        te_tc = {}
        te_pj = {}
        se_ev = {}
        ve_lr = {}
        ld_x = {}
        ld_kvr = {}
        lg_cm = {}
        lg_pk = {}


        for g in range(128):
            t, i = divmod(g, 4)
            par = g % 2
            so = 16 * par
            xv = xst[:, 1024 * par:1024 * par + 1024]
            xqv = xsq[:, 1024 * par:1024 * par + 1024]
            yv = ytok[:, 1024 * par:1024 * par + 1024]

            @blk.sync
            def _(sync, g=g, par=par, xv=xv):
                W(sync, "sync", VE, se_y.get(g - 2, 0))
                LX[par].dma(sync.dma_start(out=xv, in_=x_full[128 * g:128 * g + 128, :]))
                ld_x[g] = LX[par].n

            @blk.vector
            def _(vector, g=g, par=par, xv=xv, xqv=xqv, so=so):
                W(vector, "vector", LX[par], ld_x[g])
                W(vector, "vector", SE, se_sq1.get(g - 2, 0))
                vector.reduce_sum(stats[:, so + 0:so + 1], xv, axis=AX.X)
                vector.tensor_tensor(out=xqv, in0=xv, in1=xv, op=ALU.mult)
                vector.drain()
                vector.reduce_sum(stats[:, so + 1:so + 2], xqv, axis=AX.X)
                vector.tensor_scalar(out=stats[:, so + 2:so + 3], in0=stats[:, so + 0:so + 1],
                                     scalar1=1.0 / D, scalar2=None, op0=ALU.mult)
                vector.drain()
                vector.tensor_scalar(out=stats[:, so + 3:so + 4], in0=stats[:, so + 1:so + 2],
                                     scalar1=1.0 / D, scalar2=None, op0=ALU.mult)
                vector.drain()
                VE.inc(vector.scalar_tensor_tensor(out=stats[:, so + 4:so + 5], in0=stats[:, so + 2:so + 3],
                                                   scalar=stats[:, so + 2:so + 3], in1=stats[:, so + 3:so + 4],
                                                   op0=ALU.mult, op1=ALU.subtract))
                ve_st[(g, "var")] = VE.n

            @blk.scalar
            def _(scalar, g=g, so=so):
                W(scalar, "scalar", VE, ve_st[(g, "var")])
                SE.inc(scalar.activation(out=stats[:, so + 5:so + 6], in_=stats[:, so + 4:so + 5],
                                         func=AF.Sqrt, scale=-(1.0 + EPS_RMS), bias=cv[:, 2:3]))
                se_sq1[g] = SE.n

            @blk.vector
            def _(vector, g=g, xv=xv, yv=yv, so=so):
                W(vector, "vector", SE, se_sq1[g])
                W(vector, "vector", TE, te_tc.get(g - 2, 0))
                vector.reciprocal(stats[:, so + 6:so + 7], stats[:, so + 5:so + 6])
                vector.drain()
                vector.scalar_tensor_tensor(out=stats[:, so + 7:so + 8], in0=stats[:, so + 2:so + 3],
                                            scalar=-1.0, in1=stats[:, so + 6:so + 7],
                                            op0=ALU.mult, op1=ALU.mult)
                ve_st[g] = VE.n
                vector.drain()
                VE.inc(vector.tensor_scalar(out=yv, in0=xv, scalar1=stats[:, so + 6:so + 7],
                                            scalar2=stats[:, so + 7:so + 8], op0=ALU.mult, op1=ALU.add))
                se_y[g] = VE.n

            @blk.tensor
            def _(tensor, g=g, yv=yv, par=par):
                W(tensor, "tensor", VE, se_y[g])
                W(tensor, "tensor", VE, ve_ev.get(g - 2, 0))
                W(tensor, "tensor", VE, ve_cm.get(g - 2, 0))
                for k in range(7):
                    tensor.transpose(pbt[par][:, 128 * k:128 * k + 128], yv[:, 128 * k:128 * k + 128], idb[:])
                TE.inc(tensor.transpose(pbt[par][:, 896:1024], yv[:, 896:1024], idb[:]))
                tensor.matmul(pb[4][0:2, 0:512], oseg[:], yv[:, 0:512], start=True, stop=True)
                TE.inc(tensor.matmul(pb[5][0:2, 0:512], oseg[:], yv[:, 512:1024], start=True, stop=True))
                te_tc[g] = TE.n

            @blk.vector
            def _(vector, g=g, par=par, t=t, i=i):
                W(vector, "vector", TE, te_tc[g])
                buf = t % 2
                if t >= 2 and i == 0:
                    W(vector, "vector", TE, te_pj.get(t - 2, 0))
                dstv = yTst[:].rearrange("p (b k u c) -> p b k u c", b=2, k=8, u=4, c=128)[:, buf, :, i, :]
                srcv = pbt[par][:].rearrange("p (k c) -> p k c", k=8, c=128)
                vector.tensor_copy(dstv, srcv)
                W(vector, "vector", SC2, lg_cm.get(g - 2, 0))
                vector.tensor_copy(cm2[:, 1024 * par:1024 * par + 512], pb[4][0:2, 0:512])
                VE.inc(vector.tensor_copy(cm2[:, 1024 * par + 512:1024 * par + 1024], pb[5][0:2, 0:512]))
                ve_cm[g] = VE.n
                ve_ev[g] = VE.n

            @blk.gpsimd
            def _(gpsimd, g=g, par=par):
                W(gpsimd, "gpsimd", SC2, lg_cm.get(g - 1, 0))
                W(gpsimd, "gpsimd", VE, ve_cm[g])
                tix = g // 64
                row = (2 * g) % 128
                SC2.dma(gpsimd.dma_start(out=cmtok[row:row + 2, 1024 * tix:1024 * tix + 1024],
                                        in_=cm2[:, 1024 * par:1024 * par + 1024]))
                lg_cm[g] = SC2.n

            if i == 3:
                buf = t % 2
                pso = 512 * 0

                @blk.tensor
                def _(tensor, t=t, buf=buf, g=g):
                    W(tensor, "tensor", VE, ve_ev[g])
                    W(tensor, "tensor", SE, se_ev.get(t - 2, 0))
                    yview = yTst[:].rearrange("p (b k u) -> p b k u", b=2, k=8, u=512)
                    for k in range(8):
                        tensor.matmul(pb[0][:, 0:512], wkb[:, 128 * k:128 * k + 128],
                                      yview[:, buf, k, :], start=(k == 0), stop=(k == 7))
                    for k in range(8):
                        tensor.matmul(pb[1][:, 0:512], wvb[:, 128 * k:128 * k + 128],
                                      yview[:, buf, k, :], start=(k == 0), stop=(k == 7))
                    for k in range(8):
                        tensor.matmul(pb[2][:, 0:512], wqb[:, 128 * k:128 * k + 128],
                                      yview[:, buf, k, :], start=(k == 0), stop=(k == 7))
                    for k in range(7):
                        tensor.matmul(pb[3][0:2, 0:512], w4b[:, 2 * k:2 * k + 2],
                                      yview[:, buf, k, :], start=(k == 0), stop=False)
                    TE.inc(tensor.matmul(pb[3][0:2, 0:512], w4b[:, 14:16],
                                         yview[:, buf, 7, :], start=False, stop=True))
                    te_pj[t] = TE.n

                @blk.scalar
                def _(scalar, t=t, buf=buf):
                    W(scalar, "scalar", TE, te_pj[t])
                    W(scalar, "scalar", SK, ld_kvr.get(t - 2, 0))
                    W(scalar, "scalar", VE, ve_lr.get(t - 2, 0))
                    o = 3 * 512 * buf
                    scalar.copy(out=kvr[:, o:o + 512], in_=pb[0][:, 0:512])
                    scalar.copy(out=kvr[:, o + 512:o + 1024], in_=pb[1][:, 0:512])
                    scalar.copy(out=kvr[:, o + 1024:o + 1536], in_=pb[2][:, 0:512])
                    SE.inc(scalar.activation(out=sgm[:, 512 * buf:512 * buf + 512],
                                             in_=pb[3][0:2, 0:512], func=AF.Sigmoid))
                    se_ev[t] = SE.n

                @blk.vector
                def _(vector, t=t, buf=buf):
                    W(vector, "vector", SE, se_ev[t])
                    W(vector, "vector", SPK, lg_pk.get(t - 2, 0))
                    VE.inc(vector.tensor_scalar(out=lrrow[:, 512 * buf:512 * buf + 512],
                                                in0=sgm[0:1, 512 * buf:512 * buf + 512],
                                                scalar1=LR_EFF, scalar2=None, op0=ALU.mult))
                    ve_lr[t] = VE.n

                @blk.sync
                def _(sync, t=t, buf=buf):
                    W(sync, "sync", SK, ld_kvr.get(t - 1, 0))
                    W(sync, "sync", SE, se_ev[t])
                    o = 3 * 512 * buf
                    SK.dma(sync.dma_start(out=kTd[:, 512 * t:512 * t + 512], in_=kvr[:, o:o + 512]))
                    SK.dma(sync.dma_start(out=vTd[:, 512 * t:512 * t + 512], in_=kvr[:, o + 512:o + 1024]))
                    SK.dma(sync.dma_start(out=rTd[:, 512 * t:512 * t + 512], in_=kvr[:, o + 1024:o + 1536]))
                    ld_kvr[t] = SK.n

                @blk.gpsimd
                def _(gpsimd, t=t, buf=buf):
                    W(gpsimd, "gpsimd", SPK, lg_pk.get(t - 1, 0))
                    W(gpsimd, "gpsimd", VE, ve_lr[t])
                    SPK.dma(gpsimd.dma_start(out=lrd[0:1, 512 * t:512 * t + 512],
                                            in_=lrrow[0:1, 512 * buf:512 * buf + 512]))
                    gv = sgm[1:2, 512 * buf:512 * buf + 512]
                    if t > 0:
                        SPK.dma(gpsimd.dma_start(out=gswd[4 * t - 1:4 * t, 65:128], in_=gv[:, 0:63]))
                    SPK.dma(gpsimd.dma_start(out=gswd[4 * t:4 * t + 3, :], in_=gv[:, 63:447]))
                    SPK.dma(gpsimd.dma_start(out=gswd[4 * t + 3:4 * t + 4, 0:65], in_=gv[:, 447:512]))
                    lg_pk[t] = SPK.n

        fence()

        if "cm_d" in DEBUG:
            @blk.gpsimd
            def _(gpsimd):
                LG.dma(gpsimd.dma_start(out=dbg["cm_d"][0], in_=cmtok[:, 0:1024]))
                LG.dma(gpsimd.dma_start(out=dbg["cm_d"][1], in_=cmtok[:, 1024:2048]))
                gpsimd.wait_ge(LG.h, LG.n)

        cmbf = E1(nc.sbuf_tensor("cmbf", [128, 1024], BF16))
        cmT = E1(nc.sbuf_tensor("cmT", [128, 8 * 256], BF16))
        mrow = E1(nc.sbuf_tensor("mrow", [1, 256], F32))
        drow = E1(nc.sbuf_tensor("drow", [1, 256], F32))

        for half in range(2):
            @blk.scalar
            def _(scalar, half=half):
                if half == 1:
                    W(scalar, "scalar", VE, VE.n)
                SE.inc(scalar.copy(out=cmbf[:], in_=cmtok[:, 1024 * half:1024 * half + 1024]))

            @blk.tensor
            def _(tensor, half=half):
                W(tensor, "tensor", SE, SE.n)
                if half == 1:
                    W(tensor, "tensor", VE, VE.n)
                for k in range(7):
                    tensor.transpose(pbt[half][:, 128 * k:128 * k + 128],
                                     cmbf[:, 128 * k:128 * k + 128], idb[:])
                TE.inc(tensor.transpose(pbt[half][:, 896:1024], cmbf[:, 896:1024], idb[:]))

            @blk.vector
            def _(vector, half=half):
                W(vector, "vector", TE, TE.n)
                dstv = cmT[:].rearrange("p (k c) -> p k c", k=8, c=256)[:, :, 128 * half:128 * half + 128]
                VE.inc(vector.tensor_copy(dstv, pbt[half][:].rearrange("p (k c) -> p k c", k=8, c=128)))

        @blk.tensor
        def _(tensor):
            W(tensor, "tensor", VE, VE.n)
            for k in range(8):
                tensor.matmul(pb[0][0:1, 0:256], wmdb[:, 2 * k:2 * k + 1],
                              cmT[:, 256 * k:256 * k + 256], start=(k == 0), stop=(k == 7))
            for k in range(7):
                tensor.matmul(pb[3][0:1, 0:256], wmdb[:, 2 * k + 1:2 * k + 2],
                              cmT[:, 256 * k:256 * k + 256], start=(k == 0), stop=False)
            TE.inc(tensor.matmul(pb[3][0:1, 0:256], wmdb[:, 15:16],
                                 cmT[:, 1792:2048], start=False, stop=True))

        @blk.scalar
        def _(scalar):
            W(scalar, "scalar", TE, TE.n)
            scalar.activation(out=mrow[:], in_=pb[0][0:1, 0:256], func=AF.Sigmoid)
            SE.inc(scalar.activation(out=drow[:], in_=pb[3][0:1, 0:256], func=AF.Sigmoid))

        @blk.vector
        def _(vector):
            W(vector, "vector", SE, SE.n)
            VE.inc(vector.tensor_scalar(out=drow[:], in0=drow[:], scalar1=-1.0,
                                        scalar2=1.0, op0=ALU.mult, op1=ALU.add))

        @blk.tensor
        def _(tensor):
            W(tensor, "tensor", VE, VE.n)
            tensor.matmul(pb[1][:, 0:256], onesf[:], mrow[:], start=True, stop=True)
            TE.inc(tensor.matmul(pb[2][:, 0:256], onesf[:], drow[:], start=True, stop=True))

        @blk.vector
        def _(vector):
            W(vector, "vector", TE, TE.n)
            vector.tensor_copy(momb[:], pb[1][:, 0:256])
            VE.inc(vector.tensor_copy(decb[:], pb[2][:, 0:256]))

        fence()
        p1.close()

        if "momb_d" in DEBUG:
            @blk.gpsimd
            def _(gpsimd):
                LG.dma(gpsimd.dma_start(out=dbg["momb_d"][:, :], in_=momb[:]))
                LG.dma(gpsimd.dma_start(out=dbg["decb_d"][:, :], in_=decb[:]))
                gpsimd.wait_ge(LG.h, LG.n)

        p2 = contextlib.ExitStack()
        E2 = p2.enter_context
        kp = E2(nc.sbuf_tensor("kp", [128, 8 * 128], BF16))
        vp = E2(nc.sbuf_tensor("vp", [128, 8 * 128], BF16))
        rp = E2(nc.sbuf_tensor("rp", [128, 8 * 128], BF16))
        grow2 = E2(nc.sbuf_tensor("grow2", [1, 8 * 128], F32))
        lrow2 = E2(nc.sbuf_tensor("lrow2", [1, 8 * 128], F32))
        a1fS = E2(nc.sbuf_tensor("a1fS", [128, 2 * 128], BF16))
        a1tS = E2(nc.sbuf_tensor("a1tS", [128, 2 * 128], BF16))
        sigS = E2(nc.sbuf_tensor("sigS", [128, 2 * 128], BF16))
        dpfS = E2(nc.sbuf_tensor("dpfS", [128, 2 * 128], BF16))
        dptS = E2(nc.sbuf_tensor("dptS", [128, 2 * 128], BF16))
        dzS = E2(nc.sbuf_tensor("dzS", [128, 2 * 128], BF16))
        ktS = E2(nc.sbuf_tensor("ktS", [128, 2 * 128], BF16))
        vtS = E2(nc.sbuf_tensor("vtS", [128, 2 * 128], BF16))
        tmpa = E2(nc.sbuf_tensor("tmpa", [128, 2 * 128], F32))
        tmpb = E2(nc.sbuf_tensor("tmpb", [128, 2 * 128], F32))
        w0cS = E2(nc.sbuf_tensor("w0cS", [128, 2 * 128], BF16))
        w1cS = E2(nc.sbuf_tensor("w1cS", [128, 2 * 128], BF16))
        s1S = E2(nc.sbuf_tensor("s1S", [128, 2 * 128], BF16))
        sqS = E2(nc.sbuf_tensor("sqS", [128, 2 * 128], BF16))
        r0S = E2(nc.sbuf_tensor("r0S", [1, 2 * 128], F32))
        r2S = E2(nc.sbuf_tensor("r2S", [1, 2 * 128], F32))
        valsS = E2(nc.sbuf_tensor("valsS", [128, 2 * 128], BF16))

        ld_pr = {}
        lg_gt = {}
        se_a1 = {}
        se_srow = {}
        se_s1r = {}
        se_sq = {}
        ve_dp = {}
        ve_dpf = {}
        ve_dz = {}
        ve_r2 = {}
        ve_vals = {}
        ve_scanw = {}
        te_z = {}
        te_m = {}
        te_dpt = {}
        te_da = {}
        te_g = {}
        te_r = {}
        te_ssq = {}
        te_sc = {}
        lg_v = {}

        def emit_kv_group(G):
            lane = G % 2
            off = 512 * (G % 2)

            @blk.sync
            def _(sync, G=G, lane=lane, off=off):
                if G >= 2:
                    W(sync, "sync", TE, te_r[(4 * G - 5, 1, 0)])
                KV[lane].dma(sync.dma_start(out=kp[:, off:off + 512], in_=kTd[:, 512 * G:512 * G + 512]))
                KV[lane].dma(sync.dma_start(out=vp[:, off:off + 512], in_=vTd[:, 512 * G:512 * G + 512]))
                KV[lane].dma(sync.dma_start(out=rp[:, off:off + 512], in_=rTd[:, 512 * G + 63:512 * G + 63 + 512]))
                for pp in range(4 * G, min(4 * G + 4, NPAIR)):
                    ld_pr[pp] = (lane, KV[lane].n)

            @blk.gpsimd
            def _(gpsimd, G=G, lane=lane, off=off):
                if G >= 2:
                    W(gpsimd, "gpsimd", VE, ve_r2.get(4 * G - 5, 0))
                RW[lane].dma(gpsimd.dma_start(out=grow2[0:1, off:off + 512], in_=gswd[4 * G:4 * G + 4, :]))
                RW[lane].dma(gpsimd.dma_start(out=lrow2[0:1, off:off + 512], in_=lrd[0:1, 512 * G:512 * G + 512]))
                for pp in range(4 * G, min(4 * G + 4, NPAIR)):
                    lg_gt[pp] = (lane, RW[lane].n)

        emit_kv_group(0)
        emit_kv_group(1)

        for p in range(NPAIR):
            par = p % 2
            o = 128 * par
            s_ = slice(o, o + 128)
            ko = 128 * (p % 8)
            ks_ = slice(ko, ko + 128)
            zb = pb[0] if par == 0 else pb[1]
            wb = pb[2] if par == 0 else pb[3]
            gb0, gb1 = pb[4], pb[5]

            if p % 4 == 0 and p >= 4 and (p // 4 + 1) * 4 < NPAIR + 4:
                G = p // 4 + 1
                if 4 * G < NPAIR:
                    emit_kv_group(G)

            @blk.tensor
            def _(tensor, p=p, ks_=ks_, zb=zb):
                lane, val = ld_pr[p]
                W(tensor, "tensor", KV[lane], val)
                W(tensor, "tensor", SE, se_srow.get(p - 2, 0))
                tensor.matmul(zb[:, 0:128], kp[:, ks_], w0b[:], start=True, stop=True)
                TE.inc(tensor.matmul(zb[:, 128:256], w0b[:], kp[:, ks_], start=True, stop=True))
                te_z[p] = TE.n

            @blk.scalar
            def _(scalar, p=p, s_=s_, zb=zb):
                W(scalar, "scalar", TE, te_z[p])
                scalar.activation(out=a1fS[:, s_], in_=zb[:, 128:256], func=AF.Silu)
                scalar.activation(out=a1tS[:, s_], in_=zb[:, 0:128], func=AF.Silu)
                SE.inc(scalar.activation(out=sigS[:, s_], in_=zb[:, 0:128], func=AF.Sigmoid))
                se_a1[p] = SE.n

            @blk.tensor
            def _(tensor, p=p, s_=s_, ks_=ks_, ko=ko, zb=zb, wb=wb, par=par):
                W(tensor, "tensor", SE, se_a1[p])
                W(tensor, "tensor", VE, ve_dpf.get(p - 2, 0))
                lane, val = lg_gt[p]
                W(tensor, "tensor", RW[lane], val)
                W(tensor, "tensor", VE, ve_vals.get(p - 2, 0))
                tensor.matmul(zb[:, 384:512], a1fS[:, s_], w1b[:], start=True, stop=True)
                tensor.matmul(wb[:, 128:129], lrow2[0:1, ko:ko + 128], onec1f[:], start=True, stop=True)
                tensor.transpose(pbt[par][:, 0:128], kp[:, ks_], idb[:])
                TE.inc(tensor.transpose(pbt[par][:, 128:256], vp[:, ks_], idb[:]))
                te_m[p] = TE.n

            @blk.vector
            def _(vector, p=p, s_=s_, zb=zb, wb=wb, par=par):
                W(vector, "vector", TE, te_m[p])
                vector.tensor_copy(ktS[:, s_], pbt[par][:, 0:128])
                vector.tensor_copy(vtS[:, s_], pbt[par][:, 128:256])
                vector.drain()
                vector.tensor_tensor(out=tmpb[:, s_], in0=zb[:, 384:512], in1=vtS[:, s_], op=ALU.subtract)
                vector.drain()
                VE.inc(vector.tensor_scalar(out=dptS[:, s_], in0=tmpb[:, s_], scalar1=wb[:, 128:129],
                                            scalar2=None, op0=ALU.mult))
                ve_dp[p] = VE.n

            @blk.tensor
            def _(tensor, p=p, s_=s_, par=par):
                W(tensor, "tensor", VE, ve_dp[p])
                TE.inc(tensor.transpose(pbt[par][:, 256:384], dptS[:, s_], idb[:]))
                te_dpt[p] = TE.n

            @blk.vector
            def _(vector, p=p, s_=s_, par=par):
                W(vector, "vector", TE, te_dpt[p])
                VE.inc(vector.tensor_copy(dpfS[:, s_], pbt[par][:, 256:384]))
                ve_dpf[p] = VE.n

            @blk.tensor
            def _(tensor, p=p, s_=s_, wb=wb):
                W(tensor, "tensor", VE, ve_dpf[p])
                W(tensor, "tensor", VE, ve_vals.get(p - 2, 0))
                TE.inc(tensor.matmul(wb[:, 256:384], dpfS[:, s_], w1Tb[:], start=True, stop=True))
                te_da[p] = TE.n

            @blk.vector
            def _(vector, p=p, s_=s_, wb=wb):
                W(vector, "vector", TE, te_da[p])
                vector.scalar_tensor_tensor(out=tmpa[:, s_], in0=a1tS[:, s_], scalar=-1.0,
                                            in1=sigS[:, s_], op0=ALU.mult, op1=ALU.mult)
                vector.tensor_tensor(out=tmpb[:, s_], in0=a1tS[:, s_], in1=sigS[:, s_], op=ALU.add)
                vector.drain()
                vector.tensor_tensor(out=tmpa[:, s_], in0=tmpa[:, s_], in1=tmpb[:, s_], op=ALU.add)
                vector.drain()
                VE.inc(vector.tensor_tensor(out=dzS[:, s_], in0=wb[:, 256:384], in1=tmpa[:, s_], op=ALU.mult))
                ve_dz[p] = VE.n

            @blk.tensor
            def _(tensor, p=p, s_=s_):
                W(tensor, "tensor", VE, ve_dz[p])
                W(tensor, "tensor", VE, se_sq.get(p - 1, 0))
                tensor.matmul(gb0[:, 0:128], a1tS[0:64, s_], dptS[0:64, s_], start=True, stop=True)
                tensor.matmul(gb0[:, 128:256], ktS[0:64, s_], dzS[0:64, s_], start=True, stop=True)
                tensor.matmul(gb1[:, 0:128], a1tS[64:128, s_], dptS[64:128, s_], start=True, stop=True)
                TE.inc(tensor.matmul(gb1[:, 128:256], ktS[64:128, s_], dzS[64:128, s_], start=True, stop=True))
                te_g[p] = TE.n

            for ci in range(2):
                c = 2 * p + ci
                gbank = gb0 if ci == 0 else gb1

                @blk.vector
                def _(vector, p=p, c=c, ci=ci, gbank=gbank):
                    if ci == 0:
                        W(vector, "vector", TE, te_g[p])
                    vector.drain()
                    vector.scalar_tensor_tensor(out=M1[:], in0=M1[:], scalar=momb[:, c:c + 1],
                                                in1=gbank[:, 0:128], op0=ALU.mult, op1=ALU.subtract)
                    vector.scalar_tensor_tensor(out=M0[:], in0=M0[:], scalar=momb[:, c:c + 1],
                                                in1=gbank[:, 128:256], op0=ALU.mult, op1=ALU.subtract)
                    vector.drain()
                    vector.scalar_tensor_tensor(out=U1[:], in0=U1[:], scalar=decb[:, c:c + 1],
                                                in1=M1[:], op0=ALU.mult, op1=ALU.add)
                    vector.scalar_tensor_tensor(out=U0[:], in0=U0[:], scalar=decb[:, c:c + 1],
                                                in1=M0[:], op0=ALU.mult, op1=ALU.add)
                    vector.drain()
                    vector.tensor_tensor(out=w0cS[:, 128 * ci:128 * ci + 128], in0=U0[:], in1=mw0f[:], op=ALU.add)
                    VE.inc(vector.tensor_tensor(out=w1cS[:, 128 * ci:128 * ci + 128], in0=U1[:], in1=mw1f[:], op=ALU.add))
                    ve_scanw[(p, ci)] = VE.n

                @blk.tensor
                def _(tensor, p=p, ci=ci, ko=ko, wb=wb):
                    W(tensor, "tensor", VE, ve_scanw[(p, ci)])
                    TE.inc(tensor.matmul(wb[:, 384 + 64 * ci:384 + 64 * ci + 64],
                                         w0cS[:, 128 * ci:128 * ci + 128], rp[:, ko + 64 * ci:ko + 64 * ci + 64],
                                         start=True, stop=True))
                    te_r[(p, ci, 0)] = TE.n

            @blk.scalar
            def _(scalar, p=p, o=o, wb=wb):
                for ci in range(2):
                    W(scalar, "scalar", TE, te_r[(p, ci, 0)])
                    SE.inc(scalar.activation(out=s1S[:, o + 64 * ci:o + 64 * ci + 64],
                                             in_=wb[:, 384 + 64 * ci:384 + 64 * ci + 64], func=AF.Silu))
                    se_s1r[(p, ci)] = SE.n

            for ci in range(2):
                gbank = gb0 if ci == 0 else gb1

                @blk.tensor
                def _(tensor, p=p, ci=ci, o=o, gbank=gbank):
                    W(tensor, "tensor", SE, se_s1r[(p, ci)])
                    TE.inc(tensor.matmul(gbank[:, 256 + 64 * ci:256 + 64 * ci + 64],
                                         w1cS[:, 128 * ci:128 * ci + 128],
                                         s1S[:, o + 64 * ci:o + 64 * ci + 64], start=True, stop=True))
                    te_r[(p, ci, 1)] = TE.n

            @blk.vector
            def _(vector, p=p, s_=s_, o=o):
                W(vector, "vector", TE, te_r[(p, 1, 1)])
                vector.tensor_copy(tmpb[:, o:o + 64], gb0[:, 256:320])
                vector.tensor_copy(tmpb[:, o + 64:o + 128], gb1[:, 320:384])
                vector.drain()
                VE.inc(vector.tensor_tensor(out=s1S[:, s_], in0=tmpb[:, s_],
                                            in1=tmpb[:, s_], op=ALU.mult))
                se_sq[p] = VE.n

            @blk.tensor
            def _(tensor, p=p, s_=s_, zb=zb):
                W(tensor, "tensor", VE, se_sq[p])
                TE.inc(tensor.matmul(zb[0:1, 0:128], onescol[:], s1S[:, s_], start=True, stop=True))
                te_ssq[p] = TE.n

            @blk.scalar
            def _(scalar, p=p, par=par, zb=zb):
                W(scalar, "scalar", TE, te_ssq[p])
                SE.inc(scalar.activation(out=r0S[0:1, 128 * par:128 * par + 128], in_=zb[0:1, 0:128],
                                         func=AF.Sqrt, scale=1.0 / DH, bias=cv[0:1, 1:2]))
                se_srow[p] = SE.n

            @blk.vector
            def _(vector, p=p, par=par, ko=ko):
                W(vector, "vector", SE, se_srow[p])
                lane, val = lg_gt[p]
                W(vector, "vector", RW[lane], val)
                vector.reciprocal(r2S[0:1, 128 * par:128 * par + 128], r0S[0:1, 128 * par:128 * par + 128])
                vector.drain()
                VE.inc(vector.tensor_tensor(out=r2S[0:1, 128 * par:128 * par + 128],
                                            in0=r2S[0:1, 128 * par:128 * par + 128],
                                            in1=grow2[0:1, ko:ko + 128], op=ALU.mult))
                ve_r2[p] = VE.n

            @blk.tensor
            def _(tensor, p=p, par=par, wb=wb):
                W(tensor, "tensor", VE, ve_r2[p])
                TE.inc(tensor.matmul(wb[:, 384:512], gp1row[:], r2S[0:1, 128 * par:128 * par + 128],
                                     start=True, stop=True))
                te_sc[p] = TE.n

            @blk.vector
            def _(vector, p=p, s_=s_, o=o, wb=wb):
                W(vector, "vector", TE, te_sc[p])
                if p >= 2:
                    W(vector, "vector", SA[p % 2], lg_v.get(p - 2, 0))
                VE.inc(vector.tensor_tensor(out=valsS[:, s_], in0=tmpb[:, s_], in1=wb[:, 384:512], op=ALU.mult))
                ve_vals[p] = VE.n

            @blk.gpsimd
            def _(gpsimd, p=p, s_=s_):
                W(gpsimd, "gpsimd", VE, ve_vals[p])
                SA[p % 2].dma(gpsimd.dma_start(out=agin[:, 128 * p:128 * p + 128], in_=valsS[:, s_]))
                lg_v[p] = SA[p % 2].n

        @blk.gpsimd
        def _(gpsimd):
            W(gpsimd, "gpsimd", SA[0], SA[0].n)
            W(gpsimd, "gpsimd", SA[1], SA[1].n)
            W(gpsimd, "gpsimd", LG, LG.n)
            CC.inc(gpsimd.collective_compute(
                "AllGather", ALU.bypass,
                replica_groups=[list(range(8))],
                ins=[agin_t.ap()],
                outs=[agout_t.ap()],
            ))

        if "agin_d" in DEBUG:
            @blk.gpsimd
            def _(gpsimd):
                LG.dma(gpsimd.dma_start(out=dbg["agin_d"][:, :], in_=agin[:, :]))
                gpsimd.wait_ge(LG.h, LG.n)

        fence()
        p2.close()

        p3 = contextlib.ExitStack()
        E3 = p3.enter_context
        xst3 = E3(nc.sbuf_tensor("xst3", [128, 2 * 1024], F32))
        xsq3 = E3(nc.sbuf_tensor("xsq3", [128, 2 * 1024], F32))
        xntok = E3(nc.sbuf_tensor("xntok", [128, 2 * 1024], BF16))
        st3 = E3(nc.sbuf_tensor("st3", [128, 2 * 16], F32))
        mx8 = E3(nc.sbuf_tensor("mx8", [128, 8], F32))
        gtmp = E3(nc.sbuf_tensor("gtmp", [128, 32], F32))
        gatesS = E3(nc.sbuf_tensor("gatesS", [128, 8], F32))
        gst3 = E3(nc.sbuf_tensor("gst3", [8, 2 * 128], BF16))
        wg32 = E3(nc.sbuf_tensor("wg32", [128, 64], F32))
        xnT32 = E3(nc.sbuf_tensor("xnT32", [128, 2 * 1024], F32))
        xn32 = E3(nc.sbuf_tensor("xn32", [128, 2 * 1024], F32))

        ld3 = {}
        se3 = {}
        ve3 = {}
        te3 = {}

        @blk.sync
        def _(sync):
            sync.wait_ge(LD.h, LD.n)
            for k in range(8):
                LD.dma(sync.dma_start(out=wg32[:, 8 * k:8 * k + 8], in_=wg_in[128 * k:128 * k + 128, :]))
            ld3["wg"] = LD.n

        for tc in range(16):
            par = tc % 2
            so = 16 * par
            xv = xst3[:, 1024 * par:1024 * par + 1024]
            xqv = xsq3[:, 1024 * par:1024 * par + 1024]
            yv = xntok[:, 1024 * par:1024 * par + 1024]

            @blk.sync
            def _(sync, tc=tc, par=par, xv=xv):
                W(sync, "sync", VE, se3.get((tc - 2, "y"), 0))
                LX[par].dma(sync.dma_start(out=xv, in_=x_slice[128 * tc:128 * tc + 128, :]))
                ld3[tc] = LX[par].n

            @blk.vector
            def _(vector, tc=tc, par=par, xv=xv, xqv=xqv, so=so):
                W(vector, "vector", LX[par], ld3[tc])
                W(vector, "vector", SE, se3.get((tc - 2, "sqrt"), 0))
                vector.reduce_sum(st3[:, so + 0:so + 1], xv, axis=AX.X)
                vector.tensor_tensor(out=xqv, in0=xv, in1=xv, op=ALU.mult)
                vector.drain()
                vector.reduce_sum(st3[:, so + 1:so + 2], xqv, axis=AX.X)
                vector.tensor_scalar(out=st3[:, so + 2:so + 3], in0=st3[:, so + 0:so + 1],
                                     scalar1=1.0 / D, scalar2=None, op0=ALU.mult)
                vector.drain()
                vector.tensor_scalar(out=st3[:, so + 3:so + 4], in0=st3[:, so + 1:so + 2],
                                     scalar1=1.0 / D, scalar2=None, op0=ALU.mult)
                vector.drain()
                VE.inc(vector.scalar_tensor_tensor(out=st3[:, so + 4:so + 5], in0=st3[:, so + 2:so + 3],
                                                   scalar=st3[:, so + 2:so + 3], in1=st3[:, so + 3:so + 4],
                                                   op0=ALU.mult, op1=ALU.subtract))
                ve3[(tc, "var")] = VE.n

            @blk.scalar
            def _(scalar, tc=tc, so=so):
                W(scalar, "scalar", VE, ve3[(tc, "var")])
                SE.inc(scalar.activation(out=st3[:, so + 5:so + 6], in_=st3[:, so + 4:so + 5],
                                         func=AF.Sqrt, scale=-1.0, bias=cv[:, 0:1]))
                se3[(tc, "sqrt")] = SE.n

            @blk.vector
            def _(vector, tc=tc, so=so):
                W(vector, "vector", SE, SE.n)
                vector.reciprocal(st3[:, so + 6:so + 7], st3[:, so + 5:so + 6])
                vector.drain()
                VE.inc(vector.scalar_tensor_tensor(out=st3[:, so + 7:so + 8], in0=st3[:, so + 2:so + 3],
                                                   scalar=-1.0, in1=st3[:, so + 6:so + 7],
                                                   op0=ALU.mult, op1=ALU.mult))
                ve3[(tc, "st")] = VE.n

            @blk.vector
            def _(vector, tc=tc, par=par, xv=xv, yv=yv, so=so):
                W(vector, "vector", TE, te3.get((tc - 2, "tr"), 0))
                vector.drain()
                vector.tensor_scalar(out=xn32[:, 1024 * par:1024 * par + 1024], in0=xv,
                                     scalar1=st3[:, so + 6:so + 7], scalar2=st3[:, so + 7:so + 8],
                                     op0=ALU.mult, op1=ALU.add)
                VE.inc(vector.tensor_scalar(out=yv, in0=xv, scalar1=st3[:, so + 6:so + 7],
                                            scalar2=st3[:, so + 7:so + 8], op0=ALU.mult, op1=ALU.add))
                se3[(tc, "y")] = VE.n

            for half in range(2):
                @blk.tensor
                def _(tensor, tc=tc, par=par, half=half):
                    if half == 0:
                        W(tensor, "tensor", VE, se3[(tc, "y")])
                    W(tensor, "tensor", VE, ve3.get((tc - 2, "ev", half), 0))
                    xsrc = xn32[:, 1024 * par:1024 * par + 1024]
                    hb5 = pb[2] if half == 0 else pb[3]
                    for k in range(3):
                        tensor.transpose(hb5[:, 128 * k:128 * k + 128],
                                         xsrc[:, 512 * half + 128 * k:512 * half + 128 * k + 128], idf[:])
                    TE.inc(tensor.transpose(hb5[:, 384:512], xsrc[:, 512 * half + 384:512 * half + 512], idf[:]))
                    te3[(tc, "tr", half)] = TE.n

                @blk.vector
                def _(vector, tc=tc, par=par, half=half):
                    W(vector, "vector", TE, te3[(tc, "tr", half)])
                    hb5 = pb[2] if half == 0 else pb[3]
                    dstv = xnT[:].rearrange("p (k w) -> p k w", k=8, w=WTOK)[:, 4 * half:4 * half + 4, 128 * tc:128 * tc + 128]
                    vector.tensor_copy(dstv, hb5[:].rearrange("p (k c) -> p k c", k=4, c=128))
                    dst32 = xnT32[:].rearrange("p (b k c) -> p b k c", b=2, k=4, c=128)[:, par, :, :] \
                        if False else xnT32[:, 1024 * par + 512 * half:1024 * par + 512 * half + 512]
                    VE.inc(vector.tensor_copy(dst32, hb5[:, 0:512]))
                    ve3[(tc, "ev", half)] = VE.n
            te3[(tc, "tr")] = te3[(tc, "tr", 1)]
            ve3[(tc, "ev")] = ve3[(tc, "ev", 1)]

            @blk.tensor
            def _(tensor, tc=tc, par=par):
                W(tensor, "tensor", LD, ld3["wg"])
                W(tensor, "tensor", VE, ve3[(tc, "ev")])
                W(tensor, "tensor", VE, ve3.get((tc - 1, "g"), 0))
                for k in range(7):
                    tensor.matmul(pb[0][:, 0:8], xnT32[:, 1024 * par + 128 * k:1024 * par + 128 * k + 128],
                                  wg32[:, 8 * k:8 * k + 8], start=(k == 0), stop=False)
                TE.inc(tensor.matmul(pb[0][:, 0:8], xnT32[:, 1024 * par + 896:1024 * par + 1024],
                                     wg32[:, 56:64], start=False, stop=True))
                te3[(tc, "lg")] = TE.n

            @blk.vector
            def _(vector, tc=tc):
                W(vector, "vector", TE, te3[(tc, "lg")])
                VE.inc(vector.max(mx8[:], pb[0][:, 0:8]))
                ve3[(tc, "mx")] = VE.n

            @blk.vector
            def _(vector, tc=tc):
                vector.drain()
                VE.inc(vector.tensor_tensor(out=gtmp[:, 10:11], in0=mx8[:, 0:1], in1=mx8[:, 1:2], op=ALU.subtract))
                ve3[(tc, "d")] = VE.n

            @blk.scalar
            def _(scalar, tc=tc):
                W(scalar, "scalar", VE, ve3[(tc, "d")])
                SE.inc(scalar.activation(out=gtmp[:, 11:12], in_=gtmp[:, 10:11], func=AF.Sigmoid))
                se3[(tc, "w1")] = SE.n

            @blk.vector
            def _(vector, tc=tc):
                W(vector, "vector", SE, se3[(tc, "w1")])
                vector.tensor_scalar(out=gtmp[:, 12:13], in0=gtmp[:, 11:12], scalar1=-1.0, scalar2=1.0,
                                     op0=ALU.mult, op1=ALU.add)
                vector.tensor_scalar(out=gtmp[:, 16:24], in0=pb[0][:, 0:8], scalar1=mx8[:, 0:1],
                                     scalar2=None, op0=ALU.is_equal)
                vector.tensor_scalar(out=gtmp[:, 24:32], in0=pb[0][:, 0:8], scalar1=mx8[:, 1:2],
                                     scalar2=None, op0=ALU.is_equal)
                vector.drain()
                vector.tensor_scalar(out=gtmp[:, 16:24], in0=gtmp[:, 16:24], scalar1=gtmp[:, 11:12],
                                     scalar2=None, op0=ALU.mult)
                vector.tensor_scalar(out=gtmp[:, 24:32], in0=gtmp[:, 24:32], scalar1=gtmp[:, 12:13],
                                     scalar2=None, op0=ALU.mult)
                vector.drain()
                VE.inc(vector.tensor_tensor(out=gatesS[:], in0=gtmp[:, 16:24], in1=gtmp[:, 24:32], op=ALU.add))
                ve3[(tc, "g")] = VE.n

            @blk.tensor
            def _(tensor, tc=tc):
                W(tensor, "tensor", VE, ve3[(tc, "g")])
                TE.inc(tensor.transpose(pb[1][0:8, 0:128], gatesS[:], idf[:]))
                te3[(tc, "gt")] = TE.n

            @blk.vector
            def _(vector, tc=tc, par=par):
                W(vector, "vector", TE, te3[(tc, "gt")])
                W(vector, "vector", SG3, ve3.get((tc - 2, "grdma"), 0))
                VE.inc(vector.tensor_copy(gst3[:, 128 * par:128 * par + 128], pb[1][0:8, 0:128]))
                ve3[(tc, "gr")] = VE.n

            @blk.gpsimd
            def _(gpsimd, tc=tc, par=par):
                W(gpsimd, "gpsimd", SG3, ve3.get((tc - 1, "grdma"), 0))
                W(gpsimd, "gpsimd", VE, ve3[(tc, "gr")])
                SG3.dma(gpsimd.dma_start(out=gaterd[:, 128 * tc:128 * tc + 128],
                                        in_=gst3[0:8, 128 * par:128 * par + 128]))
                ve3[(tc, "grdma")] = SG3.n

        fence()
        p3.close()

        if "xnT_d" in DEBUG:
            @blk.gpsimd
            def _(gpsimd):
                for k in range(8):
                    LG.dma(gpsimd.dma_start(out=dbg["xnT_d"][k], in_=xnT[:, WTOK * k:WTOK * k + WTOK]))
                gpsimd.wait_ge(LG.h, LG.n)

        p4 = contextlib.ExitStack()
        E4 = p4.enter_context
        wst4 = E4(nc.sbuf_tensor("wst4", [128, 2 * 1024], F32))
        w1q = E4(nc.sbuf_tensor("w1q", [128, 2 * 8 * 1024], BF16))
        w2q = E4(nc.sbuf_tensor("w2q", [128, 2 * 8 * 1024], BF16))
        hbuf = E4(nc.sbuf_tensor("hbuf", [128, 2 * 8 * 512], BF16))
        gbc = E4(nc.sbuf_tensor("gbc", [128, WTOK], BF16))
        grow4 = E4(nc.sbuf_tensor("grow4", [1, WTOK], BF16))

        ld4 = {}
        cv4 = {}
        te4 = {}
        ve4 = {}
        se4 = {}
        h_ve = []
        acc_ve = []
        l2_te = []
        gmul_last = {}

        l2done = {}

        xnv = xnT[:].rearrange("p (k w) -> p k w", k=8, w=WTOK)

        def stage_step(eq, kk):
            if eq >= 32:
                return
            e, q = divmod(eq, 4)
            wo = 8192 * (eq % 2)
            hh = kk % 2

            def _dma(eng, ename):
                gk = 16 * eq + kk
                if gk >= 2:
                    peq, pkk = divmod(gk - 2, 16)
                    W(eng, ename, VE, cv4.get((peq, pkk), 0))
                if kk < 8:
                    src = w1_in[e, 128 * kk:128 * kk + 128, 1024 * q:1024 * q + 1024]
                else:
                    c2 = kk - 8
                    src = w2_in[e, 1024 * q + 128 * c2:1024 * q + 128 * c2 + 128, 0:1024]
                LW[hh].dma(eng.dma_start(out=wst4[:, 1024 * hh:1024 * hh + 1024], in_=src))
                ld4[(eq, kk)] = LW[hh].n

            if hh == 0:
                @blk.sync
                def _(sync):
                    _dma(sync, "sync")
            else:
                @blk.gpsimd
                def _(gpsimd):
                    _dma(gpsimd, "gpsimd")

            @blk.vector
            def _(vector):
                W(vector, "vector", LW[hh], ld4[(eq, kk)])
                if kk == 0:
                    W(vector, "vector", TE, te4.get((eq - 2, "done"), 0))
                dst = w1q if kk < 8 else w2q
                ki = kk % 8
                VE.inc(vector.tensor_copy(dst[:, wo + 1024 * ki:wo + 1024 * ki + 1024],
                                          wst4[:, 1024 * hh:1024 * hh + 1024]))
                cv4[(eq, kk)] = VE.n

        def emit_gbc(eq, e):
            @blk.gpsimd
            def _(gpsimd):
                W(gpsimd, "gpsimd", SG4, ld4.get((eq - 4, "grow"), 0))
                if e > 0:
                    W(gpsimd, "gpsimd", TE, te4.get((4 * e - 1, "gbcmm"), 0))
                SG4.dma(gpsimd.dma_start(out=grow4[:], in_=gaterd[e:e + 1, :]))
                ld4[(eq, "grow")] = SG4.n

            for tt in range(4):
                @blk.tensor
                def _(tensor, tt=tt):
                    W(tensor, "tensor", SG4, ld4[(eq, "grow")])
                    if e > 0:
                        W(tensor, "tensor", VE, gmul_last.get(e - 1, 0))
                    W(tensor, "tensor", VE, ve4.get((eq, "gbc", tt - 1), 0))
                    if h_ve:
                        W(tensor, "tensor", VE, h_ve[-1])
                    TE.inc(tensor.matmul(pb[4][:, 0:512], ones1[:],
                                         grow4[0:1, 512 * tt:512 * tt + 512], start=True, stop=True))
                    te4[(eq, "gbc", tt)] = TE.n
                    te4[(eq + 3, "gbcmm")] = TE.n

                @blk.vector
                def _(vector, tt=tt):
                    W(vector, "vector", TE, te4[(eq, "gbc", tt)])
                    VE.inc(vector.tensor_copy(gbc[:, 512 * tt:512 * tt + 512], pb[4][:, 0:512]))
                    ve4[(eq, "gbc", tt)] = VE.n
            ve4[(eq, "gbc")] = VE.n

        L1BANKS = [pb[0], pb[1], pb[5], pb[4]]

        def emit_L1(eq, tt):
            e = eq // 4
            wo = 8192 * (eq % 2)
            ho = 4096 * (tt % 2)
            for grp in range(2):
                ms = range(4 * grp, 4 * grp + 4)
                for m in ms:
                    gm = 32 * eq + 8 * tt + m
                    hb = L1BANKS[gm % 4]

                    @blk.tensor
                    def _(tensor, m=m, hb=hb, gm=gm):
                        if m == 0 and tt == 0:
                            W(tensor, "tensor", VE, cv4[(eq, 7)])
                            W(tensor, "tensor", VE, ve4[(4 * (eq // 4), "gbc")])
                        if gm >= 4:
                            W(tensor, "tensor", VE, h_ve[gm - 4])
                        for k in range(7):
                            tensor.matmul(hb[:, 0:512],
                                          w1q[:, wo + 1024 * k + 128 * m:wo + 1024 * k + 128 * m + 128],
                                          xnv[:, k, 512 * tt:512 * tt + 512], start=(k == 0), stop=False)
                        TE.inc(tensor.matmul(hb[:, 0:512],
                                             w1q[:, wo + 7 * 1024 + 128 * m:wo + 7 * 1024 + 128 * m + 128],
                                             xnv[:, 7, 512 * tt:512 * tt + 512], start=False, stop=True))
                        te4[(eq, tt, m)] = TE.n

                @blk.scalar
                def _(scalar, ms=ms):
                    for m in ms:
                        gm = 32 * eq + 8 * tt + m
                        W(scalar, "scalar", TE, te4[(eq, tt, m)])
                        SE.inc(scalar.activation(out=L1BANKS[gm % 4][:, 0:512],
                                                 in_=L1BANKS[gm % 4][:, 0:512], func=AF.Silu))
                        se4[(eq, tt, m)] = SE.n

                @blk.vector
                def _(vector, ms=ms, ho=ho):
                    for m in ms:
                        gm = 32 * eq + 8 * tt + m
                        W(vector, "vector", SE, se4[(eq, tt, m)])
                        if m == 0:
                            prev = (eq, tt - 2) if tt >= 2 else (eq - 1, tt + 2)
                            W(vector, "vector", TE, l2done.get(prev, 0))
                        VE.inc(vector.tensor_tensor(out=hbuf[:, ho + 512 * m:ho + 512 * m + 512],
                                                    in0=L1BANKS[gm % 4][:, 0:512],
                                                    in1=gbc[:, 512 * tt:512 * tt + 512],
                                                    op=ALU.mult))
                        ve4[(eq, tt, m)] = VE.n
                        h_ve.append(VE.n)
                        if m == 7:
                            gmul_last[e] = VE.n

        def emit_L2(eq, tt):
            wo = 8192 * (eq % 2)
            ho = 4096 * (tt % 2)
            for oc in range(8):
                go = 32 * eq + 8 * tt + oc
                ob = pb[2] if go % 2 == 0 else pb[3]

                @blk.tensor
                def _(tensor, oc=oc, ob=ob, go=go):
                    if tt == 0 and oc == 0:
                        W(tensor, "tensor", VE, cv4[(eq, 15)])
                    W(tensor, "tensor", VE, ve4[(eq, tt, 7)])
                    if go >= 2:
                        W(tensor, "tensor", VE, acc_ve[go - 2])
                    for m in range(7):
                        tensor.matmul(ob[:, 0:512],
                                      w2q[:, wo + 1024 * m + 128 * oc:wo + 1024 * m + 128 * oc + 128],
                                      hbuf[:, ho + 512 * m:ho + 512 * m + 512], start=(m == 0), stop=False)
                    TE.inc(tensor.matmul(ob[:, 0:512],
                                         w2q[:, wo + 7 * 1024 + 128 * oc:wo + 7 * 1024 + 128 * oc + 128],
                                         hbuf[:, ho + 512 * 7:ho + 512 * 7 + 512], start=False, stop=True))
                    te4[(eq, tt, "l2", oc)] = TE.n

                @blk.vector
                def _(vector, oc=oc, ob=ob):
                    W(vector, "vector", TE, te4[(eq, tt, "l2", oc)])
                    VE.inc(vector.tensor_tensor(
                        out=gacc[:, WTOK * oc + 512 * tt:WTOK * oc + 512 * tt + 512],
                        in0=gacc[:, WTOK * oc + 512 * tt:WTOK * oc + 512 * tt + 512],
                        in1=ob[:, 0:512], op=ALU.add))
                    acc_ve.append(VE.n)
            l2done[(eq, tt)] = TE.n
            if tt == 3:
                te4[(eq, "done")] = TE.n

        for kk in range(16):
            stage_step(0, kk)

        SEQ = [("L1", 0), ("L1", 1), ("L2", 0), ("L1", 2), ("L2", 1), ("L1", 3), ("L2", 2), ("L2", 3)]
        for eq in range(32):
            e, q = divmod(eq, 4)
            if q == 0:
                emit_gbc(eq, e)
            for bi, (kind, tt) in enumerate(SEQ):
                if kind == "L1":
                    emit_L1(eq, tt)
                else:
                    emit_L2(eq, tt)
                stage_step(eq + 1, 2 * bi)
                stage_step(eq + 1, 2 * bi + 1)

        fence()
        p4.close()

        if "gacc_d" in DEBUG:
            @blk.gpsimd
            def _(gpsimd):
                for oc in range(8):
                    LG.dma(gpsimd.dma_start(out=dbg["gacc_d"][oc], in_=gacc[:, WTOK * oc:WTOK * oc + WTOK]))
                gpsimd.wait_ge(LG.h, LG.n)

        p5 = contextlib.ExitStack()
        E5 = p5.enter_context
        wcst = E5(nc.sbuf_tensor("wcst", [128, 1024], F32))
        wcb = E5(nc.sbuf_tensor("wcb", [128, 8 * 1024], BF16))
        vw = E5(nc.sbuf_tensor("vw", [128, 8 * 2176], BF16))
        sgt = E5(nc.sbuf_tensor("sgt", [128, 2 * 512], F32))
        gsl = E5(nc.sbuf_tensor("gsl", [128, 2 * 512], F32))
        otok = E5(nc.sbuf_tensor("otok", [128, 4 * 1024], F32))
        xsl5 = E5(nc.sbuf_tensor("xsl5", [128, 2 * 1024], F32))

        ld5 = {}
        se5 = {}
        ve5 = {}
        te5 = {}
        lg5 = {}
        so5 = {}

        for k in range(8):
            @blk.sync
            def _(sync, k=k):
                if k == 0:
                    sync.wait_ge(LD.h, LD.n)
                W(sync, "sync", SE, se5.get(("wc", k - 1), 0))
                LD.dma(sync.dma_start(out=wcst[:], in_=wc_in[128 * k:128 * k + 128, :]))
                ld5[("wc", k)] = LD.n

            @blk.scalar
            def _(scalar, k=k):
                W(scalar, "scalar", LD, ld5[("wc", k)])
                SE.inc(scalar.copy(out=wcb[:, 1024 * k:1024 * k + 1024], in_=wcst[:]))
                se5[("wc", k)] = SE.n

        @blk.gpsimd
        def _(gpsimd):
            W(gpsimd, "gpsimd", CC, CC.n)
            mainview = agout.rearrange("h p (j t) -> (h p j) t", t=2048)
            bndview = agout.rearrange("h p (r t) -> (h p r) t", t=128)
            for h in range(8):
                SV5.dma(gpsimd.indirect_dma_start(
                    out=vw[:, 2176 * h + 128:2176 * h + 2176], out_offset=None,
                    in_=mainview, in_offset=bass.IndirectOffsetOnAxis(ap=widxS[:, h:h + 1], axis=0)))
                SV5.dma(gpsimd.indirect_dma_start(
                    out=vw[:, 2176 * h:2176 * h + 128], out_offset=None,
                    in_=bndview, in_offset=bass.IndirectOffsetOnAxis(ap=widxS[:, 8 + h:8 + h + 1], axis=0)))
            lg5["vw"] = SV5.n

        @blk.vector
        def _(vector):
            W(vector, "vector", SV5, lg5["vw"])
            for h in range(7):
                vector.tensor_scalar(out=vw[:, 2176 * h:2176 * h + 128], in0=vw[:, 2176 * h:2176 * h + 128],
                                     scalar1=bscaleS[:, 0:1], scalar2=None, op0=ALU.mult)
            VE.inc(vector.tensor_scalar(out=vw[:, 2176 * 7:2176 * 7 + 128], in0=vw[:, 2176 * 7:2176 * 7 + 128],
                                        scalar1=bscaleS[:, 0:1], scalar2=None, op0=ALU.mult))
            ve5["vw"] = VE.n

        sg_hist = []
        ot_hist = []
        for t in range(4):
            obuf = 0

            for oc in range(8):
                sb = pb[0] if oc % 2 == 0 else pb[1]
                spar = oc % 2
                gidx5 = 8 * t + oc

                @blk.tensor
                def _(tensor, t=t, oc=oc, sb=sb, gidx5=gidx5):
                    if t == 0 and oc == 0:
                        W(tensor, "tensor", SE, se5[("wc", 7)])
                        W(tensor, "tensor", VE, ve5["vw"])
                    if gidx5 >= 2:
                        W(tensor, "tensor", VE, sg_hist[gidx5 - 2])
                    for h in range(7):
                        tensor.matmul(sb[:, 0:512], wcb[:, 1024 * h + 128 * oc:1024 * h + 128 * oc + 128],
                                      vw[:, 2176 * h + 65 + 512 * t:2176 * h + 65 + 512 * t + 512],
                                      start=(h == 0), stop=False)
                    TE.inc(tensor.matmul(sb[:, 0:512], wcb[:, 7 * 1024 + 128 * oc:7 * 1024 + 128 * oc + 128],
                                         vw[:, 2176 * 7 + 65 + 512 * t:2176 * 7 + 65 + 512 * t + 512],
                                         start=False, stop=True))
                    te5[("s", t, oc)] = TE.n

                @blk.scalar
                def _(scalar, t=t, oc=oc, spar=spar, gidx5=gidx5):
                    W(scalar, "scalar", TE, te5[("s", t, oc)])
                    if gidx5 >= 2:
                        W(scalar, "scalar", VE, sg_hist[gidx5 - 2])
                    SE.inc(scalar.activation(out=gsl[:, 512 * spar:512 * spar + 512],
                                             in_=gacc[:, WTOK * oc + 512 * t:WTOK * oc + 512 * t + 512],
                                             func=AF.Silu))
                    se5[("g", t, oc)] = SE.n

                @blk.vector
                def _(vector, t=t, oc=oc, sb=sb, spar=spar, gidx5=gidx5):
                    W(vector, "vector", SE, se5[("g", t, oc)])
                    if gidx5 >= 2:
                        W(vector, "vector", TE, te5.get(("tr",) + divmod(gidx5 - 2, 8), 0))
                    VE.inc(vector.tensor_tensor(out=sgt[:, 512 * spar:512 * spar + 512],
                                                in0=sb[:, 0:512], in1=gsl[:, 512 * spar:512 * spar + 512],
                                                op=ALU.mult))
                    ve5[("sg", t, oc)] = VE.n
                    sg_hist.append(VE.n)

                @blk.tensor
                def _(tensor, t=t, oc=oc, spar=spar, gidx5=gidx5):
                    W(tensor, "tensor", VE, ve5[("sg", t, oc)])
                    tb = pb[2] if oc % 2 == 0 else pb[3]
                    if gidx5 >= 2:
                        W(tensor, "tensor", VE, ot_hist[gidx5 - 2])
                    for i in range(3):
                        tensor.transpose(tb[:, 128 * i:128 * i + 128], sgt[:, 512 * spar + 128 * i:512 * spar + 128 * i + 128], idf[:])
                    TE.inc(tensor.transpose(tb[:, 384:512], sgt[:, 512 * spar + 384:512 * spar + 512], idf[:]))
                    te5[("tr", t, oc)] = TE.n

                @blk.vector
                def _(vector, t=t, oc=oc, obuf=obuf, gidx5=gidx5):
                    W(vector, "vector", TE, te5[("tr", t, oc)])
                    tb = pb[2] if oc % 2 == 0 else pb[3]
                    if oc == 0:
                        W(vector, "vector", SO, so5.get(("tot", t - 1), 0))
                    for i in range(3):
                        vector.tensor_copy(otok[:, 1024 * i + 128 * oc:1024 * i + 128 * oc + 128],
                                           tb[:, 128 * i:128 * i + 128])
                    VE.inc(vector.tensor_copy(otok[:, 3072 + 128 * oc:3072 + 128 * oc + 128],
                                              tb[:, 384:512]))
                    ve5[("ot", t, oc)] = VE.n
                    ot_hist.append(VE.n)

            for i in range(4):
                @blk.sync
                def _(sync, t=t, i=i):
                    gx = 4 * t + i
                    if gx >= 2:
                        pt, pi = divmod(gx - 2, 4)
                        W(sync, "sync", VE, ve5.get(("add", pt, pi), 0))
                    LX[i % 2].dma(sync.dma_start(out=xsl5[:, 1024 * (i % 2):1024 * (i % 2) + 1024],
                                                 in_=x_slice[512 * t + 128 * i:512 * t + 128 * i + 128, :]))
                    ld5[("x", t, i)] = LX[i % 2].n

                @blk.vector
                def _(vector, t=t, i=i):
                    W(vector, "vector", VE, ve5[("ot", t, 7)])
                    W(vector, "vector", LX[i % 2], ld5[("x", t, i)])
                    VE.inc(vector.tensor_tensor(out=otok[:, 1024 * i:1024 * i + 1024],
                                                in0=otok[:, 1024 * i:1024 * i + 1024],
                                                in1=xsl5[:, 1024 * (i % 2):1024 * (i % 2) + 1024], op=ALU.add))
                    ve5[("add", t, i)] = VE.n

                @blk.sync
                def _(sync, t=t, i=i):
                    W(sync, "sync", VE, ve5[("add", t, i)])
                    SO.dma(sync.dma_start(out=out[512 * t + 128 * i:512 * t + 128 * i + 128, :],
                                          in_=otok[:, 1024 * i:1024 * i + 1024]))
                    so5[("out", t, i)] = SO.n
                    so5[("tot", t)] = SO.n

        @blk.sync
        def _(sync):
            sync.wait_ge(SO.h, SO.n)

        p5.close()

    st.close()
    return nc


_CACHE = {}


def _build_cached():
    key = tuple(sorted(DEBUG))
    if key not in _CACHE:
        _CACHE[key] = build_program()
    return _CACHE[key]


def kernel(L0_attn_seq, L0_attn_store_norm_g, L0_attn_retrieve_norm_g, L0_attn_w_kv,
           L0_attn_w_q, L0_attn_w_step, L0_attn_w_mom, L0_attn_w_decay, L0_attn_w_gate,
           L0_attn_w_combine, L0_attn_mh_gamma, L0_attn_mem_w0, L0_attn_mem_w1,
           L0_ffn_Wg, L0_ffn_W1, L0_ffn_W2):
    nc = _build_cached()
    f32 = np.float32
    x = np.ascontiguousarray(np.asarray(L0_attn_seq, dtype=f32).reshape(NTOK, D))
    w_kv = np.asarray(L0_attn_w_kv, dtype=f32)
    w_q = np.asarray(L0_attn_w_q, dtype=f32)
    w_step = np.asarray(L0_attn_w_step, dtype=f32)
    w_mom = np.asarray(L0_attn_w_mom, dtype=f32)
    w_decay = np.asarray(L0_attn_w_decay, dtype=f32)
    w_gate = np.asarray(L0_attn_w_gate, dtype=f32)
    w_comb = np.ascontiguousarray(np.asarray(L0_attn_w_combine, dtype=f32))
    mh_gamma = np.asarray(L0_attn_mh_gamma, dtype=f32)
    mem_w0 = np.ascontiguousarray(np.asarray(L0_attn_mem_w0, dtype=f32))
    mem_w1 = np.ascontiguousarray(np.asarray(L0_attn_mem_w1, dtype=f32))
    wg = np.ascontiguousarray(np.asarray(L0_ffn_Wg, dtype=f32))
    w1 = np.ascontiguousarray(np.asarray(L0_ffn_W1, dtype=f32))
    w2 = np.ascontiguousarray(np.asarray(L0_ffn_W2, dtype=f32))
    sgain = np.ascontiguousarray(np.asarray(L0_attn_store_norm_g, dtype=f32).reshape(8, 128))
    rgain = np.ascontiguousarray(np.asarray(L0_attn_retrieve_norm_g, dtype=f32).reshape(8, 128))

    idb = np.eye(128, dtype=ml_dtypes.bfloat16)
    idf = np.eye(128, dtype=np.float32)
    oseg = np.zeros((128, 2), dtype=ml_dtypes.bfloat16)
    oseg[0:64, 0] = ml_dtypes.bfloat16(1.0 / 64)
    oseg[64:128, 1] = ml_dtypes.bfloat16(1.0 / 64)
    cvc = np.zeros((128, 4), dtype=f32)
    cvc[:, 0] = EPS_LN
    cvc[:, 1] = EPS_RMS
    cvc[:, 2] = EPS_RMS * EPS_LN
    zf = np.zeros((128, 128), dtype=f32)

    in_maps = []
    for i in range(8):
        widx = np.zeros((128, 16), dtype=np.uint32)
        pr = np.arange(128, dtype=np.uint32)
        for h in range(8):
            widx[:, h] = (h * 128 + pr) * 8 + i
            widx[:, 8 + h] = (h * 128 + pr) * 128 + (16 * i - 1 if i > 0 else 0)
        bscale = np.full((128, 1), 1.0 if i > 0 else 0.0, dtype=f32)
        m = {
            "x_full": x,
            "x_slice": np.ascontiguousarray(x[WTOK * i:WTOK * (i + 1)]),
            "wk_h": np.ascontiguousarray(w_kv[:, 128 * i:128 * i + 128]),
            "wv_h": np.ascontiguousarray(w_kv[:, 1024 + 128 * i:1024 + 128 * i + 128]),
            "wq_h": np.ascontiguousarray(w_q[:, 128 * i:128 * i + 128]),
            "w4_h": np.ascontiguousarray(np.stack([w_step[:, i], w_gate[:, i]], axis=1)),
            "wmd_h": np.ascontiguousarray(np.stack([w_mom[:, i], w_decay[:, i]], axis=1)),
            "sgain": sgain, "rgain": rgain,
            "mw0": mem_w0, "mw1": mem_w1,
            "gamma_row": np.ascontiguousarray(mh_gamma[i].reshape(1, 128)),
            "wg_in": wg, "w1_in": w1, "w2_in": w2, "wc_in": w_comb,
            "idb_in": idb, "idf_in": idf, "oseg_in": oseg, "cv_in": cvc,
            "widx_in": widx, "bscale_in": bscale, "zf_in": zf,
        }
        in_maps.append(m)

    res = run_bass_kernel_spmd(nc, in_maps, core_ids=list(range(8)))
    outs = [res.results[i]["out"] for i in range(8)]
    full = np.concatenate(outs, axis=0).reshape(1, NTOK, D).astype(np.float32)
    if DEBUG:
        kernel._debug_results = res
    return full
